# version 01
# baseline (speedup 1.0000x reference)
import os
from contextlib import ExitStack
import numpy as np
import ml_dtypes
import concourse.bass as bass
import concourse.mybir as mybir
from concourse.bass_utils import run_bass_kernel_spmd

F32 = mybir.dt.float32
BF16 = mybir.dt.bfloat16
I32 = mybir.dt.int32
AF = mybir.ActivationFunctionType
AL = mybir.AluOpType
AX = mybir.AxisListType
NBF = ml_dtypes.bfloat16

NCORES = 8
B = 8
LLOC = 256
T = 2048
D = 1024
CH = 8
DFF = 4096
CF = 32
TTW = 512
NTT = T // TTW
EPS = 1e-5

_SIM = os.environ.get("BASS_SIM_FP64") == "1"
RB16 = 32768.5 if _SIM else 0.0
RS16 = 32768.0 if _SIM else 0.0
RB8 = 128.5 if _SIM else 0.0
RS8 = 128.0 if _SIM else 0.0


def colz(vec, n):
    return np.ascontiguousarray(np.asarray(vec, np.float32).reshape(n, 128).T)


def fm(x):
    return x[:, :].rearrange("(c p) t -> p c t", p=128)


def sb3(buf, nch, tw):
    return buf[:, :].rearrange("p (c t) -> p c t", t=tw)



def build_A(debug=False):
    nc = bass.Bass("TRN2", target_bir_lowering=False, detect_race_conditions=False)
    x_in = nc.dram_tensor("x_in", [D, T], F32, kind="ExternalInput")
    wx_in = nc.dram_tensor("wx_in", [D, 2 * D], F32, kind="ExternalInput")
    bxp_in = nc.dram_tensor("bxp_in", [128, 16], F32, kind="ExternalInput")
    A_in = nc.dram_tensor("A_in", [128, CH * 16], F32, kind="ExternalInput")
    gam_in = nc.dram_tensor("gam_in", [128, CH], F32, kind="ExternalInput")
    bet_in = nc.dram_tensor("bet_in", [128, CH], F32, kind="ExternalInput")
    dp_in = nc.dram_tensor("dp_in", [128, CH], F32, kind="ExternalInput")
    ones_in = nc.dram_tensor("ones_in", [128, 128], BF16, kind="ExternalInput")

    m0_out = nc.dram_tensor("m0_out", [D, T], BF16, kind="ExternalOutput")
    gate_out = nc.dram_tensor("gate_out", [D, T], BF16, kind="ExternalOutput")
    S_out = nc.dram_tensor("S_out", [128, CH * B], BF16, kind="ExternalOutput")
    if debug:
        xn_out = nc.dram_tensor("xn_out", [D, T], BF16, kind="ExternalOutput")
        h_out = nc.dram_tensor("h_out", [D, T], BF16, kind="ExternalOutput")

    with ExitStack() as _ctx:
        _e = _ctx.enter_context
        XF = _e(nc.sbuf_tensor("XF", [128, 2 * T], F32))
        XN = _e(nc.sbuf_tensor("XN", [128, CH * T], BF16))
        DG = _e(nc.sbuf_tensor("DG", [128, CH * T], BF16))
        BG = _e(nc.sbuf_tensor("BG", [128, CH * T], BF16))
        WXB = _e(nc.sbuf_tensor("WXB", [128, CH * 2 * D], BF16))
        WST = _e(nc.sbuf_tensor("WST", [128, 2 * D], F32))
        XB = _e(nc.sbuf_tensor("XB", [128, 2 * T], BF16))
        XSQ = _e(nc.sbuf_tensor("XSQ", [128, 2 * T], BF16))
        SC1 = _e(nc.sbuf_tensor("SC1", [128, T], F32))
        SC2 = _e(nc.sbuf_tensor("SC2", [128, T], F32))
        GA = _e(nc.sbuf_tensor("GA", [128, T], F32))
        BXP = _e(nc.sbuf_tensor("BXP", [128, 16], F32))
        ACOL = _e(nc.sbuf_tensor("ACOL", [128, CH * 16], F32))
        GAM = _e(nc.sbuf_tensor("GAM", [128, CH], F32))
        BET = _e(nc.sbuf_tensor("BET", [128, CH], F32))
        DP = _e(nc.sbuf_tensor("DP", [128, CH], F32))
        ONESB = _e(nc.sbuf_tensor("ONESB", [128, 128], BF16))
        EPSC = _e(nc.sbuf_tensor("EPSC", [128, 1], F32))
        ONEC = _e(nc.sbuf_tensor("ONEC", [128, 1], F32))
        SS = _e(nc.sbuf_tensor("SS", [128, CH * B], BF16))
        PA = _e(nc.psum_tensor("PA", [128, T], F32))
        PB = _e(nc.psum_tensor("PB", [128, T], F32))
        ldc = _e(nc.semaphore("ldc"))
        ldx = _e(nc.semaphore("ldx"))
        ldx2 = _e(nc.semaphore("ldx2"))
        ldw = _e(nc.semaphore("ldw"))
        wcast = _e(nc.semaphore("wcast"))
        xprep = _e(nc.semaphore("xprep"))
        stat = _e(nc.semaphore("stat"))
        vx = _e(nc.semaphore("vx"))
        mmd = _e(nc.semaphore("mmd"))
        spsc = _e(nc.semaphore("spsc"))
        pfree = _e(nc.semaphore("pfree"))
        vdone = _e(nc.semaphore("vdone"))
        gexp = _e(nc.semaphore("gexp"))
        gadd = _e(nc.semaphore("gadd"))
        sto = _e(nc.semaphore("sto"))

        with nc.Block() as b:
            @b.sync
            def _(s):
                s.dma_start(BXP[:, :], bxp_in[:, :]).then_inc(ldc, 16)
                s.dma_start(ACOL[:, :], A_in[:, :]).then_inc(ldc, 16)
                s.dma_start(GAM[:, :], gam_in[:, :]).then_inc(ldc, 16)
                s.dma_start(BET[:, :], bet_in[:, :]).then_inc(ldc, 16)
                s.dma_start(DP[:, :], dp_in[:, :]).then_inc(ldc, 16)
                s.dma_start(ONESB[:, :], ones_in[:, :]).then_inc(ldc, 16)
                for c in range(CH):
                    if c >= 2:
                        s.wait_ge(xprep, c - 1)
                    s.dma_start(XF[:, (c % 2) * T:(c % 2) * T + T],
                                x_in[c * 128:(c + 1) * 128, :]).then_inc(ldx, 16)

            @b.gpsimd
            def _(g):
                for k in range(CH):
                    g.dma_start(WXB[:, k * 2 * D:(k + 1) * 2 * D],
                                wx_in[k * 128:(k + 1) * 128, :]).then_inc(ldw, 16)

            @b.vector
            def _(v):
                v.memset(EPSC[:, :], EPS)
                v.memset(ONEC[:, :], 1.0)

            @b.scalar
            def _(sc):
                for c in range(CH):
                    sc.wait_ge(ldx, 16 * (c + 1))
                    if c >= 2:
                        sc.wait_ge(stat, c - 1)
                    sl = slice((c % 2) * T, (c % 2) * T + T)
                    sc.activation(XB[:, sl], XF[:, sl], AF.Identity, bias=0.0, scale=1.0)
                    sc.activation(XSQ[:, sl], XF[:, sl], AF.Square,
                                  bias=0.0, scale=1.0).then_inc(xprep, 1)

            @b.tensor
            def _(t):
                for c in range(CH):
                    t.wait_ge(xprep, c + 1)
                    sl0 = (c % 2) * T
                    for tt in range(NTT):
                        t.matmul(PA[:, tt * TTW:(tt + 1) * TTW], ONESB[:, :],
                                 XB[:, sl0 + tt * TTW: sl0 + (tt + 1) * TTW],
                                 start=(c == 0), stop=(c == CH - 1))
                        mi = t.matmul(PB[:, tt * TTW:(tt + 1) * TTW], ONESB[:, :],
                                      XSQ[:, sl0 + tt * TTW: sl0 + (tt + 1) * TTW],
                                      start=(c == 0), stop=(c == CH - 1))
                        if tt == NTT - 1:
                            mi.then_inc(stat, 1)

        with nc.Block() as b:
            @b.scalar
            def _(sc):
                sc.activation(SC1[:, :], PA[:, :], AF.Square, bias=0.0, scale=1.0)
        with nc.Block() as b:
            @b.vector
            def _(v):
                v.tensor_sub(SC1[:, :], PB[:, :], SC1[:, :])
        with nc.Block() as b:
            @b.scalar
            def _(sc):
                sc.activation(SC1[:, :], SC1[:, :], AF.Ln, bias=EPSC[:, 0:1], scale=1.0)
                sc.activation(PB[:, :], SC1[:, :], AF.Exp, bias=0.0, scale=-0.5)

        with nc.Block() as b:
            @b.sync
            def _(s):
                for c in range(CH):
                    if c >= 2:
                        s.wait_ge(vx, c - 1)
                    s.dma_start(XF[:, (c % 2) * T:(c % 2) * T + T],
                                x_in[c * 128:(c + 1) * 128, :]).then_inc(ldx2, 16)

            @b.vector
            def _(v):
                for c in range(CH):
                    v.wait_ge(ldx2, 16 * (c + 1))
                    sl = slice((c % 2) * T, (c % 2) * T + T)
                    v.tensor_sub(SC1[:, :], XF[:, sl], PA[:, :])
                    v.scalar_tensor_tensor(SC1[:, :], SC1[:, :], GAM[:, c:c + 1], PB[:, :],
                                           AL.mult, AL.mult)
                    v.tensor_scalar(XN[:, c * T:(c + 1) * T], SC1[:, :], BET[:, c:c + 1], None,
                                    AL.add).then_inc(vx, 1)

        with nc.Block() as b:
            @b.tensor
            def _(t):
                t.wait_ge(wcast, 0)
                for m in range(16):
                    ps = PA if (m % 2 == 0) else PB
                    if m == 0:
                        t.wait_ge(ldw, 16 * CH)
                    if m >= 2:
                        t.wait_ge(pfree, m - 1)
                    for tt in range(NTT):
                        for k in range(CH):
                            mi = t.matmul(ps[:, tt * TTW:(tt + 1) * TTW],
                                          WXB[:, k * 2 * D + m * 128: k * 2 * D + (m + 1) * 128],
                                          XN[:, k * T + tt * TTW: k * T + (tt + 1) * TTW],
                                          start=(k == 0), stop=(k == CH - 1))
                            if tt == NTT - 1 and k == CH - 1:
                                mi.then_inc(mmd, 1)

            @b.scalar
            def _(sc):
                for m in range(16):
                    ps = PA if (m % 2 == 0) else PB
                    sc.wait_ge(mmd, m + 1)
                    if m < 8:
                        if m >= 1:
                            sc.wait_ge(vdone, m)
                        sc.activation(SC1[:, :], ps[:, :], AF.Abs, bias=BXP[:, m:m + 1], scale=1.0)
                        sc.activation(SC1[:, :], SC1[:, :], AF.Exp, bias=0.0, scale=-1.0)
                        sc.activation(SC1[:, :], SC1[:, :], AF.Ln, bias=1.0,
                                      scale=1.0).then_inc(spsc, 1)
                    else:
                        c = m - 8
                        sc.activation(BG[:, c * T:(c + 1) * T], ps[:, :], AF.Identity,
                                      bias=BXP[:, m:m + 1], scale=1.0).then_inc(pfree, 1)

            @b.vector
            def _(v):
                for m in range(8):
                    ps = PA if (m % 2 == 0) else PB
                    v.wait_ge(spsc, m + 1)
                    v.tensor_scalar(SC2[:, :], ps[:, :], BXP[:, m:m + 1], 0.0, AL.add, AL.max)
                    ai = v.tensor_add(DG[:, m * T:(m + 1) * T], SC2[:, :], SC1[:, :])
                    ai.then_inc(pfree, 1)
                    v.drain().then_inc(vdone, 1)

        with nc.Block() as b:
            @b.scalar
            def _(sc):
                for c in range(CH):
                    for s in range(16):
                        i = c * 16 + s
                        if i >= 2:
                            sc.wait_ge(gadd, i - 1)
                        dstE = PA if (i % 2 == 0) else PB
                        sc.activation(dstE[:, :], DG[:, c * T:(c + 1) * T], AF.Exp,
                                      bias=0.0,
                                      scale=ACOL[:, c * 16 + s:c * 16 + s + 1]).then_inc(gexp, 1)

            @b.vector
            def _(v):
                for c in range(CH):
                    for s in range(16):
                        i = c * 16 + s
                        v.wait_ge(gexp, i + 1)
                        srcE = PA if (i % 2 == 0) else PB
                        if s == 0:
                            v.tensor_copy(GA[:, :], srcE[:, :]).then_inc(gadd, 1)
                        else:
                            v.tensor_add(GA[:, :], GA[:, :], srcE[:, :]).then_inc(gadd, 1)
                    v.tensor_copy(DG[:, c * T:(c + 1) * T], GA[:, :])

        with nc.Block() as b:
            @b.vector
            def _(v):
                for c in range(CH):
                    v.tensor_mul(BG[:, c * T:(c + 1) * T], XN[:, c * T:(c + 1) * T],
                                 BG[:, c * T:(c + 1) * T])
                for c in range(CH):
                    for b_ in range(B):
                        sl = slice(c * T + b_ * LLOC, c * T + (b_ + 1) * LLOC)
                        v.tensor_tensor_scan(BG[:, sl], ONEC[:, 0:1].broadcast_to([128, LLOC]),
                                             BG[:, sl], 0.0, AL.mult, AL.add)

        with nc.Block() as b:
            @b.vector
            def _(v):
                for c in range(CH):
                    v.tensor_copy(SS[:, c * B:(c + 1) * B],
                                  bass.AP(BG, c * T + LLOC - 1, [[CH * T, 128], [LLOC, B]]))
        if debug:
            with nc.Block() as b:
                @b.sync
                def _(s):
                    s.dma_start(fm(xn_out), sb3(XN, CH, T)).then_inc(sto, 16)
                    s.dma_start(fm(h_out), sb3(BG, CH, T)).then_inc(sto, 16)
                    s.wait_ge(sto, 32)
        with nc.Block() as b:
            @b.vector
            def _(v):
                for c in range(CH):
                    v.tensor_mul(SC1[:, :], BG[:, c * T:(c + 1) * T], DG[:, c * T:(c + 1) * T])
                    v.scalar_tensor_tensor(XN[:, c * T:(c + 1) * T], XN[:, c * T:(c + 1) * T],
                                           DP[:, c:c + 1], SC1[:, :], AL.mult, AL.add)

        with nc.Block() as b:
            @b.sync
            def _(s):
                base = 32 if debug else 0
                s.dma_start(fm(m0_out), sb3(XN, CH, T)).then_inc(sto, 16)
                s.dma_start(fm(gate_out), sb3(DG, CH, T)).then_inc(sto, 16)
                s.dma_start(S_out[:, :], SS[:, :]).then_inc(sto, 16)
                s.wait_ge(sto, base + 48)

    return nc



def build_B(debug=False):
    nc = bass.Bass("TRN2", target_bir_lowering=False, detect_race_conditions=False)
    x_in = nc.dram_tensor("x_in", [D, T], F32, kind="ExternalInput")
    m0_in = nc.dram_tensor("m0_in", [D, T], BF16, kind="ExternalInput")
    gate_in = nc.dram_tensor("gate_in", [D, T], BF16, kind="ExternalInput")
    S_in = nc.dram_tensor("S_in", [NCORES, 128, CH * B], BF16, kind="ExternalInput")
    msk_in = nc.dram_tensor("msk_in", [128, NCORES], F32, kind="ExternalInput")
    w0_in = nc.dram_tensor("w0_in", [D, D], F32, kind="ExternalInput")
    b0_in = nc.dram_tensor("b0_in", [128, CH], F32, kind="ExternalInput")
    ones_in = nc.dram_tensor("ones_in", [128, 128], BF16, kind="ExternalInput")

    x1_out = nc.dram_tensor("x1_out", [D, T], F32, kind="ExternalOutput")
    amax_out = nc.dram_tensor("amax_out", [128, 1], F32, kind="ExternalOutput")

    with ExitStack() as _ctx:
        _e = _ctx.enter_context
        M0 = _e(nc.sbuf_tensor("M0", [128, CH * T], BF16))
        GT = _e(nc.sbuf_tensor("GT", [128, CH * T], BF16))
        V = _e(nc.sbuf_tensor("V", [128, CH * T], F32))
        W0B = _e(nc.sbuf_tensor("W0B", [128, CH * D], BF16))
        XF = _e(nc.sbuf_tensor("XF", [128, 2 * T], F32))
        XB = _e(nc.sbuf_tensor("XB", [128, 2 * T], BF16))
        XSQ = _e(nc.sbuf_tensor("XSQ", [128, 2 * T], BF16))
        SC1 = _e(nc.sbuf_tensor("SC1", [128, T], F32))
        SC2 = _e(nc.sbuf_tensor("SC2", [128, T], F32))
        SALL = _e(nc.sbuf_tensor("SALL", [128, NCORES * CH * B], BF16))
        MSK = _e(nc.sbuf_tensor("MSK", [128, NCORES], F32))
        OFF = _e(nc.sbuf_tensor("OFF", [128, CH * B], F32))
        B0C = _e(nc.sbuf_tensor("B0C", [128, CH], F32))
        ONESB = _e(nc.sbuf_tensor("ONESB", [128, 128], BF16))
        EPSC = _e(nc.sbuf_tensor("EPSC", [128, 1], F32))
        AMX = _e(nc.sbuf_tensor("AMX", [128, CH], F32))
        AMX1 = _e(nc.sbuf_tensor("AMX1", [128, 1], F32))
        PA = _e(nc.psum_tensor("PA", [128, T], F32))
        PB = _e(nc.psum_tensor("PB", [128, T], F32))
        ldc = _e(nc.semaphore("ldc"))
        ldm = _e(nc.semaphore("ldm"))
        ldw = _e(nc.semaphore("ldw"))
        ldx = _e(nc.semaphore("ldx"))
        ldx2 = _e(nc.semaphore("ldx2"))
        mmd = _e(nc.semaphore("mmd"))
        scd = _e(nc.semaphore("scd"))
        pfree = _e(nc.semaphore("pfree"))
        vdone = _e(nc.semaphore("vdone"))
        xprep = _e(nc.semaphore("xprep"))
        stat = _e(nc.semaphore("stat"))
        vx = _e(nc.semaphore("vx"))
        sto = _e(nc.semaphore("sto"))

        with nc.Block() as b:
            @b.sync
            def _(s):
                s.dma_start(SALL[:, :], S_in[:, :, :].rearrange("c p n -> p c n")).then_inc(ldc, 16)
                s.dma_start(MSK[:, :], msk_in[:, :]).then_inc(ldc, 16)
                s.dma_start(B0C[:, :], b0_in[:, :]).then_inc(ldc, 16)
                s.dma_start(ONESB[:, :], ones_in[:, :]).then_inc(ldc, 16)
                s.dma_start(sb3(M0, CH, T), fm(m0_in)).then_inc(ldm, 16)
                s.dma_start(sb3(GT, CH, T), fm(gate_in)).then_inc(ldm, 16)

            @b.gpsimd
            def _(g):
                for k in range(CH):
                    g.dma_start(W0B[:, k * D:(k + 1) * D],
                                w0_in[k * 128:(k + 1) * 128, :]).then_inc(ldw, 16)

            @b.vector
            def _(v):
                v.memset(EPSC[:, :], EPS)
                v.memset(OFF[:, :], 0.0)
                v.memset(AMX[:, :], 0.0)
                v.memset(AMX1[:, :], 0.0)

        with nc.Block() as b:
            @b.vector
            def _(v):
                v.wait_ge(ldc, 32)
                for cp in range(NCORES):
                    v.scalar_tensor_tensor(OFF[:, :], SALL[:, cp * CH * B:(cp + 1) * CH * B],
                                           MSK[:, cp:cp + 1], OFF[:, :], AL.mult, AL.add)
                v.wait_ge(ldm, 32)
                for c in range(CH):
                    for b_ in range(B):
                        sl = slice(c * T + b_ * LLOC, c * T + (b_ + 1) * LLOC)
                        v.scalar_tensor_tensor(M0[:, sl], GT[:, sl],
                                               OFF[:, c * B + b_:c * B + b_ + 1],
                                               M0[:, sl], AL.mult, AL.add)

        with nc.Block() as b:
            @b.sync
            def _(s):
                for m in range(CH):
                    if m >= 2:
                        s.wait_ge(vdone, m - 1)
                    s.dma_start(XF[:, (m % 2) * T:(m % 2) * T + T],
                                x_in[m * 128:(m + 1) * 128, :]).then_inc(ldx, 16)

            @b.tensor
            def _(t):
                t.wait_ge(ldw, 16 * CH)
                for m in range(CH):
                    ps = PA if (m % 2 == 0) else PB
                    if m >= 2:
                        t.wait_ge(pfree, m - 1)
                    for tt in range(NTT):
                        for k in range(CH):
                            mi = t.matmul(ps[:, tt * TTW:(tt + 1) * TTW],
                                          W0B[:, k * D + m * 128: k * D + (m + 1) * 128],
                                          M0[:, k * T + tt * TTW: k * T + (tt + 1) * TTW],
                                          start=(k == 0), stop=(k == CH - 1))
                            if tt == NTT - 1 and k == CH - 1:
                                mi.then_inc(mmd, 1)

            @b.scalar
            def _(sc):
                sc.wait_ge(ldc, 48)
                for m in range(CH):
                    ps = PA if (m % 2 == 0) else PB
                    sc.wait_ge(mmd, m + 1)
                    if m >= 1:
                        sc.wait_ge(vdone, m)
                    sc.activation(SC1[:, :], ps[:, :], AF.Identity,
                                  bias=B0C[:, m:m + 1], scale=1.0).then_inc(scd, 1)

            @b.vector
            def _(v):
                for m in range(CH):
                    v.wait_ge(scd, m + 1)
                    v.wait_ge(ldx, 16 * (m + 1))
                    sl = slice((m % 2) * T, (m % 2) * T + T)
                    ai = v.scalar_tensor_tensor(V[:, m * T:(m + 1) * T], XF[:, sl], 2.0,
                                                SC1[:, :], AL.mult, AL.add)
                    ai.then_inc(pfree, 1)
                    v.drain().then_inc(vdone, 1)

        with nc.Block() as b:
            @b.scalar
            def _(sc):
                for c in range(CH):
                    if c >= 2:
                        sc.wait_ge(stat, c - 1)
                    sl = slice((c % 2) * T, (c % 2) * T + T)
                    sc.activation(XB[:, sl], V[:, c * T:(c + 1) * T], AF.Identity,
                                  bias=0.0, scale=1.0)
                    sc.activation(XSQ[:, sl], V[:, c * T:(c + 1) * T], AF.Square,
                                  bias=0.0, scale=1.0).then_inc(xprep, 1)

            @b.tensor
            def _(t):
                t.wait_ge(ldc, 64)
                for c in range(CH):
                    t.wait_ge(xprep, c + 1)
                    sl0 = (c % 2) * T
                    for tt in range(NTT):
                        t.matmul(PA[:, tt * TTW:(tt + 1) * TTW], ONESB[:, :],
                                 XB[:, sl0 + tt * TTW: sl0 + (tt + 1) * TTW],
                                 start=(c == 0), stop=(c == CH - 1))
                        mi = t.matmul(PB[:, tt * TTW:(tt + 1) * TTW], ONESB[:, :],
                                      XSQ[:, sl0 + tt * TTW: sl0 + (tt + 1) * TTW],
                                      start=(c == 0), stop=(c == CH - 1))
                        if tt == NTT - 1:
                            mi.then_inc(stat, 1)
        with nc.Block() as b:
            @b.scalar
            def _(sc):
                sc.activation(SC1[:, :], PA[:, :], AF.Square, bias=0.0, scale=1.0)
        with nc.Block() as b:
            @b.vector
            def _(v):
                v.tensor_sub(SC1[:, :], PB[:, :], SC1[:, :])
        with nc.Block() as b:
            @b.scalar
            def _(sc):
                sc.activation(SC1[:, :], SC1[:, :], AF.Ln, bias=EPSC[:, 0:1], scale=1.0)
                sc.activation(PB[:, :], SC1[:, :], AF.Exp, bias=0.0, scale=-0.5)
        with nc.Block() as b:
            @b.vector
            def _(v):
                for c in range(CH):
                    v.tensor_sub(V[:, c * T:(c + 1) * T], V[:, c * T:(c + 1) * T], PA[:, :])
                    v.tensor_mul(V[:, c * T:(c + 1) * T], V[:, c * T:(c + 1) * T], PB[:, :])
                    v.tensor_reduce(AMX[:, c:c + 1], V[:, c * T:(c + 1) * T], AX.X,
                                    AL.max, apply_absolute_value=True)
                v.tensor_reduce(AMX1[:, :], AMX[:, :], AX.X, AL.max,
                                apply_absolute_value=True)

        with nc.Block() as b:
            @b.sync
            def _(s):
                s.dma_start(fm(x1_out), sb3(V, CH, T)).then_inc(sto, 16)
                s.dma_start(amax_out[:, :], AMX1[:, :]).then_inc(sto, 16)
                s.wait_ge(sto, 32)

    return nc


def prep_B_inputs(inputs, resA):
    x = np.asarray(inputs["L0_attn_x"], np.float32)
    w0 = np.ascontiguousarray(np.asarray(inputs["L0_attn_w_out"], np.float32).T)
    Sall = np.stack([np.asarray(resA[c]["S_out"]) for c in range(NCORES)])
    consts = dict(
        S_in=Sall,
        b0_in=colz(inputs["L0_attn_b_out"], CH),
        w0_in=w0,
        ones_in=np.full((128, 128), 1.0 / 1024.0, dtype=NBF),
    )
    in_maps = []
    for c in range(NCORES):
        sl = x[:, c * LLOC:(c + 1) * LLOC, :]
        xc = np.ascontiguousarray(sl.transpose(2, 0, 1).reshape(D, T))
        msk = np.zeros((128, NCORES), np.float32)
        msk[:, :c] = 1.0
        in_maps.append({"x_in": xc, "m0_in": np.asarray(resA[c]["m0_out"]),
                        "gate_in": np.asarray(resA[c]["gate_out"]), "msk_in": msk, **consts})
    return in_maps



def emit_pfold_max(nc, buf, scr, width, sem, cnt, pre=None):
    first = [True]
    for step in [64, 32, 16, 8, 4, 2, 1]:
        with nc.Block() as b:
            def _mkg(step=step, fw=(pre if first[0] else None)):
                @b.gpsimd
                def _(g):
                    if fw is not None:
                        g.wait_ge(sem, fw)
                    g.dma_start(scr[0:step, 0:width], buf[step:2 * step, 0:width]).then_inc(sem, 16)
            _mkg()
            first[0] = False
        cnt[0] += 16
        with nc.Block() as b:
            def _mkv(step=step, tgt=cnt[0]):
                @b.vector
                def _(v):
                    v.wait_ge(sem, tgt)
                    v.tensor_max(buf[0:step, 0:width], buf[0:step, 0:width], scr[0:step, 0:width])
            _mkv()


def emit_pdouble(nc, bufs, width, sem, cnt):
    for step in [1, 2, 4, 8, 16, 32, 64]:
        with nc.Block() as b:
            def _mkg(step=step, base=cnt[0]):
                @b.gpsimd
                def _(g):
                    g.wait_ge(sem, base)
                    for buf in bufs:
                        g.dma_start(buf[step:2 * step, 0:width], buf[0:step, 0:width]).then_inc(sem, 16)
            _mkg()
        cnt[0] += 16 * len(bufs)
    with nc.Block() as b:
        def _mkw(tgt=cnt[0]):
            @b.vector
            def _(v):
                v.wait_ge(sem, tgt)
        _mkw()



def build_C(debug=False):
    nc = bass.Bass("TRN2", target_bir_lowering=False, detect_race_conditions=False)
    x1_in = nc.dram_tensor("x1_in", [D, T], F32, kind="ExternalInput")
    amax_in = nc.dram_tensor("amax_in", [128, NCORES], F32, kind="ExternalInput")
    win_in = nc.dram_tensor("win_in", [D, DFF], F32, kind="ExternalInput")
    bin_in = nc.dram_tensor("bin_in", [128, CF], F32, kind="ExternalInput")
    eye_in = nc.dram_tensor("eye_in", [128, 128], F32, kind="ExternalInput")

    hid_out = nc.dram_tensor("hid_out", [DFF, T], BF16, kind="ExternalOutput")
    amax2_out = nc.dram_tensor("amax2_out", [128, 1], F32, kind="ExternalOutput")
    if debug:
        xq_out = nc.dram_tensor("xq_out", [D, T], BF16, kind="ExternalOutput")
        se_out = nc.dram_tensor("se_out", [128, CF], F32, kind="ExternalOutput")
        sx_out = nc.dram_tensor("sx_out", [128 * 2, 1], F32, kind="ExternalOutput")

    with ExitStack() as _ctx:
        _e = _ctx.enter_context
        XQ = _e(nc.sbuf_tensor("XQ", [128, CH * T], BF16))
        WQ = _e(nc.sbuf_tensor("WQ", [128, CH * DFF], BF16))
        WST = _e(nc.sbuf_tensor("WST", [128, DFF], F32))
        MRUN = _e(nc.sbuf_tensor("MRUN", [128, DFF], F32))
        SCB = _e(nc.sbuf_tensor("SCB", [128, DFF], F32))
        SC_A = _e(nc.sbuf_tensor("SC_A", [128, DFF], F32))
        IC = _e(nc.sbuf_tensor("IC", [128, DFF], I32))
        XF = _e(nc.sbuf_tensor("XF", [128, 2 * T], F32))
        HST = _e(nc.sbuf_tensor("HST", [128, 2 * T], BF16))
        EY = _e(nc.sbuf_tensor("EY", [128, 128], F32))
        T32 = _e(nc.sbuf_tensor("T32", [32, 128], F32))
        SECB = _e(nc.sbuf_tensor("SECB", [128, CF], F32))
        BIN = _e(nc.sbuf_tensor("BIN", [128, CF], F32))
        AMXL = _e(nc.sbuf_tensor("AMXL", [128, 1], F32))
        SXC = _e(nc.sbuf_tensor("SXC", [128, 1], F32))
        KC = _e(nc.sbuf_tensor("KC", [128, 1], F32))
        AM2 = _e(nc.sbuf_tensor("AM2", [128, CF], F32))
        AM2F = _e(nc.sbuf_tensor("AM2F", [128, 1], F32))
        RBC = _e(nc.sbuf_tensor("RBC", [128, 1], F32))
        FSC = _e(nc.sbuf_tensor("FSC", [128, 16], F32))
        PA = _e(nc.psum_tensor("PA", [128, T], F32))
        PB = _e(nc.psum_tensor("PB", [128, T], F32))
        ldc = _e(nc.semaphore("ldc"))
        ldx = _e(nc.semaphore("ldx"))
        ldw = _e(nc.semaphore("ldw"))
        ldw2 = _e(nc.semaphore("ldw2"))
        fde = _e(nc.semaphore("fde"))
        vste = _e(nc.semaphore("vste"))
        scq = _e(nc.semaphore("scq"))
        vq = _e(nc.semaphore("vq"))
        wqd = _e(nc.semaphore("wqd"))
        mmd = _e(nc.semaphore("mmd"))
        gel = _e(nc.semaphore("gel"))
        amd = _e(nc.semaphore("amd"))
        hsto = _e(nc.semaphore("hsto"))
        sto = _e(nc.semaphore("sto"))

        fc = [0]

        with nc.Block() as b:
            @b.sync
            def _(s):
                s.dma_start(FSC[:, 0:NCORES], amax_in[:, :]).then_inc(ldc, 16)
                s.dma_start(BIN[:, :], bin_in[:, :]).then_inc(ldc, 16)
                s.dma_start(EY[:, :], eye_in[:, :]).then_inc(ldc, 16)

            @b.vector
            def _(v):
                v.memset(RBC[:, :], RB16)
                v.memset(AM2[:, :], 0.0)
                v.memset(AM2F[:, :], 0.0)
                v.wait_ge(ldc, 48)
                v.tensor_reduce(AMXL[:, :], FSC[:, 0:NCORES], AX.X, AL.max,
                                apply_absolute_value=True)
        emit_pfold_max(nc, AMXL, FSC, 1, fde, fc)
        emit_pdouble(nc, [AMXL], 1, fde, fc)
        with nc.Block() as b:
            @b.vector
            def _(v):
                v.tensor_scalar(SC_A[:, 0:512], AMXL[:, 0:1].broadcast_to([128, 512]),
                                1e-8, None, AL.max)
                v.reciprocal(SC_A[:, 512:1024], SC_A[:, 0:512])
                v.tensor_scalar(SXC[:, 0:1], SC_A[:, 512:513], 32767.0, None, AL.mult)
                v.tensor_scalar(KC[:, 0:1], SC_A[:, 0:1], 1.0 / (127.0 * 32767.0),
                                None, AL.mult)

        with nc.Block() as b:
            @b.sync
            def _(s):
                for c in range(CH):
                    if c >= 2:
                        s.wait_ge(vq, c - 1)
                    s.dma_start(XF[:, (c % 2) * T:(c % 2) * T + T],
                                x1_in[c * 128:(c + 1) * 128, :]).then_inc(ldx, 16)

            @b.scalar
            def _(sc):
                for c in range(CH):
                    sc.wait_ge(ldx, 16 * (c + 1))
                    if c >= 1:
                        sc.wait_ge(vq, c)
                    sl = slice((c % 2) * T, (c % 2) * T + T)
                    sc.activation(SC_A[:, 0:T], XF[:, sl], AF.Identity,
                                  bias=RBC[:, 0:1], scale=SXC[:, 0:1]).then_inc(scq, 1)

            @b.vector
            def _(v):
                for c in range(CH):
                    v.wait_ge(scq, c + 1)
                    v.tensor_copy(IC[:, 0:T], SC_A[:, 0:T])
                    v.tensor_scalar(SC_A[:, 0:T], IC[:, 0:T], RS16, 32767.0,
                                    AL.subtract, AL.min)
                    v.tensor_scalar(XQ[:, c * T:(c + 1) * T], SC_A[:, 0:T], -32768.0, None,
                                    AL.max).then_inc(vq, 1)

        with nc.Block() as b:
            @b.sync
            def _(s):
                for k in range(CH):
                    if k >= 1:
                        s.wait_ge(scq, CH + k)
                    s.dma_start(WST[:, :], win_in[k * 128:(k + 1) * 128, :]).then_inc(ldw, 16)

            @b.scalar
            def _(sc):
                for k in range(CH):
                    sc.wait_ge(ldw, 16 * (k + 1))
                    if k >= 1:
                        sc.wait_ge(vste, k)
                    sc.activation(SC_A[:, :], WST[:, :], AF.Abs,
                                  bias=0.0, scale=1.0).then_inc(scq, 1)

            @b.vector
            def _(v):
                for k in range(CH):
                    v.wait_ge(scq, CH + k + 1)
                    if k == 0:
                        v.tensor_copy(MRUN[:, :], SC_A[:, :]).then_inc(vste, 1)
                    else:
                        v.tensor_max(MRUN[:, :], MRUN[:, :], SC_A[:, :]).then_inc(vste, 1)

        emit_pfold_max(nc, MRUN, SC_A, DFF, fde, fc)
        with nc.Block() as b:
            @b.vector
            def _(v):
                v.tensor_scalar(MRUN[0:1, :], MRUN[0:1, :], 1e-8, None, AL.max)
                v.reciprocal(SCB[0:1, :], MRUN[0:1, :])
                v.tensor_scalar(SCB[0:1, :], SCB[0:1, :], 127.0, None, AL.mult)
        emit_pdouble(nc, [SCB], DFF, fde, fc)
        with nc.Block() as b:
            @b.gpsimd
            def _(g):
                g.dma_start(T32[:, :], bass.AP(MRUN, 0, [[DFF, 1], [128, 32], [1, 128]])).then_inc(fde, 16)
        fc[0] += 16
        with nc.Block() as b:
            def _mk(tgt=fc[0]):
                @b.tensor
                def _(t):
                    t.wait_ge(fde, tgt)
                    t.transpose(PB[:, 0:32], T32[:, :], EY[0:32, 0:32])
            _mk()
        with nc.Block() as b:
            @b.vector
            def _(v):
                v.tensor_scalar(SECB[:, :], PB[:, 0:32], KC[:, 0:1], None, AL.mult)

        with nc.Block() as b:
            @b.sync
            def _(s):
                for k in range(CH):
                    if k >= 1:
                        s.wait_ge(wqd, k)
                    s.dma_start(WST[:, :], win_in[k * 128:(k + 1) * 128, :]).then_inc(ldw2, 16)

            @b.vector
            def _(v):
                for k in range(CH):
                    v.wait_ge(ldw2, 16 * (k + 1))
                    v.tensor_mul(SC_A[:, :], WST[:, :], SCB[:, :])
                    v.tensor_scalar(SC_A[:, :], SC_A[:, :], RB8, None, AL.add)
                    v.tensor_copy(IC[:, :], SC_A[:, :])
                    v.tensor_scalar(SC_A[:, :], IC[:, :], RS8, 127.0, AL.subtract, AL.min)
                    v.tensor_scalar(WQ[:, k * DFF:(k + 1) * DFF], SC_A[:, :], -128.0, None,
                                    AL.max).then_inc(wqd, 1)

        with nc.Block() as b:
            @b.tensor
            def _(t):
                t.wait_ge(wqd, CH)
                for m in range(CF):
                    ps = PA if (m % 2 == 0) else PB
                    if m >= 2:
                        t.wait_ge(gel, m - 1)
                    for tt in range(NTT):
                        for k in range(CH):
                            mi = t.matmul(ps[:, tt * TTW:(tt + 1) * TTW],
                                          WQ[:, k * DFF + m * 128: k * DFF + (m + 1) * 128],
                                          XQ[:, k * T + tt * TTW: k * T + (tt + 1) * TTW],
                                          start=(k == 0), stop=(k == CH - 1))
                            if tt == NTT - 1 and k == CH - 1:
                                mi.then_inc(mmd, 1)

            @b.scalar
            def _(sc):
                for m in range(CF):
                    ps = PA if (m % 2 == 0) else PB
                    sc.wait_ge(mmd, m + 1)
                    if m >= 2:
                        sc.wait_ge(amd, m - 1)
                        sc.wait_ge(hsto, 16 * (m - 1))
                    sl = slice((m % 2) * T, (m % 2) * T + T)
                    sc.activation(HST[:, sl], ps[:, :], AF.Gelu,
                                  bias=BIN[:, m:m + 1], scale=SECB[:, m:m + 1]).then_inc(gel, 1)

            @b.vector
            def _(v):
                for m in range(CF):
                    v.wait_ge(gel, m + 1)
                    sl = slice((m % 2) * T, (m % 2) * T + T)
                    v.tensor_reduce(AM2[:, m:m + 1], HST[:, sl], AX.X, AL.max,
                                    apply_absolute_value=True).then_inc(amd, 1)

            @b.sync
            def _(s):
                for m in range(CF):
                    s.wait_ge(gel, m + 1)
                    sl = slice((m % 2) * T, (m % 2) * T + T)
                    s.dma_start(hid_out[m * 128:(m + 1) * 128, :], HST[:, sl]).then_inc(hsto, 16)

        with nc.Block() as b:
            @b.vector
            def _(v):
                v.tensor_reduce(AM2F[:, :], AM2[:, :], AX.X, AL.max,
                                apply_absolute_value=True)
        with nc.Block() as b:
            @b.sync
            def _(s):
                s.wait_ge(hsto, 16 * CF)
                s.dma_start(amax2_out[:, :], AM2F[:, :]).then_inc(sto, 16)
                n = 1
                if debug:
                    s.dma_start(fm(xq_out), sb3(XQ, CH, T)).then_inc(sto, 16)
                    s.dma_start(se_out[:, :], SECB[:, :]).then_inc(sto, 16)
                    s.dma_start(sx_out[0:128, :], SXC[:, :]).then_inc(sto, 16)
                    s.dma_start(sx_out[128:256, :], KC[:, :]).then_inc(sto, 16)
                    n = 5
                s.wait_ge(sto, 16 * n)

    return nc


def prep_C_inputs(inputs, resB):
    amax_all = np.concatenate([np.asarray(resB[c]["amax_out"]) for c in range(NCORES)], axis=1)
    consts = dict(
        amax_in=np.ascontiguousarray(amax_all),
        win_in=np.ascontiguousarray(np.asarray(inputs["L0_ffn_W_in"], np.float32)),
        bin_in=colz(inputs["L0_ffn_b_in"], CF),
        eye_in=np.eye(128, dtype=np.float32),
    )
    in_maps = []
    for c in range(NCORES):
        in_maps.append({"x1_in": np.asarray(resB[c]["x1_out"]), **consts})
    return in_maps



def emit_postln(nc, env, STR):
    XB, XSQ, SC1 = env["XB"], env["XSQ"], env["SC1"]
    ONESB, EPSC = env["ONESB"], env["EPSC"]
    xprep, stat = env["xprep"], env["stat"]
    cnt = env["cnt"]
    xp0 = cnt.get("xprep", 0)
    st0 = cnt.get("stat", 0)
    pid = cnt["pln_id"] = cnt.get("pln_id", 0) + 1
    _ps = nc.psum_tensor(f"pln{pid}", [128, 4096], F32)
    PS = _ps.__enter__()
    PA, PB = PS[:, 0:2048], PS[:, 2048:4096]
    with nc.Block() as b:
        @b.scalar
        def _(sc):
            for c in range(CH):
                if c >= 2:
                    sc.wait_ge(stat, st0 + c - 1)
                sl = slice((c % 2) * T, (c % 2) * T + T)
                sc.activation(XB[:, sl], STR[:, c * T:(c + 1) * T], AF.Identity,
                              bias=0.0, scale=1.0)
                sc.activation(XSQ[:, sl], STR[:, c * T:(c + 1) * T], AF.Square,
                              bias=0.0, scale=1.0).then_inc(xprep, 1)

        @b.tensor
        def _(t):
            for c in range(CH):
                t.wait_ge(xprep, xp0 + c + 1)
                sl0 = (c % 2) * T
                for tt in range(NTT):
                    t.matmul(PA[:, tt * TTW:(tt + 1) * TTW], ONESB[:, :],
                             XB[:, sl0 + tt * TTW: sl0 + (tt + 1) * TTW],
                             start=(c == 0), stop=(c == CH - 1))
                    mi = t.matmul(PB[:, tt * TTW:(tt + 1) * TTW], ONESB[:, :],
                                  XSQ[:, sl0 + tt * TTW: sl0 + (tt + 1) * TTW],
                                  start=(c == 0), stop=(c == CH - 1))
                    if tt == NTT - 1:
                        mi.then_inc(stat, 1)
    cnt["xprep"] = xp0 + CH
    cnt["stat"] = st0 + CH
    with nc.Block() as b:
        @b.scalar
        def _(sc):
            sc.activation(SC1[:, :], PA[:, :], AF.Square, bias=0.0, scale=1.0)
    with nc.Block() as b:
        @b.vector
        def _(v):
            v.tensor_sub(SC1[:, :], PB[:, :], SC1[:, :])
    with nc.Block() as b:
        @b.scalar
        def _(sc):
            sc.activation(SC1[:, :], SC1[:, :], AF.Ln, bias=EPSC[:, 0:1], scale=1.0)
            sc.activation(PB[:, :], SC1[:, :], AF.Exp, bias=0.0, scale=-0.5)
    with nc.Block() as b:
        @b.vector
        def _(v):
            for c in range(CH):
                v.tensor_sub(STR[:, c * T:(c + 1) * T], STR[:, c * T:(c + 1) * T], PA[:, :])
                v.tensor_mul(STR[:, c * T:(c + 1) * T], STR[:, c * T:(c + 1) * T], PB[:, :])
    _ps.__exit__(None, None, None)


def emit_recip_cols(nc, env, dst, src, mult):
    W = env["WIDE"]
    with nc.Block() as b:
        @b.vector
        def _(v):
            v.tensor_scalar(W[:, 0:512], src[:, 0:1].broadcast_to([128, 512]),
                            1e-30, None, AL.max)
            v.reciprocal(W[:, 512:1024], W[:, 0:512])
            v.tensor_scalar(dst[:, 0:1], W[:, 512:513], mult, None, AL.mult)



def _store_y_dbg(nc, y_out, STR, tag):
    with nc.semaphore(f"sty{tag}") as s5:
        with nc.Block() as b:
            @b.sync
            def _(s):
                s.dma_start(fm(y_out), sb3(STR, CH, T)).then_inc(s5, 16)
                s.wait_ge(s5, 16)


def build_D(debug=False, stop=None, d5stop=0, d6stop=0):
    nc = bass.Bass("TRN2", target_bir_lowering=False, detect_race_conditions=False)
    x1_in = nc.dram_tensor("x1_in", [D, T], F32, kind="ExternalInput")
    hid_in = nc.dram_tensor("hid_in", [DFF, T], BF16, kind="ExternalInput")
    amax2_in = nc.dram_tensor("amax2_in", [128, NCORES], F32, kind="ExternalInput")
    wout_in = nc.dram_tensor("wout_in", [DFF, D], F32, kind="ExternalInput")
    bout_in = nc.dram_tensor("bout_in", [128, CH], F32, kind="ExternalInput")
    wq1_in = nc.dram_tensor("wq1_in", [D, 512], F32, kind="ExternalInput")
    bq_in = nc.dram_tensor("bq_in", [128, 4], F32, kind="ExternalInput")
    wp_in = nc.dram_tensor("wp_in", [1, 40], F32, kind="ExternalInput")
    w1o_in = nc.dram_tensor("w1o_in", [D, D], F32, kind="ExternalInput")
    b1_in = nc.dram_tensor("b1_in", [128, CH], F32, kind="ExternalInput")
    wfc_in = nc.dram_tensor("wfc_in", [D, DFF], F32, kind="ExternalInput")
    bfc_in = nc.dram_tensor("bfc_in", [128, CF], F32, kind="ExternalInput")
    afc_in = nc.dram_tensor("afc_in", [D, 16], F32, kind="ExternalInput")
    bfcB_in = nc.dram_tensor("bfcB_in", [16, DFF], F32, kind="ExternalInput")
    wpj_in = nc.dram_tensor("wpj_in", [DFF, D], F32, kind="ExternalInput")
    bpj_in = nc.dram_tensor("bpj_in", [128, CH], F32, kind="ExternalInput")
    apj_in = nc.dram_tensor("apj_in", [DFF, 16], F32, kind="ExternalInput")
    bpjB_in = nc.dram_tensor("bpjB_in", [16, D], F32, kind="ExternalInput")
    ones_in = nc.dram_tensor("ones_in", [128, 128], BF16, kind="ExternalInput")
    eye_in = nc.dram_tensor("eye_in", [128, 128], F32, kind="ExternalInput")
    onesr_in = nc.dram_tensor("onesr_in", [1, 128], BF16, kind="ExternalInput")

    wqo_scr = nc.dram_tensor("wqo_scr", [DFF, D], BF16, kind="Internal")
    hq_scr = nc.dram_tensor("hq_scr", [DFF, T], BF16, kind="Internal")
    hid2_scr = nc.dram_tensor("hid2_scr", [DFF, T], BF16, kind="Internal")
    wfcq_scr = nc.dram_tensor("wfcq_scr", [D, DFF], BF16, kind="Internal")
    wpjq_scr = nc.dram_tensor("wpjq_scr", [DFF, D], BF16, kind="Internal")

    y_out = nc.dram_tensor("y_out", [D, T], F32, kind="ExternalOutput")
    if debug:
        x2_out = nc.dram_tensor("x2_out", [D, T], F32, kind="ExternalOutput")
        x3_out = nc.dram_tensor("x3_out", [D, T], F32, kind="ExternalOutput")
        ofm_out = nc.dram_tensor("ofm_out", [D, T], BF16, kind="ExternalOutput")
        hq_out = nc.dram_tensor("hq_out", [DFF, T], BF16, kind="ExternalOutput")
        wq_out = nc.dram_tensor("wq_out", [DFF, D], BF16, kind="ExternalOutput")
        if stop == "x2":
            pst_out = nc.dram_tensor("pst_out", [CH * 128, TTW], F32, kind="ExternalOutput")
            hqt_out = nc.dram_tensor("hqt_out", [CF * 128, TTW], BF16, kind="ExternalOutput")
            shkh_out = nc.dram_tensor("shkh_out", [256, 1], F32, kind="ExternalOutput")
            gcol_out = nc.dram_tensor("gcol_out", [128, CF], F32, kind="ExternalOutput")
            sra_out = nc.dram_tensor("sra_out", [128, CF], F32, kind="ExternalOutput")
            ama_out = nc.dram_tensor("ama_out", [128, CF], F32, kind="ExternalOutput")

    with ExitStack() as _ctx:
        _e = _ctx.enter_context
        STR = _e(nc.sbuf_tensor("STR", [128, CH * T], F32))
        XB = _e(nc.sbuf_tensor("XB", [128, 2 * T], BF16))
        XSQ = _e(nc.sbuf_tensor("XSQ", [128, 2 * T], BF16))
        SC1 = _e(nc.sbuf_tensor("SC1", [128, T], F32))
        WIDE = _e(nc.sbuf_tensor("WIDE", [128, 1024], F32))
        ONESB = _e(nc.sbuf_tensor("ONESB", [128, 128], BF16))
        EYF = _e(nc.sbuf_tensor("EYF", [128, 128], F32))
        ONER = _e(nc.sbuf_tensor("ONER", [1, 128], BF16))
        EPSC = _e(nc.sbuf_tensor("EPSC", [128, 1], F32))
        RBC = _e(nc.sbuf_tensor("RBC", [128, 1], F32))
        RBC8 = _e(nc.sbuf_tensor("RBC8", [128, 1], F32))
        SHC = _e(nc.sbuf_tensor("SHC", [128, 1], F32))
        KH = _e(nc.sbuf_tensor("KH", [128, 1], F32))
        AMXL = _e(nc.sbuf_tensor("AMXL", [128, 1], F32))
        FSC = _e(nc.sbuf_tensor("FSC", [128, 16], F32))
        GCOL = _e(nc.sbuf_tensor("GCOL", [128, CF], F32))
        BOC = _e(nc.sbuf_tensor("BOC", [128, CH], F32))
        B1C = _e(nc.sbuf_tensor("B1C", [128, CH], F32))
        BQC = _e(nc.sbuf_tensor("BQC", [128, 4], F32))
        BFCC = _e(nc.sbuf_tensor("BFCC", [128, CF], F32))
        BPJC = _e(nc.sbuf_tensor("BPJC", [128, CH], F32))
        WPC = _e(nc.sbuf_tensor("WPC", [128, 40], F32))
        ldc = _e(nc.semaphore("ldc"))
        fde = _e(nc.semaphore("fde"))
        sto = _e(nc.semaphore("sto"))
        fc = [0]
        env = {"XB": XB, "XSQ": XSQ, "SC1": SC1, "ONESB": ONESB, "EPSC": EPSC,
               "xprep": _e(nc.semaphore("xprep")),
               "stat": _e(nc.semaphore("stat")), "cnt": {}, "WIDE": WIDE}

        with nc.Block() as b:
            @b.sync
            def _(s):
                for ap_, src_ in [(FSC[:, 0:NCORES], amax2_in[:, :]), (BOC[:, :], bout_in[:, :]),
                                  (B1C[:, :], b1_in[:, :]), (BQC[:, :], bq_in[:, :]),
                                  (BFCC[:, :], bfc_in[:, :]), (BPJC[:, :], bpj_in[:, :]),
                                  (ONESB[:, :], ones_in[:, :]), (EYF[:, :], eye_in[:, :]),
                                  (ONER[:, :], onesr_in[:, :])]:
                    s.dma_start(ap_, src_).then_inc(ldc, 16)

            @b.vector
            def _(v):
                v.memset(EPSC[:, :], EPS)
                v.memset(RBC[:, :], RB16)
                v.memset(RBC8[:, :], RB8).then_inc(fde, 1)
                v.wait_ge(ldc, 16 * 9)
                v.tensor_reduce(AMXL[:, :], FSC[:, 0:NCORES], AX.X, AL.max,
                                apply_absolute_value=True).then_inc(fde, 1)
        fc[0] = 2
        emit_pfold_max(nc, AMXL, FSC, 1, fde, fc, pre=2)
        emit_pdouble(nc, [AMXL], 1, fde, fc)
        with nc.Block() as b:
            @b.vector
            def _(v):
                v.tensor_scalar(WIDE[:, 0:512], AMXL[:, 0:1].broadcast_to([128, 512]),
                                1e-8, None, AL.max)
                v.reciprocal(WIDE[:, 512:1024], WIDE[:, 0:512])
                v.tensor_scalar(SHC[:, 0:1], WIDE[:, 512:513], 32767.0, None,
                                AL.mult).then_inc(fde, 1)
                v.tensor_scalar(KH[:, 0:1], WIDE[:, 0:1], 1.0 / (127.0 * 32767.0),
                                None, AL.mult).then_inc(fde, 1)

        d0_done = fc[0] * 1 + 2

        if debug and stop == "x2":
            with nc.semaphore("dbg0") as dbg0:
                with nc.Block() as b:
                    @b.sync
                    def _(s):
                        s.dma_start(shkh_out[0:128, :], SHC[:, 0:1]).then_inc(dbg0, 16)
                        s.dma_start(shkh_out[128:256, :], KH[:, 0:1]).then_inc(dbg0, 16)
                        s.wait_ge(dbg0, 32)

        with ExitStack() as _c1:
            WST1 = _c1.enter_context(nc.sbuf_tensor("WST1", [128, 2 * D], F32))
            SCA1 = _c1.enter_context(nc.sbuf_tensor("SCA1", [128, D], F32))
            IC1 = _c1.enter_context(nc.sbuf_tensor("IC1", [128, D], I32))
            WQ1 = _c1.enter_context(nc.sbuf_tensor("WQ1", [128, 2 * D], BF16))
            AMA = _c1.enter_context(nc.sbuf_tensor("AMA", [128, CF], F32))
            SRA = _c1.enter_context(nc.sbuf_tensor("SRA", [128, CF], F32))
            d1a = [_c1.enter_context(nc.semaphore(f"d1a{i_}")) for i_ in range(2)]
            d1r = _c1.enter_context(nc.semaphore("d1r"))
            d1f = [_c1.enter_context(nc.semaphore(f"d1f{i_}")) for i_ in range(2)]
            d1c = _c1.enter_context(nc.semaphore("d1c"))
            d1d = _c1.enter_context(nc.semaphore("d1d"))
            d1e = [_c1.enter_context(nc.semaphore(f"d1e{i_}")) for i_ in range(2)]
            d1w = _c1.enter_context(nc.semaphore("d1w"))
            with nc.Block() as b:
                @b.sync
                def _(s):
                    for k in range(CF):
                        if k >= 2:
                            s.wait_ge(d1r, k - 1)
                        s.dma_start(WST1[:, (k % 2) * D:(k % 2) * D + D],
                                    wout_in[k * 128:(k + 1) * 128, :]).then_inc(d1a[k % 2], 16)

                @b.vector
                def _(v):
                    for k in range(CF):
                        v.wait_ge(d1a[k % 2], 16 * (k // 2 + 1))
                        sl = slice((k % 2) * D, (k % 2) * D + D)
                        v.tensor_reduce(AMA[:, k:k + 1], WST1[:, sl], AX.X, AL.max,
                                        apply_absolute_value=True)
                        v.drain().then_inc(d1r, 1)
                    v.tensor_scalar(AMA[:, :], AMA[:, :], 1e-8, None, AL.max)
                    v.drain()
                    v.reciprocal(SRA[:, :], AMA[:, :])
                    v.drain()
                    v.tensor_scalar(SRA[:, :], SRA[:, :], 127.0, None, AL.mult)
                    v.drain().then_inc(d1w, 1)
                    v.tensor_scalar(GCOL[:, :], AMA[:, :], KH[:, 0:1], None, AL.mult)
                    v.drain().then_inc(d1w, 1)
            with nc.Block() as b:
                @b.sync
                def _(s):
                    for k in range(CF):
                        if k >= 2:
                            s.wait_ge(d1c, k - 1)
                        s.dma_start(WST1[:, (k % 2) * D:(k % 2) * D + D],
                                    wout_in[k * 128:(k + 1) * 128, :]).then_inc(d1f[k % 2], 16)

                @b.scalar
                def _(sc):
                    sc.wait_ge(d1w, 2)
                    for k in range(CF):
                        sc.wait_ge(d1f[k % 2], 16 * (k // 2 + 1))
                        if k >= 1:
                            sc.wait_ge(d1d, k)
                        sl = slice((k % 2) * D, (k % 2) * D + D)
                        sc.activation(SCA1[:, :], WST1[:, sl], AF.Identity,
                                      bias=RBC8[:, 0:1], scale=SRA[:, k:k + 1]).then_inc(d1c, 1)

                @b.vector
                def _(v):
                    for k in range(CF):
                        v.wait_ge(d1c, k + 1)
                        if k >= 2:
                            v.wait_ge(d1e[k % 2], 16 * (k // 2))
                        sl = slice((k % 2) * D, (k % 2) * D + D)
                        v.tensor_copy(IC1[:, :], SCA1[:, :])
                        v.tensor_scalar(SCA1[:, :], IC1[:, :], RS8, 127.0,
                                        AL.subtract, AL.min)
                        v.tensor_scalar(WQ1[:, sl], SCA1[:, :], -128.0, None,
                                        AL.max).then_inc(d1d, 1)

                @b.gpsimd
                def _(g):
                    for k in range(CF):
                        g.wait_ge(d1d, k + 1)
                        sl = slice((k % 2) * D, (k % 2) * D + D)
                        g.dma_start(wqo_scr[k * 128:(k + 1) * 128, :],
                                    WQ1[:, sl]).then_inc(d1e[k % 2], 16)
            with nc.Block() as b:
                @b.vector
                def _(v):
                    v.wait_ge(d1e[0], 16 * (CF // 2))
                    v.wait_ge(d1e[1], 16 * (CF // 2))
        if debug and stop == "x2":
            with nc.semaphore("dbg1") as dbg1:
                with nc.Block() as b:
                    @b.sync
                    def _(s):
                        s.dma_start(gcol_out[:, :], GCOL[:, :]).then_inc(dbg1, 16)
                        s.dma_start(sra_out[:, :], SRA[:, :]).then_inc(dbg1, 16)
                        s.dma_start(ama_out[:, :], AMA[:, :]).then_inc(dbg1, 16)
                        s.wait_ge(dbg1, 48)

        with ExitStack() as _c2:
            HSTI = _c2.enter_context(nc.sbuf_tensor("HSTI", [128, 2 * T], BF16))
            SCA2 = _c2.enter_context(nc.sbuf_tensor("SCA2", [128, T], F32))
            IC2 = _c2.enter_context(nc.sbuf_tensor("IC2", [128, T], I32))
            HQS = _c2.enter_context(nc.sbuf_tensor("HQS", [128, 2 * T], BF16))
            d2a = [_c2.enter_context(nc.semaphore(f"d2a{i_}")) for i_ in range(2)]
            d2b = _c2.enter_context(nc.semaphore("d2b"))
            d2c = _c2.enter_context(nc.semaphore("d2c"))
            d2d = _c2.enter_context(nc.semaphore("d2d"))
            with nc.Block() as b:
                @b.sync
                def _(s):
                    for k in range(CF):
                        if k >= 2:
                            s.wait_ge(d2b, k - 1)
                        s.dma_start(HSTI[:, (k % 2) * T:(k % 2) * T + T],
                                    hid_in[k * 128:(k + 1) * 128, :]).then_inc(d2a[k % 2], 16)

                @b.scalar
                def _(sc):
                    sc.wait_ge(fde, d0_done)
                    for k in range(CF):
                        sc.wait_ge(d2a[k % 2], 16 * (k // 2 + 1))
                        if k >= 1:
                            sc.wait_ge(d2c, k)
                        sl = slice((k % 2) * T, (k % 2) * T + T)
                        sc.activation(SCA2[:, :], HSTI[:, sl], AF.Identity,
                                      bias=RBC[:, 0:1], scale=SHC[:, 0:1]).then_inc(d2b, 1)

                @b.vector
                def _(v):
                    for k in range(CF):
                        v.wait_ge(d2b, k + 1)
                        if k >= 2:
                            v.wait_ge(d2d, 16 * (k - 1))
                        sl = slice((k % 2) * T, (k % 2) * T + T)
                        v.tensor_copy(IC2[:, :], SCA2[:, :])
                        v.tensor_scalar(SCA2[:, :], IC2[:, :], RS16, 32767.0,
                                        AL.subtract, AL.min)
                        v.tensor_scalar(HQS[:, sl], SCA2[:, :], -32768.0, GCOL[:, k:k + 1],
                                        AL.max, AL.mult).then_inc(d2c, 1)

                @b.gpsimd
                def _(g):
                    for k in range(CF):
                        g.wait_ge(d2c, k + 1)
                        sl = slice((k % 2) * T, (k % 2) * T + T)
                        g.dma_start(hq_scr[k * 128:(k + 1) * 128, :],
                                    HQS[:, sl]).then_inc(d2d, 16)
            with nc.Block() as b:
                @b.vector
                def _(v):
                    v.wait_ge(d2d, 16 * CF)

        with ExitStack() as _c3:
            PSALL = _c3.enter_context(nc.psum_tensor("PS3", [128, 4096], F32))
            HQT = _c3.enter_context(nc.sbuf_tensor("HQT", [128, CF * TTW], BF16))
            WQD = _c3.enter_context(nc.sbuf_tensor("WQD", [128, 2 * D], BF16))
            X1T = _c3.enter_context(nc.sbuf_tensor("X1T", [128, 2 * TTW], F32))
            d3w = [_c3.enter_context(nc.semaphore(f"d3w{i_}")) for i_ in range(2)]
            d3xs = [_c3.enter_context(nc.semaphore(f"d3xs{i_}")) for i_ in range(2)]
            d3h = _c3.enter_context(nc.semaphore("d3h"))
            d3k = _c3.enter_context(nc.semaphore("d3k"))
            d3s = _c3.enter_context(nc.semaphore("d3s"))
            d3v = _c3.enter_context(nc.semaphore("d3v"))
            d3x = _c3.enter_context(nc.semaphore("d3x"))
            dbgD = debug and stop == "x2"
            if dbgD:
                dbg3 = _c3.enter_context(nc.semaphore("dbg3"))
            with nc.Block() as b:
                @b.sync
                def _(s):
                    for tt in range(NTT):
                        for k in range(CF):
                            i = tt * CF + k
                            if i >= 2:
                                s.wait_ge(d3k, i - 1)
                            s.dma_start(WQD[:, (k % 2) * D:(k % 2) * D + D],
                                        wqo_scr[k * 128:(k + 1) * 128, :]).then_inc(d3w[k % 2], 16)
                        for m in range(CH):
                            j = tt * CH + m
                            if j >= 2:
                                s.wait_ge(d3v, j - 1)
                            s.dma_start(X1T[:, (j % 2) * TTW:(j % 2 + 1) * TTW],
                                        x1_in[m * 128:(m + 1) * 128,
                                              tt * TTW:(tt + 1) * TTW]).then_inc(d3xs[j % 2], 16)

                @b.gpsimd
                def _(g):
                    for tt in range(NTT):
                        if tt >= 1:
                            g.wait_ge(d3k, tt * CF)
                        if dbgD and tt == 1:
                            g.wait_ge(dbg3, 16 * (CH + CF))
                        for k in range(CF):
                            g.dma_start(HQT[:, k * TTW:(k + 1) * TTW],
                                        hq_scr[k * 128:(k + 1) * 128,
                                               tt * TTW:(tt + 1) * TTW]).then_inc(d3h, 16)

                @b.tensor
                def _(t):
                    for tt in range(NTT):
                        if tt >= 1:
                            t.wait_ge(d3s, 8 * tt)
                        t.wait_ge(d3h, 16 * CF * (tt + 1))
                        for k in range(CF):
                            i = tt * CF + k
                            t.wait_ge(d3w[k % 2], 16 * (i // 2 + 1))
                            for m in range(CH):
                                mi = t.matmul(PSALL[:, m * TTW:(m + 1) * TTW],
                                              WQD[:, (k % 2) * D + m * 128:(k % 2) * D + (m + 1) * 128],
                                              HQT[:, k * TTW:(k + 1) * TTW],
                                              start=(k == 0), stop=(k == CF - 1))
                            mi.then_inc(d3k, 1)

                @b.scalar
                def _(sc):
                    for tt in range(NTT):
                        if dbgD and tt == 0:
                            sc.wait_ge(d3k, CF)
                            for mm_ in range(CH):
                                sc.activation(SC1[:, (mm_ % 4) * TTW:(mm_ % 4 + 1) * TTW],
                                              PSALL[:, mm_ * TTW:(mm_ + 1) * TTW],
                                              AF.Identity, bias=EPSC[:, 0:1], scale=1.0)
                                sc.dma_start(pst_out[mm_ * 128:(mm_ + 1) * 128, :],
                                             SC1[:, (mm_ % 4) * TTW:(mm_ % 4 + 1) * TTW]).then_inc(dbg3, 16)
                                if mm_ % 4 == 3:
                                    sc.wait_ge(dbg3, 16 * (mm_ + 1))
                            for kk_ in range(CF):
                                sc.dma_start(hqt_out[kk_ * 128:(kk_ + 1) * 128, :],
                                             HQT[:, kk_ * TTW:(kk_ + 1) * TTW]).then_inc(dbg3, 16)
                            sc.wait_ge(dbg3, 16 * (CH + CF))
                        for m in range(CH):
                            sc.wait_ge(d3k, (tt + 1) * CF)
                            if tt * CH + m >= 4:
                                sc.wait_ge(d3v, tt * CH + m - 3)
                            sc.activation(SC1[:, (m % 4) * TTW:(m % 4 + 1) * TTW],
                                          PSALL[:, m * TTW:(m + 1) * TTW], AF.Identity,
                                          bias=BOC[:, m:m + 1], scale=1.0).then_inc(d3s, 1)

                @b.vector
                def _(v):
                    for tt in range(NTT):
                        for m in range(CH):
                            j = tt * CH + m
                            v.wait_ge(d3s, j + 1)
                            v.wait_ge(d3xs[j % 2], 16 * (j // 2 + 1))
                            v.tensor_add(STR[:, m * T + tt * TTW: m * T + (tt + 1) * TTW],
                                         SC1[:, (m % 4) * TTW:(m % 4 + 1) * TTW],
                                         X1T[:, (j % 2) * TTW:(j % 2 + 1) * TTW]).then_inc(d3v, 1)

            with nc.Block() as b:
                @b.sync
                def _(s):
                    s.wait_ge(d3v, NTT * CH)

        if debug and stop == "x2":
            with nc.Block() as b:
                @b.sync
                def _(s):
                    s.dma_start(fm(x3_out), sb3(STR, CH, T)).then_inc(sto, 16)
                    s.wait_ge(sto, 16)
        emit_postln(nc, env, STR)
        if debug:
            with nc.Block() as b:
                @b.sync
                def _(s):
                    base = 16 if (debug and stop == "x2") else 0
                    s.dma_start(fm(x2_out), sb3(STR, CH, T)).then_inc(sto, 16)
                    s.dma_start(hq_out[:, :], hq_scr[:, :]).then_inc(sto, 16)
                    s.dma_start(wq_out[:, :], wqo_scr[:, :]).then_inc(sto, 16)
                    s.wait_ge(sto, base + 48)
        if stop == "x2":
            with nc.Block() as b:
                @b.sync
                def _(s):
                    base = 64 if debug else 0
                    s.dma_start(fm(y_out), sb3(STR, CH, T)).then_inc(sto, 16)
                    s.wait_ge(sto, base + 16)
            return nc

        with ExitStack() as _c5o:
            VBUF = _c5o.enter_context(nc.sbuf_tensor("VBUF", [128, 10 * 1024], BF16))
            AB = _c5o.enter_context(nc.sbuf_tensor("AB", [128, 1280], BF16))
            d5s3 = _c5o.enter_context(nc.semaphore("d5s3"))
            with ExitStack() as _c5ab:
                QTM = _c5ab.enter_context(nc.sbuf_tensor("QTM", [128, 16 * 512], BF16))
                KBUF = _c5ab.enter_context(nc.sbuf_tensor("KBUF", [128, 10 * 512], BF16))
                with ExitStack() as _c5a:
                    _e5 = _c5a.enter_context
                    EYB = _e5(nc.sbuf_tensor("EYB", [128, 128], BF16))
                    WPB = _e5(nc.sbuf_tensor("WPB", [1, 40], BF16))
                    WPCC = _e5(nc.sbuf_tensor("WPCC", [128, 40], F32))
                    WI = _e5(nc.sbuf_tensor("WI", [128, 40 * 128], BF16))
                    WQR = _e5(nc.sbuf_tensor("WQR", [128, 8 * 512], BF16))
                    XC5 = _e5(nc.sbuf_tensor("XC5", [128, 2 * 512], BF16))
                    QF = _e5(nc.sbuf_tensor("QF", [128, 2 * 512], BF16))
                    XTM = _e5(nc.sbuf_tensor("XTM", [128, 8 * 1024], BF16))
                    PS5 = _e5(nc.psum_tensor("PS5", [128, 2048], F32))
                    PTB = _e5(nc.psum_tensor("PTB", [128, 2048], BF16))
                    PTF = _e5(nc.psum_tensor("PTF", [128, 1024], F32))
                    d5l = _e5(nc.semaphore("d5l"))
                    d5w = _e5(nc.semaphore("d5w"))
                    d5c = _e5(nc.semaphore("d5c"))
                    d5m = _e5(nc.semaphore("d5m"))
                    d5q = _e5(nc.semaphore("d5q"))
                    d5t = _e5(nc.semaphore("d5t"))
                    d5e = _e5(nc.semaphore("d5e"))
                    d5f = _e5(nc.semaphore("d5f"))
                    d5g = _e5(nc.semaphore("d5g"))
                    d5h = _e5(nc.semaphore("d5h"))
                    d5v = _e5(nc.semaphore("d5v"))
                    d5i = _e5(nc.semaphore("d5i"))
                    d5j = _e5(nc.semaphore("d5j"))
                    with nc.Block() as b:
                        @b.gpsimd
                        def _(g):
                            g.dma_start(WPB[:, :], wp_in[:, :]).then_inc(d5l, 16)
                            for k in range(CH):
                                g.dma_start(WQR[:, k * 512:(k + 1) * 512],
                                            wq1_in[k * 128:(k + 1) * 128, :]).then_inc(d5l, 16)

                        @b.vector
                        def _(v):
                            v.tensor_copy(EYB[:, :], EYF[:, :])
                            v.drain().then_inc(d5w, 1)

                        @b.tensor
                        def _(t):
                            t.wait_ge(d5l, 16 * 9)
                            t.wait_ge(d5w, 1)
                            t.matmul(PS5[:, 0:40], ONER[:, :], WPB[0:1, 0:40],
                                     start=True, stop=True).then_inc(d5w, 1)

                        @b.vector
                        def _(v):
                            v.wait_ge(d5w, 2)
                            v.tensor_copy(WPCC[:, :], PS5[:, 0:40])
                            v.drain()
                            for j in range(40):
                                v.tensor_scalar(WI[:, j * 128:(j + 1) * 128], EYB[:, :],
                                                WPCC[:, j:j + 1], None, AL.mult)
                            v.drain().then_inc(d5w, 1)

                    if d5stop == 1:
                        _store_y_dbg(nc, y_out, STR, 1)
                        return nc
                    with nc.Block() as b:
                        @b.vector
                        def _(v):
                            for tt in range(NTT):
                                for k in range(CH):
                                    i = tt * CH + k
                                    if i >= 2:
                                        v.wait_ge(d5m, i - 1)
                                    v.tensor_copy(XC5[:, (i % 2) * 512:(i % 2) * 512 + 512],
                                                  STR[:, k * T + tt * TTW: k * T + (tt + 1) * TTW])
                                    v.drain().then_inc(d5c, 1)
                                for cq in range(4):
                                    for j in range(4):
                                        n = tt * 16 + cq * 4 + j
                                        v.wait_ge(d5t, n + 1)
                                        v.tensor_copy(
                                            QTM[:, (tt * 4 + j) * 512 + cq * 128:
                                                   (tt * 4 + j) * 512 + (cq + 1) * 128],
                                            PTB[:, (n % 2) * 1024:(n % 2) * 1024 + 128])
                                        v.drain().then_inc(d5e, 1)

                        @b.tensor
                        def _(t):
                            for tt in range(NTT):
                                if tt >= 1:
                                    t.wait_ge(d5q, tt * 4)
                                for k in range(CH):
                                    i = tt * CH + k
                                    t.wait_ge(d5c, i + 1)
                                    for cq in range(4):
                                        mi = t.matmul(PS5[:, cq * 512:(cq + 1) * 512],
                                                      WQR[:, k * 512 + cq * 128:k * 512 + (cq + 1) * 128],
                                                      XC5[:, (i % 2) * 512:(i % 2) * 512 + 512],
                                                      start=(k == 0), stop=(k == CH - 1))
                                    mi.then_inc(d5m, 1)
                                for cq in range(4):
                                    t.wait_ge(d5q, tt * 4 + cq + 1)
                                    for j in range(4):
                                        n = tt * 16 + cq * 4 + j
                                        if n >= 2:
                                            t.wait_ge(d5e, n - 1)
                                        t.transpose(PTB[:, (n % 2) * 1024:(n % 2) * 1024 + 128],
                                                    QF[:, (cq % 2) * 512 + j * 128:
                                                          (cq % 2) * 512 + (j + 1) * 128],
                                                    EYB[:, :]).then_inc(d5t, 1)

                        @b.scalar
                        def _(sc):
                            for tt in range(NTT):
                                sc.wait_ge(d5m, tt * CH + CH)
                                for cq in range(4):
                                    if cq >= 2:
                                        sc.wait_ge(d5t, tt * 16 + (cq - 1) * 4)
                                    sc.activation(QF[:, (cq % 2) * 512:(cq % 2) * 512 + 512],
                                                  PS5[:, cq * 512:(cq + 1) * 512], AF.Identity,
                                                  bias=BQC[:, cq:cq + 1],
                                                  scale=1.0).then_inc(d5q, 1)

                    if d5stop == 2:
                        _store_y_dbg(nc, y_out, STR, 2)
                        return nc
                    for hf in range(2):
                        with nc.Block() as b:
                            @b.tensor
                            def _(t, hf=hf):
                                for n in range(64):
                                    bb, cd = n // 8, n % 8
                                    if n >= 2:
                                        t.wait_ge(d5f, hf * 64 + n - 1)
                                    t.transpose(PTF[:, (n % 2) * 512:(n % 2) * 512 + 128],
                                                STR[:, cd * T + bb * LLOC + hf * 128:
                                                       cd * T + bb * LLOC + hf * 128 + 128],
                                                EYF[:, :]).then_inc(d5g, 1)
                                t.wait_ge(d5f, hf * 64 + 64)
                                t.wait_ge(d5w, 3)
                                for p in range(5):
                                    for dh in range(2):
                                        g_ = p * 2 + dh
                                        if g_ >= 2:
                                            t.wait_ge(d5v, hf * 10 + g_ - 1)
                                        for bb in range(8):
                                            mi = t.matmul(PS5[:, (g_ % 2) * 512:(g_ % 2) * 512 + 512],
                                                          WI[:, (bb * 5 + p) * 128:(bb * 5 + p + 1) * 128],
                                                          XTM[:, bb * 1024 + dh * 512:bb * 1024 + (dh + 1) * 512],
                                                          start=(bb == 0), stop=(bb == 7))
                                        mi.then_inc(d5h, 1)
                                for p in range(5):
                                    if p >= 2:
                                        t.wait_ge(d5j, hf * 5 + p - 1)
                                    for bb in range(8):
                                        mi = t.matmul(PS5[:, 1024 + (p % 2) * 512:1024 + (p % 2) * 512 + 512],
                                                      WI[:, (bb * 5 + p) * 128:(bb * 5 + p + 1) * 128],
                                                      QTM[:, (bb * 2 + hf) * 512:(bb * 2 + hf + 1) * 512],
                                                      start=(bb == 0), stop=(bb == 7))
                                    mi.then_inc(d5i, 1)

                            @b.vector
                            def _(v, hf=hf):
                                for n in range(64):
                                    v.wait_ge(d5g, hf * 64 + n + 1)
                                    v.tensor_copy(XTM[:, (n // 8) * 1024 + (n % 8) * 128:
                                                     (n // 8) * 1024 + (n % 8 + 1) * 128],
                                                  PTF[:, (n % 2) * 512:(n % 2) * 512 + 128])
                                    v.drain().then_inc(d5f, 1)
                                for p in range(5):
                                    for dh in range(2):
                                        g_ = p * 2 + dh
                                        v.wait_ge(d5h, hf * 10 + g_ + 1)
                                        v.tensor_copy(VBUF[:, (hf * 5 + p) * 1024 + dh * 512:
                                                         (hf * 5 + p) * 1024 + (dh + 1) * 512],
                                                      PS5[:, (g_ % 2) * 512:(g_ % 2) * 512 + 512])
                                        v.drain().then_inc(d5v, 1)
                                for p in range(5):
                                    v.wait_ge(d5i, hf * 5 + p + 1)
                                    v.tensor_copy(KBUF[:, (hf * 5 + p) * 512:(hf * 5 + p + 1) * 512],
                                                  PS5[:, 1024 + (p % 2) * 512:1024 + (p % 2) * 512 + 512])
                                    v.drain().then_inc(d5j, 1)

                if d5stop == 3:
                    _store_y_dbg(nc, y_out, STR, 3)
                    return nc
                with ExitStack() as _c5b:
                    LSB = _c5b.enter_context(nc.sbuf_tensor("LSB", [128, 1280], F32))
                    LGP = _c5b.enter_context(nc.sbuf_tensor("LGP", [128, 1280], F32))
                    MXB = _c5b.enter_context(nc.sbuf_tensor("MXB", [128, 256], F32))
                    SMB = _c5b.enter_context(nc.sbuf_tensor("SMB", [128, 256], F32))
                    d5s1 = _c5b.enter_context(nc.semaphore("d5s1"))
                    d5s2 = _c5b.enter_context(nc.semaphore("d5s2"))
                    with nc.Block() as b:
                        @b.vector
                        def _(v):
                            for hf in range(2):
                                for h in range(16):
                                    qv = bass.AP(QTM, hf * 512 + h * 32,
                                                 [[16 * 512, 128], [1024, 8], [0, 5], [1, 32]])
                                    kv = bass.AP(KBUF, hf * 2560 + h,
                                                 [[10 * 512, 128], [0, 8], [512, 5], [16, 32]])
                                    v.tensor_mul(bass.AP(LGP, 0,
                                                         [[1280, 128], [160, 8], [32, 5], [1, 32]]),
                                                 qv, kv)
                                    v.tensor_reduce(bass.AP(LSB, hf * 640 + h * 40,
                                                            [[1280, 128], [5, 8], [1, 5]]),
                                                    bass.AP(LGP, 0,
                                                            [[1280, 128], [32, 40], [1, 32]]),
                                                    AX.X, AL.add)
                            v.drain()
                            v.tensor_reduce(MXB[:, :],
                                            bass.AP(LSB, 0, [[1280, 128], [5, 256], [1, 5]]),
                                            AX.X, AL.max)
                            v.drain()
                            v.tensor_sub(bass.AP(LSB, 0, [[1280, 128], [5, 256], [1, 5]]),
                                         bass.AP(LSB, 0, [[1280, 128], [5, 256], [1, 5]]),
                                         bass.AP(MXB, 0, [[256, 128], [1, 256], [0, 5]]))
                            v.drain().then_inc(d5s1, 1)

                        @b.scalar
                        def _(sc):
                            sc.wait_ge(d5s1, 1)
                            sc.activation(LGP[:, :], LSB[:, :], AF.Exp,
                                          bias=0.0, scale=0.125).then_inc(d5s2, 1)

                        @b.vector
                        def _(v):
                            v.wait_ge(d5s2, 1)
                            v.tensor_reduce(SMB[:, :],
                                            bass.AP(LGP, 0, [[1280, 128], [5, 256], [1, 5]]),
                                            AX.X, AL.add)
                            v.drain()
                            v.reciprocal(MXB[:, :], SMB[:, :])
                            v.drain()
                            v.tensor_mul(bass.AP(AB, 0, [[1280, 128], [5, 256], [1, 5]]),
                                         bass.AP(LGP, 0, [[1280, 128], [5, 256], [1, 5]]),
                                         bass.AP(MXB, 0, [[256, 128], [1, 256], [0, 5]]))
                            v.drain().then_inc(d5s3, 1)

            if d5stop == 4:
                _store_y_dbg(nc, y_out, STR, 4)
                return nc
            with ExitStack() as _c5c:
                _e5c = _c5c.enter_context
                OFM = _e5c(nc.sbuf_tensor("OFM", [128, CH * T], BF16))
                OTMB = _e5c(nc.sbuf_tensor("OTMB", [128, 2 * 1024], F32))
                PRD = _e5c(nc.sbuf_tensor("PRD", [128, 320], F32))
                W1R = _e5c(nc.sbuf_tensor("W1R", [128, 8 * 1024], BF16))
                d5o = _e5c(nc.semaphore("d5o"))
                d5r = _e5c(nc.semaphore("d5r"))
                d5p = _e5c(nc.semaphore("d5p"))
                d5u = _e5c(nc.semaphore("d5u"))
                with ExitStack() as _c5t:
                    PTF2 = _c5t.enter_context(nc.psum_tensor("PTF2", [128, 1024], F32))
                    with nc.Block() as b:
                        @b.gpsimd
                        def _(g):
                            for k in range(CH):
                                g.dma_start(W1R[:, k * 1024:(k + 1) * 1024],
                                            w1o_in[k * 128:(k + 1) * 128, :]).then_inc(d5u, 16)

                        @b.vector
                        def _(v):
                            v.wait_ge(d5s3, 1)
                            for m in range(16):
                                hf, bb = m // 8, m % 8
                                if m >= 2:
                                    v.wait_ge(d5r, (m - 1) * 8)
                                for h in range(16):
                                    av = bass.AP(AB, hf * 640 + h * 40 + bb * 5,
                                                 [[1280, 128], [0, 64], [1, 5]])
                                    vv = bass.AP(VBUF, hf * 5 * 1024 + h,
                                                 [[10 * 1024, 128], [16, 64], [1024, 5]])
                                    v.tensor_mul(bass.AP(PRD, 0, [[320, 128], [5, 64], [1, 5]]),
                                                 av, vv)
                                    v.tensor_reduce(bass.AP(OTMB, (m % 2) * 1024 + h,
                                                            [[2048, 128], [16, 64]]),
                                                    bass.AP(PRD, 0, [[320, 128], [5, 64], [1, 5]]),
                                                    AX.X, AL.add)
                                v.drain().then_inc(d5o, 1)
                                if m >= 1:
                                    mm_ = m - 1
                                    for cd in range(CH):
                                        v.wait_ge(d5r, mm_ * 8 + cd + 1)
                                        hfp, bbp = mm_ // 8, mm_ % 8
                                        v.tensor_copy(
                                            OFM[:, cd * T + bbp * LLOC + hfp * 128:
                                                   cd * T + bbp * LLOC + hfp * 128 + 128],
                                            PTF2[:, ((mm_ * 8 + cd) % 2) * 512:
                                                    ((mm_ * 8 + cd) % 2) * 512 + 128])
                                        v.drain().then_inc(d5p, 1)
                            mm_ = 15
                            for cd in range(CH):
                                v.wait_ge(d5r, mm_ * 8 + cd + 1)
                                v.tensor_copy(
                                    OFM[:, cd * T + (mm_ % 8) * LLOC + (mm_ // 8) * 128:
                                           cd * T + (mm_ % 8) * LLOC + (mm_ // 8) * 128 + 128],
                                    PTF2[:, ((mm_ * 8 + cd) % 2) * 512:
                                            ((mm_ * 8 + cd) % 2) * 512 + 128])
                                v.drain().then_inc(d5p, 1)

                        @b.tensor
                        def _(t):
                            for m in range(16):
                                t.wait_ge(d5o, m + 1)
                                for cd in range(CH):
                                    n = m * 8 + cd
                                    if n >= 2:
                                        t.wait_ge(d5p, n - 1)
                                    t.transpose(PTF2[:, (n % 2) * 512:(n % 2) * 512 + 128],
                                                OTMB[:, (m % 2) * 1024 + cd * 128:
                                                        (m % 2) * 1024 + (cd + 1) * 128],
                                                EYF[:, :]).then_inc(d5r, 1)
                if d5stop == 5:
                    _store_y_dbg(nc, y_out, STR, 5)
                    return nc
                with ExitStack() as _c5w:
                    PW = _c5w.enter_context(nc.psum_tensor("PW", [128, 4096], F32))
                    d5z = _c5w.enter_context(nc.semaphore("d5z"))
                    d5x2 = _c5w.enter_context(nc.semaphore("d5x2"))
                    d5y = _c5w.enter_context(nc.semaphore("d5y"))
                    with nc.Block() as b:
                        @b.tensor
                        def _(t):
                            t.wait_ge(d5u, 16 * CH)
                            for tt in range(NTT):
                                if tt >= 1:
                                    t.wait_ge(d5y, 8 * tt)
                                for k in range(CH):
                                    for m in range(CH):
                                        mi = t.matmul(PW[:, m * TTW:(m + 1) * TTW],
                                                      W1R[:, k * 1024 + m * 128:k * 1024 + (m + 1) * 128],
                                                      OFM[:, k * T + tt * TTW:k * T + (tt + 1) * TTW],
                                                      start=(k == 0), stop=(k == CH - 1))
                                    mi.then_inc(d5z, 1)

                        @b.scalar
                        def _(sc):
                            for tt in range(NTT):
                                sc.wait_ge(d5z, (tt + 1) * CH)
                                for m in range(CH):
                                    j = tt * CH + m
                                    if j >= 4:
                                        sc.wait_ge(d5y, j - 3)
                                    sc.activation(SC1[:, (m % 4) * TTW:(m % 4 + 1) * TTW],
                                                  PW[:, m * TTW:(m + 1) * TTW], AF.Identity,
                                                  bias=B1C[:, m:m + 1], scale=1.0).then_inc(d5x2, 1)

                        @b.vector
                        def _(v):
                            for tt in range(NTT):
                                for m in range(CH):
                                    j = tt * CH + m
                                    v.wait_ge(d5x2, j + 1)
                                    v.tensor_add(STR[:, m * T + tt * TTW:m * T + (tt + 1) * TTW],
                                                 SC1[:, (m % 4) * TTW:(m % 4 + 1) * TTW],
                                                 STR[:, m * T + tt * TTW:m * T + (tt + 1) * TTW]
                                                 ).then_inc(d5y, 1)
                            v.drain()
                if debug:
                    with nc.semaphore("dbg5") as dbg5:
                        with nc.Block() as b:
                            @b.sync
                            def _(s):
                                s.dma_start(fm(ofm_out), sb3(OFM, CH, T)).then_inc(dbg5, 16)
                                s.wait_ge(dbg5, 16)
        emit_postln(nc, env, STR)
        if debug:
            with nc.semaphore("dbg6") as dbg6:
                with nc.Block() as b:
                    @b.sync
                    def _(s):
                        s.dma_start(fm(x3_out), sb3(STR, CH, T)).then_inc(dbg6, 16)
                        s.wait_ge(dbg6, 16)
        if stop == "x3":
            with nc.Block() as b:
                @b.sync
                def _(s):
                    s.dma_start(fm(y_out), sb3(STR, CH, T)).then_inc(sto, 16)
                    s.wait_ge(sto, (48 if debug else 0) + 16)
            return nc

        for (w_in_, scr_, nch_, wcols_, tag_) in [(wfc_in, wfcq_scr, CH, DFF, "fc"),
                                                  (wpj_in, wpjq_scr, CF, D, "pj")]:
            with ExitStack() as _c6q:
                _e6 = _c6q.enter_context
                WFST = _e6(nc.sbuf_tensor(f"WFST{tag_}", [128, 2 * wcols_], F32))
                SCAQ = _e6(nc.sbuf_tensor(f"SCAQ{tag_}", [128, wcols_], F32))
                ICQ = _e6(nc.sbuf_tensor(f"ICQ{tag_}", [128, wcols_], I32))
                WQQ = _e6(nc.sbuf_tensor(f"WQQ{tag_}", [128, 2 * wcols_], BF16))
                AMA6 = _e6(nc.sbuf_tensor(f"AMA6{tag_}", [128, nch_], F32))
                AMT = _e6(nc.sbuf_tensor(f"AMT{tag_}", [128, 1], F32))
                SRF = _e6(nc.sbuf_tensor(f"SRF{tag_}", [128, 1], F32))
                SCF = _e6(nc.sbuf_tensor(f"SCF{tag_}", [128, 1], F32))
                q6a = [_e6(nc.semaphore(f"q6a{tag_}{i_}")) for i_ in range(2)]
                q6r = _e6(nc.semaphore(f"q6r{tag_}"))
                q6f = _e6(nc.semaphore(f"q6f{tag_}"))
                q6b = [_e6(nc.semaphore(f"q6b{tag_}{i_}")) for i_ in range(2)]
                q6c = _e6(nc.semaphore(f"q6c{tag_}"))
                q6d = _e6(nc.semaphore(f"q6d{tag_}"))
                q6e = [_e6(nc.semaphore(f"q6e{tag_}{i_}")) for i_ in range(2)]
                with nc.Block() as b:
                    def _mk1(w_in_=w_in_, nch_=nch_, wcols_=wcols_):
                        @b.sync
                        def _(s):
                            for k in range(nch_):
                                if k >= 2:
                                    s.wait_ge(q6r, k - 1)
                                s.dma_start(WFST[:, (k % 2) * wcols_:(k % 2 + 1) * wcols_],
                                            w_in_[k * 128:(k + 1) * 128, :]).then_inc(q6a[k % 2], 16)

                        @b.vector
                        def _(v):
                            for k in range(nch_):
                                v.wait_ge(q6a[k % 2], 16 * (k // 2 + 1))
                                v.tensor_reduce(AMA6[:, k:k + 1],
                                                WFST[:, (k % 2) * wcols_:(k % 2 + 1) * wcols_],
                                                AX.X, AL.max, apply_absolute_value=True)
                                v.drain().then_inc(q6r, 1)
                            v.tensor_reduce(AMT[:, :], AMA6[:, :], AX.X, AL.max)
                            v.drain().then_inc(q6f, 1)
                    _mk1()
                fcq = [1]
                emit_pfold_max(nc, AMT, FSC, 1, q6f, fcq, pre=1)
                emit_pdouble(nc, [AMT], 1, q6f, fcq)
                with nc.Block() as b:
                    @b.vector
                    def _(v):
                        v.tensor_scalar(WIDE[:, 0:512], AMT[:, 0:1].broadcast_to([128, 512]),
                                        1e-30, None, AL.max)
                        v.drain()
                        v.reciprocal(WIDE[:, 512:1024], WIDE[:, 0:512])
                        v.drain()
                        v.tensor_scalar(SRF[:, 0:1], WIDE[:, 512:513], 127.0, None, AL.mult)
                        v.drain()
                        v.tensor_scalar(SCF[:, 0:1], AMT[:, 0:1], 1.0 / 127.0, None, AL.mult)
                        v.drain().then_inc(q6f, 1)
                with nc.Block() as b:
                    def _mk2(w_in_=w_in_, scr_=scr_, nch_=nch_, wcols_=wcols_, tgt=fcq[0] + 1):
                        @b.sync
                        def _(s):
                            for k in range(nch_):
                                if k >= 2:
                                    s.wait_ge(q6c, k - 1)
                                s.dma_start(WFST[:, (k % 2) * wcols_:(k % 2 + 1) * wcols_],
                                            w_in_[k * 128:(k + 1) * 128, :]).then_inc(q6b[k % 2], 16)

                        @b.scalar
                        def _(sc):
                            sc.wait_ge(q6f, tgt)
                            for k in range(nch_):
                                sc.wait_ge(q6b[k % 2], 16 * (k // 2 + 1))
                                if k >= 1:
                                    sc.wait_ge(q6d, k)
                                sc.activation(SCAQ[:, :],
                                              WFST[:, (k % 2) * wcols_:(k % 2 + 1) * wcols_],
                                              AF.Identity, bias=RBC8[:, 0:1],
                                              scale=SRF[:, 0:1]).then_inc(q6c, 1)

                        @b.vector
                        def _(v):
                            for k in range(nch_):
                                v.wait_ge(q6c, k + 1)
                                if k >= 2:
                                    v.wait_ge(q6e[k % 2], 16 * (k // 2))
                                v.tensor_copy(ICQ[:, :], SCAQ[:, :])
                                v.tensor_scalar(SCAQ[:, :], ICQ[:, :], RS8, 127.0,
                                                AL.subtract, AL.min)
                                v.tensor_scalar(WQQ[:, (k % 2) * wcols_:(k % 2 + 1) * wcols_],
                                                SCAQ[:, :], -128.0, SCF[:, 0:1],
                                                AL.max, AL.mult).then_inc(q6d, 1)

                        @b.gpsimd
                        def _(g):
                            for k in range(nch_):
                                g.wait_ge(q6d, k + 1)
                                g.dma_start(scr_[k * 128:(k + 1) * 128, :],
                                            WQQ[:, (k % 2) * wcols_:(k % 2 + 1) * wcols_]
                                            ).then_inc(q6e[k % 2], 16)
                    _mk2()
                with nc.Block() as b:
                    def _mkw(nch_=nch_):
                        @b.vector
                        def _(v):
                            v.wait_ge(q6e[0], 16 * ((nch_ + 1) // 2))
                            v.wait_ge(q6e[1], 16 * (nch_ // 2))
                    _mkw()
        if d6stop == 1:
            _store_y_dbg(nc, y_out, STR, 61)
            return nc

        with ExitStack() as _c6h:
            _e6h = _c6h.enter_context
            X3B = _e6h(nc.sbuf_tensor("X3B", [128, CH * T], BF16))
            T1B = _e6h(nc.sbuf_tensor("T1B", [128, T], BF16))
            AFCB = _e6h(nc.sbuf_tensor("AFCB", [128, CH * 16], BF16))
            BFCB = _e6h(nc.sbuf_tensor("BFCB", [128, DFF], BF16))
            WGR = _e6h(nc.sbuf_tensor("WGR", [128, CH * 512], BF16))
            HG = _e6h(nc.sbuf_tensor("HG", [128, 2 * 2048], BF16))
            PS6 = _e6h(nc.psum_tensor("PS6", [128, 4096], F32))
            d6l = _e6h(nc.semaphore("d6l"))
            d6x3 = _e6h(nc.semaphore("d6x3"))
            d6t1 = _e6h(nc.semaphore("d6t1"))
            d6g = _e6h(nc.semaphore("d6g"))
            d6m = _e6h(nc.semaphore("d6m"))
            d6e2 = _e6h(nc.semaphore("d6e2"))
            d6s = _e6h(nc.semaphore("d6s"))
            with nc.Block() as b:
                @b.gpsimd
                def _(g):
                    for k in range(CH):
                        g.dma_start(AFCB[:, k * 16:(k + 1) * 16],
                                    afc_in[k * 128:(k + 1) * 128, :]).then_inc(d6l, 16)
                    g.dma_start(BFCB[0:16, :], bfcB_in[:, :]).then_inc(d6l, 16)

                @b.vector
                def _(v):
                    for k in range(CH):
                        v.tensor_copy(X3B[:, k * T:(k + 1) * T], STR[:, k * T:(k + 1) * T])
                    v.drain().then_inc(d6x3, 1)

                @b.tensor
                def _(t):
                    t.wait_ge(d6l, 16 * 9)
                    t.wait_ge(d6x3, 1)
                    for tt in range(NTT):
                        for k in range(CH):
                            mi = t.matmul(PS6[0:16, tt * 512:(tt + 1) * 512],
                                          AFCB[:, k * 16:(k + 1) * 16],
                                          X3B[:, k * T + tt * TTW:k * T + (tt + 1) * TTW],
                                          start=(k == 0), stop=(k == CH - 1))
                        mi.then_inc(d6t1, 1)

                @b.scalar
                def _(sc):
                    sc.wait_ge(d6t1, NTT)
                    sc.activation(T1B[0:16, :], PS6[0:16, 0:2048], AF.Identity,
                                  bias=0.0, scale=2.0).then_inc(d6t1, 1)
            with nc.Block() as b:
                @b.sync
                def _(s):
                    for mg in range(8):
                        if mg >= 1:
                            s.wait_ge(d6m, 4 * mg)
                        for k in range(CH):
                            s.dma_start(WGR[:, k * 512:(k + 1) * 512],
                                        wfcq_scr[k * 128:(k + 1) * 128,
                                                 mg * 512:(mg + 1) * 512]).then_inc(d6g, 16)

                @b.tensor
                def _(t):
                    t.wait_ge(d6t1, NTT + 1)
                    for mg in range(8):
                        t.wait_ge(d6g, 128 * (mg + 1))
                        for tt in range(NTT):
                            q = mg * 4 + tt
                            if q >= 2:
                                t.wait_ge(d6e2, 4 * (q - 1))
                            for k in range(CH):
                                for j in range(4):
                                    t.matmul(PS6[:, ((q % 2) * 4 + j) * 512:
                                                    ((q % 2) * 4 + j + 1) * 512],
                                             WGR[:, k * 512 + j * 128:k * 512 + (j + 1) * 128],
                                             X3B[:, k * T + tt * TTW:k * T + (tt + 1) * TTW],
                                             start=(k == 0), stop=False)
                            for j in range(4):
                                mi = t.matmul(PS6[:, ((q % 2) * 4 + j) * 512:
                                                    ((q % 2) * 4 + j + 1) * 512],
                                              BFCB[0:16, (mg * 4 + j) * 128:(mg * 4 + j + 1) * 128],
                                              T1B[0:16, tt * 512:(tt + 1) * 512],
                                              start=False, stop=True)
                            mi.then_inc(d6m, 1)

                @b.scalar
                def _(sc):
                    for q in range(32):
                        mg, tt = q // 4, q % 4
                        sc.wait_ge(d6m, q + 1)
                        if q >= 2:
                            sc.wait_ge(d6s, 64 * (q - 1))
                        for j in range(4):
                            sc.activation(HG[:, (q % 2) * 2048 + j * 512:
                                                (q % 2) * 2048 + (j + 1) * 512],
                                          PS6[:, ((q % 2) * 4 + j) * 512:
                                                  ((q % 2) * 4 + j + 1) * 512],
                                          AF.Gelu, bias=BFCC[:, mg * 4 + j:mg * 4 + j + 1],
                                          scale=1.0).then_inc(d6e2, 1)

                @b.gpsimd
                def _(g):
                    for q in range(32):
                        mg, tt = q // 4, q % 4
                        for j in range(4):
                            g.wait_ge(d6e2, 4 * q + j + 1)
                            g.dma_start(hid2_scr[(mg * 4 + j) * 128:(mg * 4 + j + 1) * 128,
                                                 tt * 512:(tt + 1) * 512],
                                        HG[:, (q % 2) * 2048 + j * 512:
                                              (q % 2) * 2048 + (j + 1) * 512]).then_inc(d6s, 16)
            with nc.Block() as b:
                @b.vector
                def _(v):
                    v.wait_ge(d6s, 64 * 32)
        if d6stop == 2:
            _store_y_dbg(nc, y_out, STR, 62)
            return nc

        with ExitStack() as _c6p:
            _e6p = _c6p.enter_context
            H1T = _e6p(nc.sbuf_tensor("H1T", [128, CF * TTW], BF16))
            APJB = _e6p(nc.sbuf_tensor("APJB", [128, CF * 16], BF16))
            T2B = _e6p(nc.sbuf_tensor("T2B", [128, T], BF16))
            BPJB = _e6p(nc.sbuf_tensor("BPJB", [128, D], BF16))
            WQD6 = _e6p(nc.sbuf_tensor("WQD6", [128, 2 * D], BF16))
            PS6B = _e6p(nc.psum_tensor("PS6B", [128, 4096], F32))
            d6l2 = _e6p(nc.semaphore("d6l2"))
            d6h2 = _e6p(nc.semaphore("d6h2"))
            d6w2 = [_e6p(nc.semaphore(f"d6w2{i_}")) for i_ in range(2)]
            d6q2 = _e6p(nc.semaphore("d6q2"))
            d6t2 = _e6p(nc.semaphore("d6t2"))
            d6k = _e6p(nc.semaphore("d6k"))
            d6lo = _e6p(nc.semaphore("d6lo"))
            d6sc = _e6p(nc.semaphore("d6sc"))
            d6v2 = _e6p(nc.semaphore("d6v2"))
            with nc.Block() as b:
                @b.gpsimd
                def _(g):
                    for k in range(CF):
                        g.dma_start(APJB[:, k * 16:(k + 1) * 16],
                                    apj_in[k * 128:(k + 1) * 128, :]).then_inc(d6l2, 16)
                    g.dma_start(BPJB[0:16, :], bpjB_in[:, :]).then_inc(d6l2, 16)
                    for tt in range(NTT):
                        if tt >= 1:
                            g.wait_ge(d6k, CF * tt)
                        for k in range(CF):
                            g.dma_start(H1T[:, k * TTW:(k + 1) * TTW],
                                        hid2_scr[k * 128:(k + 1) * 128,
                                                 tt * TTW:(tt + 1) * TTW]).then_inc(d6h2, 16)

                @b.sync
                def _(s):
                    for tt in range(NTT):
                        for k in range(CF):
                            i = tt * CF + k
                            if i >= 2:
                                s.wait_ge(d6k, i - 1)
                            s.dma_start(WQD6[:, (k % 2) * D:(k % 2) * D + D],
                                        wpjq_scr[k * 128:(k + 1) * 128, :]
                                        ).then_inc(d6w2[k % 2], 16)

                @b.tensor
                def _(t):
                    t.wait_ge(d6l2, 16 * (CF + 1))
                    for tt in range(NTT):
                        t.wait_ge(d6h2, 16 * CF * (tt + 1))
                        if tt >= 1:
                            t.wait_ge(d6sc, 8 * tt)
                        for k in range(CF):
                            mi = t.matmul(PS6B[0:16, 0:512], APJB[:, k * 16:(k + 1) * 16],
                                          H1T[:, k * TTW:(k + 1) * TTW],
                                          start=(k == 0), stop=(k == CF - 1))
                        mi.then_inc(d6q2, 1)
                        t.wait_ge(d6t2, tt + 1)
                        for k in range(CF):
                            i = tt * CF + k
                            t.wait_ge(d6w2[k % 2], 16 * (i // 2 + 1))
                            for m in range(CH):
                                t.matmul(PS6B[:, m * TTW:(m + 1) * TTW],
                                         WQD6[:, (k % 2) * D + m * 128:(k % 2) * D + (m + 1) * 128],
                                         H1T[:, k * TTW:(k + 1) * TTW],
                                         start=(k == 0), stop=False).then_inc(d6k, 1) \
                                    if m == CH - 1 else \
                                    t.matmul(PS6B[:, m * TTW:(m + 1) * TTW],
                                             WQD6[:, (k % 2) * D + m * 128:(k % 2) * D + (m + 1) * 128],
                                             H1T[:, k * TTW:(k + 1) * TTW],
                                             start=(k == 0), stop=False)
                        for m in range(CH):
                            mi = t.matmul(PS6B[:, m * TTW:(m + 1) * TTW],
                                          BPJB[0:16, m * 128:(m + 1) * 128],
                                          T2B[0:16, tt * 512:(tt + 1) * 512],
                                          start=False, stop=True)
                        mi.then_inc(d6lo, 1)

                @b.scalar
                def _(sc):
                    for tt in range(NTT):
                        sc.wait_ge(d6q2, tt + 1)
                        sc.activation(T2B[0:16, tt * 512:(tt + 1) * 512], PS6B[0:16, 0:512],
                                      AF.Identity, bias=0.0, scale=2.0).then_inc(d6t2, 1)
                        sc.wait_ge(d6lo, tt + 1)
                        for m in range(CH):
                            j = tt * CH + m
                            if j >= 4:
                                sc.wait_ge(d6v2, j - 3)
                            sc.activation(SC1[:, (m % 4) * TTW:(m % 4 + 1) * TTW],
                                          PS6B[:, m * TTW:(m + 1) * TTW], AF.Identity,
                                          bias=BPJC[:, m:m + 1], scale=1.0).then_inc(d6sc, 1)

                @b.vector
                def _(v):
                    for tt in range(NTT):
                        for m in range(CH):
                            j = tt * CH + m
                            v.wait_ge(d6sc, j + 1)
                            v.tensor_add(STR[:, m * T + tt * TTW:m * T + (tt + 1) * TTW],
                                         SC1[:, (m % 4) * TTW:(m % 4 + 1) * TTW],
                                         STR[:, m * T + tt * TTW:m * T + (tt + 1) * TTW]
                                         ).then_inc(d6v2, 1)
                    v.drain()
        emit_postln(nc, env, STR)
        with nc.Block() as b:
            @b.sync
            def _(s):
                base = 48 if debug else 0
                s.dma_start(fm(y_out), sb3(STR, CH, T)).then_inc(sto, 16)
                s.wait_ge(sto, base + 16)

    return nc


def prep_D_inputs(inputs, resB, resC):
    amax2_all = np.concatenate([np.asarray(resC[c]["amax2_out"]) for c in range(NCORES)], axis=1)
    w_kv = np.asarray(inputs["L1_attn_w_kv"], np.float32)
    wp_row = np.ascontiguousarray(w_kv[0:5, 0:8].T.reshape(1, 40))
    consts = dict(
        amax2_in=np.ascontiguousarray(amax2_all),
        wout_in=np.ascontiguousarray(np.asarray(inputs["L0_ffn_W_out"], np.float32)),
        bout_in=colz(inputs["L0_ffn_b_out"], CH),
        wq1_in=np.ascontiguousarray(np.asarray(inputs["L1_attn_w_q"], np.float32).T),
        bq_in=colz(inputs["L1_attn_b_q"], 4),
        wp_in=wp_row,
        w1o_in=np.ascontiguousarray(np.asarray(inputs["L1_attn_w_out"], np.float32).T),
        b1_in=colz(inputs["L1_attn_b_out"], CH),
        wfc_in=np.ascontiguousarray(np.asarray(inputs["L1_ffn_W_fc"], np.float32).T),
        bfc_in=colz(inputs["L1_ffn_b_fc"], CF),
        afc_in=np.ascontiguousarray(np.asarray(inputs["L1_ffn_A_fc"], np.float32).T),
        bfcB_in=np.ascontiguousarray(np.asarray(inputs["L1_ffn_B_fc"], np.float32).T),
        wpj_in=np.ascontiguousarray(np.asarray(inputs["L1_ffn_W_proj"], np.float32).T),
        bpj_in=colz(inputs["L1_ffn_b_proj"], CH),
        apj_in=np.ascontiguousarray(np.asarray(inputs["L1_ffn_A_proj"], np.float32).T),
        bpjB_in=np.ascontiguousarray(np.asarray(inputs["L1_ffn_B_proj"], np.float32).T),
        ones_in=np.full((128, 128), 1.0 / 1024.0, dtype=NBF),
        eye_in=np.eye(128, dtype=np.float32),
        onesr_in=np.ones((1, 128), dtype=NBF),
    )
    in_maps = []
    for c in range(NCORES):
        in_maps.append({"x1_in": np.asarray(resB[c]["x1_out"]),
                        "hid_in": np.asarray(resC[c]["hid_out"]), **consts})
    return in_maps



def prep_A_inputs(inputs):
    x = np.asarray(inputs["L0_attn_x"], np.float32)
    wx = np.ascontiguousarray(np.asarray(inputs["L0_attn_w_xproj"], np.float32).T)
    Amat = np.asarray(inputs["L0_attn_A"], np.float32).reshape(CH, 128, 16)
    acol = np.ascontiguousarray(Amat.transpose(1, 0, 2).reshape(128, CH * 16))
    consts = dict(
        wx_in=wx,
        bxp_in=colz(inputs["L0_attn_b_xproj"], 16),
        A_in=acol,
        gam_in=colz(inputs["L0_attn_gamma"], CH),
        bet_in=colz(inputs["L0_attn_beta"], CH),
        dp_in=colz(inputs["L0_attn_D_param"], CH),
        ones_in=np.full((128, 128), 1.0 / 1024.0, dtype=NBF),
    )
    in_maps = []
    for c in range(NCORES):
        sl = x[:, c * LLOC:(c + 1) * LLOC, :]
        xc = np.ascontiguousarray(sl.transpose(2, 0, 1).reshape(D, T))
        in_maps.append({"x_in": xc, **consts})
    return in_maps


_BUILT = {}


def _get(name, fn):
    if name not in _BUILT:
        _BUILT[name] = fn()
    return _BUILT[name]


def kernel(**inputs):
    cores = list(range(NCORES))
    ncA = _get("A", build_A)
    resA = run_bass_kernel_spmd(ncA, prep_A_inputs(inputs), core_ids=cores).results
    resA = [{k: np.asarray(v) for k, v in r.items()} for r in resA]
    ncB = _get("B", build_B)
    resB = run_bass_kernel_spmd(ncB, prep_B_inputs(inputs, resA), core_ids=cores).results
    resB = [{k: np.asarray(v) for k, v in r.items()} for r in resB]
    ncC = _get("C", build_C)
    resC = run_bass_kernel_spmd(ncC, prep_C_inputs(inputs, resB), core_ids=cores).results
    resC = [{k: np.asarray(v) for k, v in r.items()} for r in resC]
    ncD = _get("D", build_D)
    resD = run_bass_kernel_spmd(ncD, prep_D_inputs(inputs, resB, resC), core_ids=cores).results
    out = np.empty((B, NCORES * LLOC, D), np.float32)
    for c in cores:
        yc = np.asarray(resD[c]["y_out"], np.float32)
        out[:, c * LLOC:(c + 1) * LLOC, :] = yc.reshape(D, B, LLOC).transpose(1, 2, 0)
    return out


# version 02
# speedup vs baseline: 1.0112x; 1.0112x over previous
import os
from contextlib import ExitStack
import numpy as np
import ml_dtypes
import concourse.bass as bass
import concourse.mybir as mybir
from concourse.bass_utils import run_bass_kernel_spmd

F32 = mybir.dt.float32
BF16 = mybir.dt.bfloat16
I32 = mybir.dt.int32
AF = mybir.ActivationFunctionType
AL = mybir.AluOpType
AX = mybir.AxisListType
NBF = ml_dtypes.bfloat16

NCORES = 8
B = 8
LLOC = 256
T = 2048
D = 1024
CH = 8
DFF = 4096
CF = 32
TTW = 512
NTT = T // TTW
EPS = 1e-5

_SIM = os.environ.get("BASS_SIM_FP64") == "1"
RB16 = 32768.5 if _SIM else 0.0
RS16 = 32768.0 if _SIM else 0.0
RB8 = 128.5 if _SIM else 0.0
RS8 = 128.0 if _SIM else 0.0


def colz(vec, n):
    return np.ascontiguousarray(np.asarray(vec, np.float32).reshape(n, 128).T)


def fm(x):
    return x[:, :].rearrange("(c p) t -> p c t", p=128)


def sb3(buf, nch, tw):
    return buf[:, :].rearrange("p (c t) -> p c t", t=tw)



def build_A(debug=False):
    nc = bass.Bass("TRN2", target_bir_lowering=False, detect_race_conditions=False)
    x_in = nc.dram_tensor("x_in", [D, T], F32, kind="ExternalInput")
    wx_in = nc.dram_tensor("wx_in", [D, 2 * D], F32, kind="ExternalInput")
    bxp_in = nc.dram_tensor("bxp_in", [128, 16], F32, kind="ExternalInput")
    A_in = nc.dram_tensor("A_in", [128, CH * 16], F32, kind="ExternalInput")
    gam_in = nc.dram_tensor("gam_in", [128, CH], F32, kind="ExternalInput")
    bet_in = nc.dram_tensor("bet_in", [128, CH], F32, kind="ExternalInput")
    dp_in = nc.dram_tensor("dp_in", [128, CH], F32, kind="ExternalInput")
    ones_in = nc.dram_tensor("ones_in", [128, 128], BF16, kind="ExternalInput")

    m0_out = nc.dram_tensor("m0_out", [D, T], BF16, kind="ExternalOutput")
    gate_out = nc.dram_tensor("gate_out", [D, T], BF16, kind="ExternalOutput")
    S_out = nc.dram_tensor("S_out", [128, CH * B], BF16, kind="ExternalOutput")
    if debug:
        xn_out = nc.dram_tensor("xn_out", [D, T], BF16, kind="ExternalOutput")
        h_out = nc.dram_tensor("h_out", [D, T], BF16, kind="ExternalOutput")

    with ExitStack() as _ctx:
        _e = _ctx.enter_context
        XF = _e(nc.sbuf_tensor("XF", [128, 2 * T], F32))
        XN = _e(nc.sbuf_tensor("XN", [128, CH * T], BF16))
        DG = _e(nc.sbuf_tensor("DG", [128, CH * T], BF16))
        BG = _e(nc.sbuf_tensor("BG", [128, CH * T], BF16))
        WXB = _e(nc.sbuf_tensor("WXB", [128, CH * 2 * D], BF16))
        WST = _e(nc.sbuf_tensor("WST", [128, 2 * D], F32))
        XB = _e(nc.sbuf_tensor("XB", [128, 2 * T], BF16))
        XSQ = _e(nc.sbuf_tensor("XSQ", [128, 2 * T], BF16))
        SC1 = _e(nc.sbuf_tensor("SC1", [128, T], F32))
        SC2 = _e(nc.sbuf_tensor("SC2", [128, T], F32))
        GA = _e(nc.sbuf_tensor("GA", [128, T], F32))
        BXP = _e(nc.sbuf_tensor("BXP", [128, 16], F32))
        ACOL = _e(nc.sbuf_tensor("ACOL", [128, CH * 16], F32))
        GAM = _e(nc.sbuf_tensor("GAM", [128, CH], F32))
        BET = _e(nc.sbuf_tensor("BET", [128, CH], F32))
        DP = _e(nc.sbuf_tensor("DP", [128, CH], F32))
        ONESB = _e(nc.sbuf_tensor("ONESB", [128, 128], BF16))
        EPSC = _e(nc.sbuf_tensor("EPSC", [128, 1], F32))
        ONEC = _e(nc.sbuf_tensor("ONEC", [128, 1], F32))
        SS = _e(nc.sbuf_tensor("SS", [128, CH * B], BF16))
        PA = _e(nc.psum_tensor("PA", [128, T], F32))
        PB = _e(nc.psum_tensor("PB", [128, T], F32))
        ldc = _e(nc.semaphore("ldc"))
        ldx = _e(nc.semaphore("ldx"))
        ldx2 = _e(nc.semaphore("ldx2"))
        ldw = _e(nc.semaphore("ldw"))
        wcast = _e(nc.semaphore("wcast"))
        xprep = _e(nc.semaphore("xprep"))
        stat = _e(nc.semaphore("stat"))
        vx = _e(nc.semaphore("vx"))
        mmd = _e(nc.semaphore("mmd"))
        spsc = _e(nc.semaphore("spsc"))
        pfree = _e(nc.semaphore("pfree"))
        vdone = _e(nc.semaphore("vdone"))
        gexp = _e(nc.semaphore("gexp"))
        gadd = _e(nc.semaphore("gadd"))
        sto = _e(nc.semaphore("sto"))

        with nc.Block() as b:
            @b.sync
            def _(s):
                s.dma_start(BXP[:, :], bxp_in[:, :]).then_inc(ldc, 16)
                s.dma_start(ACOL[:, :], A_in[:, :]).then_inc(ldc, 16)
                s.dma_start(GAM[:, :], gam_in[:, :]).then_inc(ldc, 16)
                s.dma_start(BET[:, :], bet_in[:, :]).then_inc(ldc, 16)
                s.dma_start(DP[:, :], dp_in[:, :]).then_inc(ldc, 16)
                s.dma_start(ONESB[:, :], ones_in[:, :]).then_inc(ldc, 16)
                for c in range(CH):
                    if c >= 2:
                        s.wait_ge(xprep, c - 1)
                    s.dma_start(XF[:, (c % 2) * T:(c % 2) * T + T],
                                x_in[c * 128:(c + 1) * 128, :]).then_inc(ldx, 16)

            @b.gpsimd
            def _(g):
                for k in range(CH):
                    g.dma_start(WXB[:, k * 2 * D:(k + 1) * 2 * D],
                                wx_in[k * 128:(k + 1) * 128, :]).then_inc(ldw, 16)

            @b.vector
            def _(v):
                v.memset(EPSC[:, :], EPS)
                v.memset(ONEC[:, :], 1.0)

            @b.scalar
            def _(sc):
                for c in range(CH):
                    sc.wait_ge(ldx, 16 * (c + 1))
                    if c >= 2:
                        sc.wait_ge(stat, c - 1)
                    sl = slice((c % 2) * T, (c % 2) * T + T)
                    sc.activation(XB[:, sl], XF[:, sl], AF.Identity, bias=0.0, scale=1.0)
                    sc.activation(XSQ[:, sl], XF[:, sl], AF.Square,
                                  bias=0.0, scale=1.0).then_inc(xprep, 1)

            @b.tensor
            def _(t):
                for c in range(CH):
                    t.wait_ge(xprep, c + 1)
                    sl0 = (c % 2) * T
                    for tt in range(NTT):
                        t.matmul(PA[:, tt * TTW:(tt + 1) * TTW], ONESB[:, :],
                                 XB[:, sl0 + tt * TTW: sl0 + (tt + 1) * TTW],
                                 start=(c == 0), stop=(c == CH - 1))
                        mi = t.matmul(PB[:, tt * TTW:(tt + 1) * TTW], ONESB[:, :],
                                      XSQ[:, sl0 + tt * TTW: sl0 + (tt + 1) * TTW],
                                      start=(c == 0), stop=(c == CH - 1))
                        if tt == NTT - 1:
                            mi.then_inc(stat, 1)

        with nc.Block() as b:
            @b.scalar
            def _(sc):
                sc.activation(SC1[:, :], PA[:, :], AF.Square, bias=0.0, scale=1.0)
        with nc.Block() as b:
            @b.vector
            def _(v):
                v.tensor_sub(SC1[:, :], PB[:, :], SC1[:, :])
        with nc.Block() as b:
            @b.scalar
            def _(sc):
                sc.activation(SC1[:, :], SC1[:, :], AF.Ln, bias=EPSC[:, 0:1], scale=1.0)
                sc.activation(PB[:, :], SC1[:, :], AF.Exp, bias=0.0, scale=-0.5)

        with nc.Block() as b:
            @b.sync
            def _(s):
                for c in range(CH):
                    if c >= 2:
                        s.wait_ge(vx, c - 1)
                    s.dma_start(XF[:, (c % 2) * T:(c % 2) * T + T],
                                x_in[c * 128:(c + 1) * 128, :]).then_inc(ldx2, 16)

            @b.vector
            def _(v):
                for c in range(CH):
                    v.wait_ge(ldx2, 16 * (c + 1))
                    sl = slice((c % 2) * T, (c % 2) * T + T)
                    v.tensor_sub(SC1[:, :], XF[:, sl], PA[:, :])
                    v.scalar_tensor_tensor(SC1[:, :], SC1[:, :], GAM[:, c:c + 1], PB[:, :],
                                           AL.mult, AL.mult)
                    v.tensor_scalar(XN[:, c * T:(c + 1) * T], SC1[:, :], BET[:, c:c + 1], None,
                                    AL.add).then_inc(vx, 1)

        with nc.Block() as b:
            @b.tensor
            def _(t):
                t.wait_ge(wcast, 0)
                for m in range(16):
                    ps = PA if (m % 2 == 0) else PB
                    if m == 0:
                        t.wait_ge(ldw, 16 * CH)
                    if m >= 2:
                        t.wait_ge(pfree, m - 1)
                    for tt in range(NTT):
                        for k in range(CH):
                            mi = t.matmul(ps[:, tt * TTW:(tt + 1) * TTW],
                                          WXB[:, k * 2 * D + m * 128: k * 2 * D + (m + 1) * 128],
                                          XN[:, k * T + tt * TTW: k * T + (tt + 1) * TTW],
                                          start=(k == 0), stop=(k == CH - 1))
                            if tt == NTT - 1 and k == CH - 1:
                                mi.then_inc(mmd, 1)

            @b.scalar
            def _(sc):
                for m in range(16):
                    ps = PA if (m % 2 == 0) else PB
                    sc.wait_ge(mmd, m + 1)
                    if m < 8:
                        if m >= 1:
                            sc.wait_ge(vdone, m)
                        sc.activation(SC1[:, :], ps[:, :], AF.Abs, bias=BXP[:, m:m + 1], scale=1.0)
                        sc.activation(SC1[:, :], SC1[:, :], AF.Exp, bias=0.0, scale=-1.0)
                        sc.activation(SC1[:, :], SC1[:, :], AF.Ln, bias=1.0,
                                      scale=1.0).then_inc(spsc, 1)
                    else:
                        c = m - 8
                        sc.activation(BG[:, c * T:(c + 1) * T], ps[:, :], AF.Identity,
                                      bias=BXP[:, m:m + 1], scale=1.0).then_inc(pfree, 1)

            @b.vector
            def _(v):
                for m in range(8):
                    ps = PA if (m % 2 == 0) else PB
                    v.wait_ge(spsc, m + 1)
                    v.tensor_scalar(SC2[:, :], ps[:, :], BXP[:, m:m + 1], 0.0, AL.add, AL.max)
                    ai = v.tensor_add(DG[:, m * T:(m + 1) * T], SC2[:, :], SC1[:, :])
                    ai.then_inc(pfree, 1)
                    v.drain().then_inc(vdone, 1)

        with nc.Block() as b:
            @b.scalar
            def _(sc):
                for c in range(CH):
                    for s in range(16):
                        i = c * 16 + s
                        if i >= 2:
                            sc.wait_ge(gadd, i - 1)
                        dstE = PA if (i % 2 == 0) else PB
                        sc.activation(dstE[:, :], DG[:, c * T:(c + 1) * T], AF.Exp,
                                      bias=0.0,
                                      scale=ACOL[:, c * 16 + s:c * 16 + s + 1]).then_inc(gexp, 1)

            @b.vector
            def _(v):
                for c in range(CH):
                    for s in range(16):
                        i = c * 16 + s
                        v.wait_ge(gexp, i + 1)
                        srcE = PA if (i % 2 == 0) else PB
                        if s == 0:
                            v.tensor_copy(GA[:, :], srcE[:, :]).then_inc(gadd, 1)
                        else:
                            v.tensor_add(GA[:, :], GA[:, :], srcE[:, :]).then_inc(gadd, 1)
                    v.tensor_copy(DG[:, c * T:(c + 1) * T], GA[:, :])

        with nc.Block() as b:
            @b.vector
            def _(v):
                for c in range(CH):
                    v.tensor_mul(BG[:, c * T:(c + 1) * T], XN[:, c * T:(c + 1) * T],
                                 BG[:, c * T:(c + 1) * T])
                for c in range(CH):
                    for b_ in range(B):
                        sl = slice(c * T + b_ * LLOC, c * T + (b_ + 1) * LLOC)
                        v.tensor_tensor_scan(BG[:, sl], ONEC[:, 0:1].broadcast_to([128, LLOC]),
                                             BG[:, sl], 0.0, AL.mult, AL.add)

        with nc.Block() as b:
            @b.vector
            def _(v):
                for c in range(CH):
                    v.tensor_copy(SS[:, c * B:(c + 1) * B],
                                  bass.AP(BG, c * T + LLOC - 1, [[CH * T, 128], [LLOC, B]]))
        if debug:
            with nc.Block() as b:
                @b.sync
                def _(s):
                    s.dma_start(fm(xn_out), sb3(XN, CH, T)).then_inc(sto, 16)
                    s.dma_start(fm(h_out), sb3(BG, CH, T)).then_inc(sto, 16)
                    s.wait_ge(sto, 32)
        with nc.Block() as b:
            @b.vector
            def _(v):
                for c in range(CH):
                    v.tensor_mul(SC1[:, :], BG[:, c * T:(c + 1) * T], DG[:, c * T:(c + 1) * T])
                    v.scalar_tensor_tensor(XN[:, c * T:(c + 1) * T], XN[:, c * T:(c + 1) * T],
                                           DP[:, c:c + 1], SC1[:, :], AL.mult, AL.add)

        with nc.Block() as b:
            @b.sync
            def _(s):
                base = 32 if debug else 0
                s.dma_start(fm(m0_out), sb3(XN, CH, T)).then_inc(sto, 16)
                s.dma_start(fm(gate_out), sb3(DG, CH, T)).then_inc(sto, 16)
                s.dma_start(S_out[:, :], SS[:, :]).then_inc(sto, 16)
                s.wait_ge(sto, base + 48)

    return nc



def build_B(debug=False):
    nc = bass.Bass("TRN2", target_bir_lowering=False, detect_race_conditions=False)
    x_in = nc.dram_tensor("x_in", [D, T], F32, kind="ExternalInput")
    m0_in = nc.dram_tensor("m0_in", [D, T], BF16, kind="ExternalInput")
    gate_in = nc.dram_tensor("gate_in", [D, T], BF16, kind="ExternalInput")
    S_in = nc.dram_tensor("S_in", [NCORES, 128, CH * B], BF16, kind="ExternalInput")
    msk_in = nc.dram_tensor("msk_in", [128, NCORES], F32, kind="ExternalInput")
    w0_in = nc.dram_tensor("w0_in", [D, D], F32, kind="ExternalInput")
    b0_in = nc.dram_tensor("b0_in", [128, CH], F32, kind="ExternalInput")
    ones_in = nc.dram_tensor("ones_in", [128, 128], BF16, kind="ExternalInput")

    x1_out = nc.dram_tensor("x1_out", [D, T], F32, kind="ExternalOutput")
    amax_out = nc.dram_tensor("amax_out", [128, 1], F32, kind="ExternalOutput")

    with ExitStack() as _ctx:
        _e = _ctx.enter_context
        M0 = _e(nc.sbuf_tensor("M0", [128, CH * T], BF16))
        GT = _e(nc.sbuf_tensor("GT", [128, CH * T], BF16))
        V = _e(nc.sbuf_tensor("V", [128, CH * T], F32))
        W0B = _e(nc.sbuf_tensor("W0B", [128, CH * D], BF16))
        XF = _e(nc.sbuf_tensor("XF", [128, 2 * T], F32))
        XB = _e(nc.sbuf_tensor("XB", [128, 2 * T], BF16))
        XSQ = _e(nc.sbuf_tensor("XSQ", [128, 2 * T], BF16))
        SC1 = _e(nc.sbuf_tensor("SC1", [128, T], F32))
        SC2 = _e(nc.sbuf_tensor("SC2", [128, T], F32))
        SALL = _e(nc.sbuf_tensor("SALL", [128, NCORES * CH * B], BF16))
        MSK = _e(nc.sbuf_tensor("MSK", [128, NCORES], F32))
        OFF = _e(nc.sbuf_tensor("OFF", [128, CH * B], F32))
        B0C = _e(nc.sbuf_tensor("B0C", [128, CH], F32))
        ONESB = _e(nc.sbuf_tensor("ONESB", [128, 128], BF16))
        EPSC = _e(nc.sbuf_tensor("EPSC", [128, 1], F32))
        AMX = _e(nc.sbuf_tensor("AMX", [128, CH], F32))
        AMX1 = _e(nc.sbuf_tensor("AMX1", [128, 1], F32))
        PA = _e(nc.psum_tensor("PA", [128, T], F32))
        PB = _e(nc.psum_tensor("PB", [128, T], F32))
        ldc = _e(nc.semaphore("ldc"))
        ldm = _e(nc.semaphore("ldm"))
        ldw = _e(nc.semaphore("ldw"))
        ldx = _e(nc.semaphore("ldx"))
        ldx2 = _e(nc.semaphore("ldx2"))
        mmd = _e(nc.semaphore("mmd"))
        scd = _e(nc.semaphore("scd"))
        pfree = _e(nc.semaphore("pfree"))
        vdone = _e(nc.semaphore("vdone"))
        xprep = _e(nc.semaphore("xprep"))
        stat = _e(nc.semaphore("stat"))
        vx = _e(nc.semaphore("vx"))
        sto = _e(nc.semaphore("sto"))

        with nc.Block() as b:
            @b.sync
            def _(s):
                s.dma_start(SALL[:, :], S_in[:, :, :].rearrange("c p n -> p c n")).then_inc(ldc, 16)
                s.dma_start(MSK[:, :], msk_in[:, :]).then_inc(ldc, 16)
                s.dma_start(B0C[:, :], b0_in[:, :]).then_inc(ldc, 16)
                s.dma_start(ONESB[:, :], ones_in[:, :]).then_inc(ldc, 16)
                s.dma_start(sb3(M0, CH, T), fm(m0_in)).then_inc(ldm, 16)
                s.dma_start(sb3(GT, CH, T), fm(gate_in)).then_inc(ldm, 16)

            @b.gpsimd
            def _(g):
                for k in range(CH):
                    g.dma_start(W0B[:, k * D:(k + 1) * D],
                                w0_in[k * 128:(k + 1) * 128, :]).then_inc(ldw, 16)

            @b.vector
            def _(v):
                v.memset(EPSC[:, :], EPS)
                v.memset(OFF[:, :], 0.0)
                v.memset(AMX[:, :], 0.0)
                v.memset(AMX1[:, :], 0.0)

        with nc.Block() as b:
            @b.vector
            def _(v):
                v.wait_ge(ldc, 32)
                for cp in range(NCORES):
                    v.scalar_tensor_tensor(OFF[:, :], SALL[:, cp * CH * B:(cp + 1) * CH * B],
                                           MSK[:, cp:cp + 1], OFF[:, :], AL.mult, AL.add)
                v.wait_ge(ldm, 32)
                for c in range(CH):
                    for b_ in range(B):
                        sl = slice(c * T + b_ * LLOC, c * T + (b_ + 1) * LLOC)
                        v.scalar_tensor_tensor(M0[:, sl], GT[:, sl],
                                               OFF[:, c * B + b_:c * B + b_ + 1],
                                               M0[:, sl], AL.mult, AL.add)

        with nc.Block() as b:
            @b.sync
            def _(s):
                for m in range(CH):
                    if m >= 2:
                        s.wait_ge(vdone, m - 1)
                    s.dma_start(XF[:, (m % 2) * T:(m % 2) * T + T],
                                x_in[m * 128:(m + 1) * 128, :]).then_inc(ldx, 16)

            @b.tensor
            def _(t):
                t.wait_ge(ldw, 16 * CH)
                for m in range(CH):
                    ps = PA if (m % 2 == 0) else PB
                    if m >= 2:
                        t.wait_ge(pfree, m - 1)
                    for tt in range(NTT):
                        for k in range(CH):
                            mi = t.matmul(ps[:, tt * TTW:(tt + 1) * TTW],
                                          W0B[:, k * D + m * 128: k * D + (m + 1) * 128],
                                          M0[:, k * T + tt * TTW: k * T + (tt + 1) * TTW],
                                          start=(k == 0), stop=(k == CH - 1))
                            if tt == NTT - 1 and k == CH - 1:
                                mi.then_inc(mmd, 1)

            @b.scalar
            def _(sc):
                sc.wait_ge(ldc, 48)
                for m in range(CH):
                    ps = PA if (m % 2 == 0) else PB
                    sc.wait_ge(mmd, m + 1)
                    if m >= 1:
                        sc.wait_ge(vdone, m)
                    sc.activation(SC1[:, :], ps[:, :], AF.Identity,
                                  bias=B0C[:, m:m + 1], scale=1.0).then_inc(scd, 1)

            @b.vector
            def _(v):
                for m in range(CH):
                    v.wait_ge(scd, m + 1)
                    v.wait_ge(ldx, 16 * (m + 1))
                    sl = slice((m % 2) * T, (m % 2) * T + T)
                    ai = v.scalar_tensor_tensor(V[:, m * T:(m + 1) * T], XF[:, sl], 2.0,
                                                SC1[:, :], AL.mult, AL.add)
                    ai.then_inc(pfree, 1)
                    v.drain().then_inc(vdone, 1)

        with nc.Block() as b:
            @b.scalar
            def _(sc):
                for c in range(CH):
                    if c >= 2:
                        sc.wait_ge(stat, c - 1)
                    sl = slice((c % 2) * T, (c % 2) * T + T)
                    sc.activation(XB[:, sl], V[:, c * T:(c + 1) * T], AF.Identity,
                                  bias=0.0, scale=1.0)
                    sc.activation(XSQ[:, sl], V[:, c * T:(c + 1) * T], AF.Square,
                                  bias=0.0, scale=1.0).then_inc(xprep, 1)

            @b.tensor
            def _(t):
                t.wait_ge(ldc, 64)
                for c in range(CH):
                    t.wait_ge(xprep, c + 1)
                    sl0 = (c % 2) * T
                    for tt in range(NTT):
                        t.matmul(PA[:, tt * TTW:(tt + 1) * TTW], ONESB[:, :],
                                 XB[:, sl0 + tt * TTW: sl0 + (tt + 1) * TTW],
                                 start=(c == 0), stop=(c == CH - 1))
                        mi = t.matmul(PB[:, tt * TTW:(tt + 1) * TTW], ONESB[:, :],
                                      XSQ[:, sl0 + tt * TTW: sl0 + (tt + 1) * TTW],
                                      start=(c == 0), stop=(c == CH - 1))
                        if tt == NTT - 1:
                            mi.then_inc(stat, 1)
        with nc.Block() as b:
            @b.scalar
            def _(sc):
                sc.activation(SC1[:, :], PA[:, :], AF.Square, bias=0.0, scale=1.0)
        with nc.Block() as b:
            @b.vector
            def _(v):
                v.tensor_sub(SC1[:, :], PB[:, :], SC1[:, :])
        with nc.Block() as b:
            @b.scalar
            def _(sc):
                sc.activation(SC1[:, :], SC1[:, :], AF.Ln, bias=EPSC[:, 0:1], scale=1.0)
                sc.activation(PB[:, :], SC1[:, :], AF.Exp, bias=0.0, scale=-0.5)
        with nc.Block() as b:
            @b.vector
            def _(v):
                for c in range(CH):
                    v.tensor_sub(V[:, c * T:(c + 1) * T], V[:, c * T:(c + 1) * T], PA[:, :])
                    v.tensor_mul(V[:, c * T:(c + 1) * T], V[:, c * T:(c + 1) * T], PB[:, :])
                    v.tensor_reduce(AMX[:, c:c + 1], V[:, c * T:(c + 1) * T], AX.X,
                                    AL.max, apply_absolute_value=True)
                v.tensor_reduce(AMX1[:, :], AMX[:, :], AX.X, AL.max,
                                apply_absolute_value=True)

        with nc.Block() as b:
            @b.sync
            def _(s):
                s.dma_start(fm(x1_out), sb3(V, CH, T)).then_inc(sto, 16)
                s.dma_start(amax_out[:, :], AMX1[:, :]).then_inc(sto, 16)
                s.wait_ge(sto, 32)

    return nc


def prep_B_inputs(inputs, resA):
    x = np.asarray(inputs["L0_attn_x"], np.float32)
    w0 = np.ascontiguousarray(np.asarray(inputs["L0_attn_w_out"], np.float32).T)
    Sall = np.stack([np.asarray(resA[c]["S_out"]) for c in range(NCORES)])
    consts = dict(
        S_in=Sall,
        b0_in=colz(inputs["L0_attn_b_out"], CH),
        w0_in=w0,
        ones_in=np.full((128, 128), 1.0 / 1024.0, dtype=NBF),
    )
    in_maps = []
    for c in range(NCORES):
        sl = x[:, c * LLOC:(c + 1) * LLOC, :]
        xc = np.ascontiguousarray(sl.transpose(2, 0, 1).reshape(D, T))
        msk = np.zeros((128, NCORES), np.float32)
        msk[:, :c] = 1.0
        in_maps.append({"x_in": xc, "m0_in": np.asarray(resA[c]["m0_out"]),
                        "gate_in": np.asarray(resA[c]["gate_out"]), "msk_in": msk, **consts})
    return in_maps



def emit_pfold_max(nc, buf, scr, width, sem, cnt, pre=None):
    first = [True]
    for step in [64, 32, 16, 8, 4, 2, 1]:
        with nc.Block() as b:
            def _mkg(step=step, fw=(pre if first[0] else None)):
                @b.gpsimd
                def _(g):
                    if fw is not None:
                        g.wait_ge(fw[0], fw[1])
                    g.dma_start(scr[0:step, 0:width], buf[step:2 * step, 0:width]).then_inc(sem, 16)
            _mkg()
            first[0] = False
        cnt[0] += 16
        with nc.Block() as b:
            def _mkv(step=step, tgt=cnt[0]):
                @b.vector
                def _(v):
                    v.wait_ge(sem, tgt)
                    v.tensor_max(buf[0:step, 0:width], buf[0:step, 0:width], scr[0:step, 0:width])
            _mkv()


def emit_pdouble(nc, bufs, width, sem, cnt):
    for step in [1, 2, 4, 8, 16, 32, 64]:
        with nc.Block() as b:
            def _mkg(step=step, base=cnt[0]):
                @b.gpsimd
                def _(g):
                    g.wait_ge(sem, base)
                    for buf in bufs:
                        g.dma_start(buf[step:2 * step, 0:width], buf[0:step, 0:width]).then_inc(sem, 16)
            _mkg()
        cnt[0] += 16 * len(bufs)
    with nc.Block() as b:
        def _mkw(tgt=cnt[0]):
            @b.vector
            def _(v):
                v.wait_ge(sem, tgt)
        _mkw()



def build_C(debug=False):
    nc = bass.Bass("TRN2", target_bir_lowering=False, detect_race_conditions=False)
    x1_in = nc.dram_tensor("x1_in", [D, T], F32, kind="ExternalInput")
    amax_in = nc.dram_tensor("amax_in", [128, NCORES], F32, kind="ExternalInput")
    win_in = nc.dram_tensor("win_in", [D, DFF], F32, kind="ExternalInput")
    bin_in = nc.dram_tensor("bin_in", [128, CF], F32, kind="ExternalInput")
    eye_in = nc.dram_tensor("eye_in", [128, 128], F32, kind="ExternalInput")

    hid_out = nc.dram_tensor("hid_out", [DFF, T], BF16, kind="ExternalOutput")
    amax2_out = nc.dram_tensor("amax2_out", [128, 1], F32, kind="ExternalOutput")
    if debug:
        xq_out = nc.dram_tensor("xq_out", [D, T], BF16, kind="ExternalOutput")
        se_out = nc.dram_tensor("se_out", [128, CF], F32, kind="ExternalOutput")
        sx_out = nc.dram_tensor("sx_out", [128 * 2, 1], F32, kind="ExternalOutput")

    with ExitStack() as _ctx:
        _e = _ctx.enter_context
        XQ = _e(nc.sbuf_tensor("XQ", [128, CH * T], BF16))
        WQ = _e(nc.sbuf_tensor("WQ", [128, CH * DFF], BF16))
        WST = _e(nc.sbuf_tensor("WST", [128, DFF], F32))
        MRUN = _e(nc.sbuf_tensor("MRUN", [128, DFF], F32))
        SCB = _e(nc.sbuf_tensor("SCB", [128, DFF], F32))
        SC_A = _e(nc.sbuf_tensor("SC_A", [128, DFF], F32))
        IC = _e(nc.sbuf_tensor("IC", [128, DFF], I32))
        XF = _e(nc.sbuf_tensor("XF", [128, 2 * T], F32))
        HST = _e(nc.sbuf_tensor("HST", [128, 2 * T], BF16))
        EY = _e(nc.sbuf_tensor("EY", [128, 128], F32))
        T32 = _e(nc.sbuf_tensor("T32", [32, 128], F32))
        SECB = _e(nc.sbuf_tensor("SECB", [128, CF], F32))
        BIN = _e(nc.sbuf_tensor("BIN", [128, CF], F32))
        AMXL = _e(nc.sbuf_tensor("AMXL", [128, 1], F32))
        SXC = _e(nc.sbuf_tensor("SXC", [128, 1], F32))
        KC = _e(nc.sbuf_tensor("KC", [128, 1], F32))
        AM2 = _e(nc.sbuf_tensor("AM2", [128, CF], F32))
        AM2F = _e(nc.sbuf_tensor("AM2F", [128, 1], F32))
        RBC = _e(nc.sbuf_tensor("RBC", [128, 1], F32))
        FSC = _e(nc.sbuf_tensor("FSC", [128, 16], F32))
        PA = _e(nc.psum_tensor("PA", [128, T], F32))
        PB = _e(nc.psum_tensor("PB", [128, T], F32))
        ldc = _e(nc.semaphore("ldc"))
        ldx = _e(nc.semaphore("ldx"))
        ldw = _e(nc.semaphore("ldw"))
        ldw2 = _e(nc.semaphore("ldw2"))
        fde = _e(nc.semaphore("fde"))
        vste = _e(nc.semaphore("vste"))
        scq = _e(nc.semaphore("scq"))
        vq = _e(nc.semaphore("vq"))
        wqd = _e(nc.semaphore("wqd"))
        mmd = _e(nc.semaphore("mmd"))
        gel = _e(nc.semaphore("gel"))
        amd = _e(nc.semaphore("amd"))
        hsto = _e(nc.semaphore("hsto"))
        sto = _e(nc.semaphore("sto"))

        fc = [0]

        with nc.Block() as b:
            @b.sync
            def _(s):
                s.dma_start(FSC[:, 0:NCORES], amax_in[:, :]).then_inc(ldc, 16)
                s.dma_start(BIN[:, :], bin_in[:, :]).then_inc(ldc, 16)
                s.dma_start(EY[:, :], eye_in[:, :]).then_inc(ldc, 16)

            @b.vector
            def _(v):
                v.memset(RBC[:, :], RB16)
                v.memset(AM2[:, :], 0.0)
                v.memset(AM2F[:, :], 0.0)
                v.wait_ge(ldc, 48)
                v.tensor_reduce(AMXL[:, :], FSC[:, 0:NCORES], AX.X, AL.max,
                                apply_absolute_value=True)
        emit_pfold_max(nc, AMXL, FSC, 1, fde, fc)
        emit_pdouble(nc, [AMXL], 1, fde, fc)
        with nc.Block() as b:
            @b.vector
            def _(v):
                v.tensor_scalar(SC_A[:, 0:512], AMXL[:, 0:1].broadcast_to([128, 512]),
                                1e-8, None, AL.max)
                v.reciprocal(SC_A[:, 512:1024], SC_A[:, 0:512])
                v.tensor_scalar(SXC[:, 0:1], SC_A[:, 512:513], 32767.0, None, AL.mult)
                v.tensor_scalar(KC[:, 0:1], SC_A[:, 0:1], 1.0 / (127.0 * 32767.0),
                                None, AL.mult)

        with nc.Block() as b:
            @b.sync
            def _(s):
                for c in range(CH):
                    if c >= 2:
                        s.wait_ge(vq, c - 1)
                    s.dma_start(XF[:, (c % 2) * T:(c % 2) * T + T],
                                x1_in[c * 128:(c + 1) * 128, :]).then_inc(ldx, 16)

            @b.scalar
            def _(sc):
                for c in range(CH):
                    sc.wait_ge(ldx, 16 * (c + 1))
                    if c >= 1:
                        sc.wait_ge(vq, c)
                    sl = slice((c % 2) * T, (c % 2) * T + T)
                    sc.activation(SC_A[:, 0:T], XF[:, sl], AF.Identity,
                                  bias=RBC[:, 0:1], scale=SXC[:, 0:1]).then_inc(scq, 1)

            @b.vector
            def _(v):
                for c in range(CH):
                    v.wait_ge(scq, c + 1)
                    v.tensor_copy(IC[:, 0:T], SC_A[:, 0:T])
                    v.tensor_scalar(SC_A[:, 0:T], IC[:, 0:T], RS16, 32767.0,
                                    AL.subtract, AL.min)
                    v.tensor_scalar(XQ[:, c * T:(c + 1) * T], SC_A[:, 0:T], -32768.0, None,
                                    AL.max).then_inc(vq, 1)

        with nc.Block() as b:
            @b.sync
            def _(s):
                for k in range(CH):
                    if k >= 1:
                        s.wait_ge(scq, CH + k)
                    s.dma_start(WST[:, :], win_in[k * 128:(k + 1) * 128, :]).then_inc(ldw, 16)

            @b.scalar
            def _(sc):
                for k in range(CH):
                    sc.wait_ge(ldw, 16 * (k + 1))
                    if k >= 1:
                        sc.wait_ge(vste, k)
                    sc.activation(SC_A[:, :], WST[:, :], AF.Abs,
                                  bias=0.0, scale=1.0).then_inc(scq, 1)

            @b.vector
            def _(v):
                for k in range(CH):
                    v.wait_ge(scq, CH + k + 1)
                    if k == 0:
                        v.tensor_copy(MRUN[:, :], SC_A[:, :]).then_inc(vste, 1)
                    else:
                        v.tensor_max(MRUN[:, :], MRUN[:, :], SC_A[:, :]).then_inc(vste, 1)

        emit_pfold_max(nc, MRUN, SC_A, DFF, fde, fc)
        with nc.Block() as b:
            @b.vector
            def _(v):
                v.tensor_scalar(MRUN[0:1, :], MRUN[0:1, :], 1e-8, None, AL.max)
                v.reciprocal(SCB[0:1, :], MRUN[0:1, :])
                v.tensor_scalar(SCB[0:1, :], SCB[0:1, :], 127.0, None, AL.mult)
        emit_pdouble(nc, [SCB], DFF, fde, fc)
        with nc.Block() as b:
            @b.gpsimd
            def _(g):
                g.dma_start(T32[:, :], bass.AP(MRUN, 0, [[DFF, 1], [128, 32], [1, 128]])).then_inc(fde, 16)
        fc[0] += 16
        with nc.Block() as b:
            def _mk(tgt=fc[0]):
                @b.tensor
                def _(t):
                    t.wait_ge(fde, tgt)
                    t.transpose(PB[:, 0:32], T32[:, :], EY[0:32, 0:32])
            _mk()
        with nc.Block() as b:
            @b.vector
            def _(v):
                v.tensor_scalar(SECB[:, :], PB[:, 0:32], KC[:, 0:1], None, AL.mult)

        with nc.Block() as b:
            @b.sync
            def _(s):
                for k in range(CH):
                    if k >= 1:
                        s.wait_ge(wqd, k)
                    s.dma_start(WST[:, :], win_in[k * 128:(k + 1) * 128, :]).then_inc(ldw2, 16)

            @b.vector
            def _(v):
                for k in range(CH):
                    v.wait_ge(ldw2, 16 * (k + 1))
                    v.tensor_mul(SC_A[:, :], WST[:, :], SCB[:, :])
                    v.tensor_scalar(SC_A[:, :], SC_A[:, :], RB8, None, AL.add)
                    v.tensor_copy(IC[:, :], SC_A[:, :])
                    v.tensor_scalar(SC_A[:, :], IC[:, :], RS8, 127.0, AL.subtract, AL.min)
                    v.tensor_scalar(WQ[:, k * DFF:(k + 1) * DFF], SC_A[:, :], -128.0, None,
                                    AL.max).then_inc(wqd, 1)

        with nc.Block() as b:
            @b.tensor
            def _(t):
                t.wait_ge(wqd, CH)
                for m in range(CF):
                    ps = PA if (m % 2 == 0) else PB
                    if m >= 2:
                        t.wait_ge(gel, m - 1)
                    for tt in range(NTT):
                        for k in range(CH):
                            mi = t.matmul(ps[:, tt * TTW:(tt + 1) * TTW],
                                          WQ[:, k * DFF + m * 128: k * DFF + (m + 1) * 128],
                                          XQ[:, k * T + tt * TTW: k * T + (tt + 1) * TTW],
                                          start=(k == 0), stop=(k == CH - 1))
                            if tt == NTT - 1 and k == CH - 1:
                                mi.then_inc(mmd, 1)

            @b.scalar
            def _(sc):
                for m in range(CF):
                    ps = PA if (m % 2 == 0) else PB
                    sc.wait_ge(mmd, m + 1)
                    if m >= 2:
                        sc.wait_ge(amd, m - 1)
                        sc.wait_ge(hsto, 16 * (m - 1))
                    sl = slice((m % 2) * T, (m % 2) * T + T)
                    sc.activation(HST[:, sl], ps[:, :], AF.Gelu,
                                  bias=BIN[:, m:m + 1], scale=SECB[:, m:m + 1]).then_inc(gel, 1)

            @b.vector
            def _(v):
                for m in range(CF):
                    v.wait_ge(gel, m + 1)
                    sl = slice((m % 2) * T, (m % 2) * T + T)
                    v.tensor_reduce(AM2[:, m:m + 1], HST[:, sl], AX.X, AL.max,
                                    apply_absolute_value=True).then_inc(amd, 1)

            @b.sync
            def _(s):
                for m in range(CF):
                    s.wait_ge(gel, m + 1)
                    sl = slice((m % 2) * T, (m % 2) * T + T)
                    s.dma_start(hid_out[m * 128:(m + 1) * 128, :], HST[:, sl]).then_inc(hsto, 16)

        with nc.Block() as b:
            @b.vector
            def _(v):
                v.tensor_reduce(AM2F[:, :], AM2[:, :], AX.X, AL.max,
                                apply_absolute_value=True)
        with nc.Block() as b:
            @b.sync
            def _(s):
                s.wait_ge(hsto, 16 * CF)
                s.dma_start(amax2_out[:, :], AM2F[:, :]).then_inc(sto, 16)
                n = 1
                if debug:
                    s.dma_start(fm(xq_out), sb3(XQ, CH, T)).then_inc(sto, 16)
                    s.dma_start(se_out[:, :], SECB[:, :]).then_inc(sto, 16)
                    s.dma_start(sx_out[0:128, :], SXC[:, :]).then_inc(sto, 16)
                    s.dma_start(sx_out[128:256, :], KC[:, :]).then_inc(sto, 16)
                    n = 5
                s.wait_ge(sto, 16 * n)

    return nc


def prep_C_inputs(inputs, resB):
    amax_all = np.concatenate([np.asarray(resB[c]["amax_out"]) for c in range(NCORES)], axis=1)
    consts = dict(
        amax_in=np.ascontiguousarray(amax_all),
        win_in=np.ascontiguousarray(np.asarray(inputs["L0_ffn_W_in"], np.float32)),
        bin_in=colz(inputs["L0_ffn_b_in"], CF),
        eye_in=np.eye(128, dtype=np.float32),
    )
    in_maps = []
    for c in range(NCORES):
        in_maps.append({"x1_in": np.asarray(resB[c]["x1_out"]), **consts})
    return in_maps



def emit_postln(nc, env, STR):
    XB, XSQ, SC1 = env["XB"], env["XSQ"], env["SC1"]
    ONESB, EPSC = env["ONESB"], env["EPSC"]
    xprep, stat = env["xprep"], env["stat"]
    cnt = env["cnt"]
    xp0 = cnt.get("xprep", 0)
    st0 = cnt.get("stat", 0)
    pid = cnt["pln_id"] = cnt.get("pln_id", 0) + 1
    _ps = nc.psum_tensor(f"pln{pid}", [128, 4096], F32)
    PS = _ps.__enter__()
    PA, PB = PS[:, 0:2048], PS[:, 2048:4096]
    with nc.Block() as b:
        @b.scalar
        def _(sc):
            for c in range(CH):
                if c >= 2:
                    sc.wait_ge(stat, st0 + c - 1)
                sl = slice((c % 2) * T, (c % 2) * T + T)
                sc.activation(XB[:, sl], STR[:, c * T:(c + 1) * T], AF.Identity,
                              bias=0.0, scale=1.0)
                sc.activation(XSQ[:, sl], STR[:, c * T:(c + 1) * T], AF.Square,
                              bias=0.0, scale=1.0).then_inc(xprep, 1)

        @b.tensor
        def _(t):
            for c in range(CH):
                t.wait_ge(xprep, xp0 + c + 1)
                sl0 = (c % 2) * T
                for tt in range(NTT):
                    t.matmul(PA[:, tt * TTW:(tt + 1) * TTW], ONESB[:, :],
                             XB[:, sl0 + tt * TTW: sl0 + (tt + 1) * TTW],
                             start=(c == 0), stop=(c == CH - 1))
                    mi = t.matmul(PB[:, tt * TTW:(tt + 1) * TTW], ONESB[:, :],
                                  XSQ[:, sl0 + tt * TTW: sl0 + (tt + 1) * TTW],
                                  start=(c == 0), stop=(c == CH - 1))
                    if tt == NTT - 1:
                        mi.then_inc(stat, 1)
    cnt["xprep"] = xp0 + CH
    cnt["stat"] = st0 + CH
    with nc.Block() as b:
        @b.scalar
        def _(sc):
            sc.activation(SC1[:, :], PA[:, :], AF.Square, bias=0.0, scale=1.0)
    with nc.Block() as b:
        @b.vector
        def _(v):
            v.tensor_sub(SC1[:, :], PB[:, :], SC1[:, :])
    with nc.Block() as b:
        @b.scalar
        def _(sc):
            sc.activation(SC1[:, :], SC1[:, :], AF.Ln, bias=EPSC[:, 0:1], scale=1.0)
            sc.activation(PB[:, :], SC1[:, :], AF.Exp, bias=0.0, scale=-0.5)
    with nc.Block() as b:
        @b.vector
        def _(v):
            for c in range(CH):
                v.tensor_sub(STR[:, c * T:(c + 1) * T], STR[:, c * T:(c + 1) * T], PA[:, :])
                v.tensor_mul(STR[:, c * T:(c + 1) * T], STR[:, c * T:(c + 1) * T], PB[:, :])
    _ps.__exit__(None, None, None)


def emit_recip_cols(nc, env, dst, src, mult):
    W = env["WIDE"]
    with nc.Block() as b:
        @b.vector
        def _(v):
            v.tensor_scalar(W[:, 0:512], src[:, 0:1].broadcast_to([128, 512]),
                            1e-30, None, AL.max)
            v.reciprocal(W[:, 512:1024], W[:, 0:512])
            v.tensor_scalar(dst[:, 0:1], W[:, 512:513], mult, None, AL.mult)



def _store_y_dbg(nc, y_out, STR, tag):
    with nc.semaphore(f"sty{tag}") as s5:
        with nc.Block() as b:
            @b.sync
            def _(s):
                s.dma_start(fm(y_out), sb3(STR, CH, T)).then_inc(s5, 16)
                s.wait_ge(s5, 16)


def build_D(debug=False, stop=None, d5stop=0, d6stop=0):
    nc = bass.Bass("TRN2", target_bir_lowering=False, detect_race_conditions=False)
    x1_in = nc.dram_tensor("x1_in", [D, T], F32, kind="ExternalInput")
    hid_in = nc.dram_tensor("hid_in", [DFF, T], BF16, kind="ExternalInput")
    amax2_in = nc.dram_tensor("amax2_in", [128, NCORES], F32, kind="ExternalInput")
    wout_in = nc.dram_tensor("wout_in", [DFF, D], F32, kind="ExternalInput")
    bout_in = nc.dram_tensor("bout_in", [128, CH], F32, kind="ExternalInput")
    wq1_in = nc.dram_tensor("wq1_in", [D, 512], F32, kind="ExternalInput")
    bq_in = nc.dram_tensor("bq_in", [128, 4], F32, kind="ExternalInput")
    wp_in = nc.dram_tensor("wp_in", [1, 40], F32, kind="ExternalInput")
    w1o_in = nc.dram_tensor("w1o_in", [D, D], F32, kind="ExternalInput")
    b1_in = nc.dram_tensor("b1_in", [128, CH], F32, kind="ExternalInput")
    wfc_in = nc.dram_tensor("wfc_in", [D, DFF], F32, kind="ExternalInput")
    bfc_in = nc.dram_tensor("bfc_in", [128, CF], F32, kind="ExternalInput")
    afc_in = nc.dram_tensor("afc_in", [D, 16], F32, kind="ExternalInput")
    bfcB_in = nc.dram_tensor("bfcB_in", [16, DFF], F32, kind="ExternalInput")
    wpj_in = nc.dram_tensor("wpj_in", [DFF, D], F32, kind="ExternalInput")
    bpj_in = nc.dram_tensor("bpj_in", [128, CH], F32, kind="ExternalInput")
    apj_in = nc.dram_tensor("apj_in", [DFF, 16], F32, kind="ExternalInput")
    bpjB_in = nc.dram_tensor("bpjB_in", [16, D], F32, kind="ExternalInput")
    ones_in = nc.dram_tensor("ones_in", [128, 128], BF16, kind="ExternalInput")
    eye_in = nc.dram_tensor("eye_in", [128, 128], F32, kind="ExternalInput")
    onesr_in = nc.dram_tensor("onesr_in", [1, 128], BF16, kind="ExternalInput")

    wqo_scr = nc.dram_tensor("wqo_scr", [DFF, D], BF16, kind="Internal")
    hq_scr = nc.dram_tensor("hq_scr", [DFF, T], BF16, kind="Internal")
    hid2_scr = nc.dram_tensor("hid2_scr", [DFF, T], BF16, kind="Internal")
    wfcq_scr = nc.dram_tensor("wfcq_scr", [D, DFF], BF16, kind="Internal")
    wpjq_scr = nc.dram_tensor("wpjq_scr", [DFF, D], BF16, kind="Internal")

    y_out = nc.dram_tensor("y_out", [D, T], F32, kind="ExternalOutput")
    if debug:
        x2_out = nc.dram_tensor("x2_out", [D, T], F32, kind="ExternalOutput")
        x3_out = nc.dram_tensor("x3_out", [D, T], F32, kind="ExternalOutput")
        ofm_out = nc.dram_tensor("ofm_out", [D, T], BF16, kind="ExternalOutput")
        hq_out = nc.dram_tensor("hq_out", [DFF, T], BF16, kind="ExternalOutput")
        wq_out = nc.dram_tensor("wq_out", [DFF, D], BF16, kind="ExternalOutput")
        if stop == "x2":
            pst_out = nc.dram_tensor("pst_out", [CH * 128, TTW], F32, kind="ExternalOutput")
            hqt_out = nc.dram_tensor("hqt_out", [CF * 128, TTW], BF16, kind="ExternalOutput")
            shkh_out = nc.dram_tensor("shkh_out", [256, 1], F32, kind="ExternalOutput")
            gcol_out = nc.dram_tensor("gcol_out", [128, CF], F32, kind="ExternalOutput")
            sra_out = nc.dram_tensor("sra_out", [128, CF], F32, kind="ExternalOutput")
            ama_out = nc.dram_tensor("ama_out", [128, CF], F32, kind="ExternalOutput")

    with ExitStack() as _ctx:
        _e = _ctx.enter_context
        STR = _e(nc.sbuf_tensor("STR", [128, CH * T], F32))
        XB = _e(nc.sbuf_tensor("XB", [128, 2 * T], BF16))
        XSQ = _e(nc.sbuf_tensor("XSQ", [128, 2 * T], BF16))
        SC1 = _e(nc.sbuf_tensor("SC1", [128, T], F32))
        WIDE = _e(nc.sbuf_tensor("WIDE", [128, 1024], F32))
        ONESB = _e(nc.sbuf_tensor("ONESB", [128, 128], BF16))
        EYF = _e(nc.sbuf_tensor("EYF", [128, 128], F32))
        ONER = _e(nc.sbuf_tensor("ONER", [1, 128], BF16))
        EPSC = _e(nc.sbuf_tensor("EPSC", [128, 1], F32))
        RBC = _e(nc.sbuf_tensor("RBC", [128, 1], F32))
        RBC8 = _e(nc.sbuf_tensor("RBC8", [128, 1], F32))
        SHC = _e(nc.sbuf_tensor("SHC", [128, 1], F32))
        KH = _e(nc.sbuf_tensor("KH", [128, 1], F32))
        AMXL = _e(nc.sbuf_tensor("AMXL", [128, 1], F32))
        FSC = _e(nc.sbuf_tensor("FSC", [128, 16], F32))
        GCOL = _e(nc.sbuf_tensor("GCOL", [128, CF], F32))
        BOC = _e(nc.sbuf_tensor("BOC", [128, CH], F32))
        B1C = _e(nc.sbuf_tensor("B1C", [128, CH], F32))
        BQC = _e(nc.sbuf_tensor("BQC", [128, 4], F32))
        BFCC = _e(nc.sbuf_tensor("BFCC", [128, CF], F32))
        BPJC = _e(nc.sbuf_tensor("BPJC", [128, CH], F32))
        WPC = _e(nc.sbuf_tensor("WPC", [128, 40], F32))
        ldc = _e(nc.semaphore("ldc"))
        fde = _e(nc.semaphore("fde"))
        d0p = _e(nc.semaphore("d0p"))
        sto = _e(nc.semaphore("sto"))
        fc = [0]
        env = {"XB": XB, "XSQ": XSQ, "SC1": SC1, "ONESB": ONESB, "EPSC": EPSC,
               "xprep": _e(nc.semaphore("xprep")),
               "stat": _e(nc.semaphore("stat")), "cnt": {}, "WIDE": WIDE}

        with nc.Block() as b:
            @b.sync
            def _(s):
                for ap_, src_ in [(FSC[:, 0:NCORES], amax2_in[:, :]), (BOC[:, :], bout_in[:, :]),
                                  (B1C[:, :], b1_in[:, :]), (BQC[:, :], bq_in[:, :]),
                                  (BFCC[:, :], bfc_in[:, :]), (BPJC[:, :], bpj_in[:, :]),
                                  (ONESB[:, :], ones_in[:, :]), (EYF[:, :], eye_in[:, :]),
                                  (ONER[:, :], onesr_in[:, :])]:
                    s.dma_start(ap_, src_).then_inc(ldc, 16)

            @b.vector
            def _(v):
                v.memset(EPSC[:, :], EPS)
                v.memset(RBC[:, :], RB16)
                v.memset(RBC8[:, :], RB8).then_inc(d0p, 1)
                v.wait_ge(ldc, 16 * 9)
                v.tensor_reduce(AMXL[:, :], FSC[:, 0:NCORES], AX.X, AL.max,
                                apply_absolute_value=True).then_inc(d0p, 1)
        emit_pfold_max(nc, AMXL, FSC, 1, fde, fc, pre=(d0p, 2))
        emit_pdouble(nc, [AMXL], 1, fde, fc)
        with nc.Block() as b:
            @b.vector
            def _(v):
                v.tensor_scalar(WIDE[:, 0:512], AMXL[:, 0:1].broadcast_to([128, 512]),
                                1e-8, None, AL.max)
                v.reciprocal(WIDE[:, 512:1024], WIDE[:, 0:512])
                v.tensor_scalar(SHC[:, 0:1], WIDE[:, 512:513], 32767.0, None,
                                AL.mult).then_inc(fde, 1)
                v.tensor_scalar(KH[:, 0:1], WIDE[:, 0:1], 1.0 / (127.0 * 32767.0),
                                None, AL.mult).then_inc(fde, 1)

        d0_done = fc[0] + 2

        if debug and stop == "x2":
            with nc.semaphore("dbg0") as dbg0:
                with nc.Block() as b:
                    @b.sync
                    def _(s):
                        s.dma_start(shkh_out[0:128, :], SHC[:, 0:1]).then_inc(dbg0, 16)
                        s.dma_start(shkh_out[128:256, :], KH[:, 0:1]).then_inc(dbg0, 16)
                        s.wait_ge(dbg0, 32)

        with ExitStack() as _c1:
            WST1 = _c1.enter_context(nc.sbuf_tensor("WST1", [128, 2 * D], F32))
            SCA1 = _c1.enter_context(nc.sbuf_tensor("SCA1", [128, D], F32))
            IC1 = _c1.enter_context(nc.sbuf_tensor("IC1", [128, D], I32))
            WQ1 = _c1.enter_context(nc.sbuf_tensor("WQ1", [128, 2 * D], BF16))
            AMA = _c1.enter_context(nc.sbuf_tensor("AMA", [128, CF], F32))
            SRA = _c1.enter_context(nc.sbuf_tensor("SRA", [128, CF], F32))
            d1a = [_c1.enter_context(nc.semaphore(f"d1a{i_}")) for i_ in range(2)]
            d1r = _c1.enter_context(nc.semaphore("d1r"))
            d1f = [_c1.enter_context(nc.semaphore(f"d1f{i_}")) for i_ in range(2)]
            d1c = _c1.enter_context(nc.semaphore("d1c"))
            d1d = _c1.enter_context(nc.semaphore("d1d"))
            d1e = [_c1.enter_context(nc.semaphore(f"d1e{i_}")) for i_ in range(2)]
            d1w = _c1.enter_context(nc.semaphore("d1w"))
            with nc.Block() as b:
                @b.sync
                def _(s):
                    for k in range(CF):
                        if k >= 2:
                            s.wait_ge(d1r, k - 1)
                        s.dma_start(WST1[:, (k % 2) * D:(k % 2) * D + D],
                                    wout_in[k * 128:(k + 1) * 128, :]).then_inc(d1a[k % 2], 16)

                @b.vector
                def _(v):
                    for k in range(CF):
                        v.wait_ge(d1a[k % 2], 16 * (k // 2 + 1))
                        sl = slice((k % 2) * D, (k % 2) * D + D)
                        v.tensor_reduce(AMA[:, k:k + 1], WST1[:, sl], AX.X, AL.max,
                                        apply_absolute_value=True)
                        v.drain().then_inc(d1r, 1)
                    v.tensor_scalar(AMA[:, :], AMA[:, :], 1e-8, None, AL.max)
                    v.drain()
                    v.reciprocal(SRA[:, :], AMA[:, :])
                    v.drain()
                    v.tensor_scalar(SRA[:, :], SRA[:, :], 127.0, None, AL.mult)
                    v.drain().then_inc(d1w, 1)
                    v.tensor_scalar(GCOL[:, :], AMA[:, :], KH[:, 0:1], None, AL.mult)
                    v.drain().then_inc(d1w, 1)
            with nc.Block() as b:
                @b.sync
                def _(s):
                    for k in range(CF):
                        if k >= 2:
                            s.wait_ge(d1c, k - 1)
                        s.dma_start(WST1[:, (k % 2) * D:(k % 2) * D + D],
                                    wout_in[k * 128:(k + 1) * 128, :]).then_inc(d1f[k % 2], 16)

                @b.scalar
                def _(sc):
                    sc.wait_ge(d1w, 2)
                    for k in range(CF):
                        sc.wait_ge(d1f[k % 2], 16 * (k // 2 + 1))
                        if k >= 1:
                            sc.wait_ge(d1d, k)
                        sl = slice((k % 2) * D, (k % 2) * D + D)
                        sc.activation(SCA1[:, :], WST1[:, sl], AF.Identity,
                                      bias=RBC8[:, 0:1], scale=SRA[:, k:k + 1]).then_inc(d1c, 1)

                @b.vector
                def _(v):
                    for k in range(CF):
                        v.wait_ge(d1c, k + 1)
                        if k >= 2:
                            v.wait_ge(d1e[k % 2], 16 * (k // 2))
                        sl = slice((k % 2) * D, (k % 2) * D + D)
                        v.tensor_copy(IC1[:, :], SCA1[:, :])
                        v.tensor_scalar(SCA1[:, :], IC1[:, :], RS8, 127.0,
                                        AL.subtract, AL.min)
                        v.tensor_scalar(WQ1[:, sl], SCA1[:, :], -128.0, None,
                                        AL.max).then_inc(d1d, 1)

                @b.gpsimd
                def _(g):
                    for k in range(CF):
                        g.wait_ge(d1d, k + 1)
                        sl = slice((k % 2) * D, (k % 2) * D + D)
                        g.dma_start(wqo_scr[k * 128:(k + 1) * 128, :],
                                    WQ1[:, sl]).then_inc(d1e[k % 2], 16)
            with nc.Block() as b:
                @b.vector
                def _(v):
                    v.wait_ge(d1e[0], 16 * (CF // 2))
                    v.wait_ge(d1e[1], 16 * (CF // 2))
        if debug and stop == "x2":
            with nc.semaphore("dbg1") as dbg1:
                with nc.Block() as b:
                    @b.sync
                    def _(s):
                        s.dma_start(gcol_out[:, :], GCOL[:, :]).then_inc(dbg1, 16)
                        s.dma_start(sra_out[:, :], SRA[:, :]).then_inc(dbg1, 16)
                        s.dma_start(ama_out[:, :], AMA[:, :]).then_inc(dbg1, 16)
                        s.wait_ge(dbg1, 48)

        with ExitStack() as _c2:
            HSTI = _c2.enter_context(nc.sbuf_tensor("HSTI", [128, 2 * T], BF16))
            SCA2 = _c2.enter_context(nc.sbuf_tensor("SCA2", [128, T], F32))
            IC2 = _c2.enter_context(nc.sbuf_tensor("IC2", [128, T], I32))
            HQS = _c2.enter_context(nc.sbuf_tensor("HQS", [128, 2 * T], BF16))
            d2a = [_c2.enter_context(nc.semaphore(f"d2a{i_}")) for i_ in range(2)]
            d2b = _c2.enter_context(nc.semaphore("d2b"))
            d2c = _c2.enter_context(nc.semaphore("d2c"))
            d2d = _c2.enter_context(nc.semaphore("d2d"))
            with nc.Block() as b:
                @b.sync
                def _(s):
                    for k in range(CF):
                        if k >= 2:
                            s.wait_ge(d2b, k - 1)
                        s.dma_start(HSTI[:, (k % 2) * T:(k % 2) * T + T],
                                    hid_in[k * 128:(k + 1) * 128, :]).then_inc(d2a[k % 2], 16)

                @b.scalar
                def _(sc):
                    sc.wait_ge(fde, d0_done)
                    for k in range(CF):
                        sc.wait_ge(d2a[k % 2], 16 * (k // 2 + 1))
                        if k >= 1:
                            sc.wait_ge(d2c, k)
                        sl = slice((k % 2) * T, (k % 2) * T + T)
                        sc.activation(SCA2[:, :], HSTI[:, sl], AF.Identity,
                                      bias=RBC[:, 0:1], scale=SHC[:, 0:1]).then_inc(d2b, 1)

                @b.vector
                def _(v):
                    for k in range(CF):
                        v.wait_ge(d2b, k + 1)
                        if k >= 2:
                            v.wait_ge(d2d, 16 * (k - 1))
                        sl = slice((k % 2) * T, (k % 2) * T + T)
                        v.tensor_copy(IC2[:, :], SCA2[:, :])
                        v.tensor_scalar(SCA2[:, :], IC2[:, :], RS16, 32767.0,
                                        AL.subtract, AL.min)
                        v.tensor_scalar(HQS[:, sl], SCA2[:, :], -32768.0, GCOL[:, k:k + 1],
                                        AL.max, AL.mult).then_inc(d2c, 1)

                @b.gpsimd
                def _(g):
                    for k in range(CF):
                        g.wait_ge(d2c, k + 1)
                        sl = slice((k % 2) * T, (k % 2) * T + T)
                        g.dma_start(hq_scr[k * 128:(k + 1) * 128, :],
                                    HQS[:, sl]).then_inc(d2d, 16)
            with nc.Block() as b:
                @b.vector
                def _(v):
                    v.wait_ge(d2d, 16 * CF)

        with ExitStack() as _c3:
            PSALL = _c3.enter_context(nc.psum_tensor("PS3", [128, 4096], F32))
            HQT = _c3.enter_context(nc.sbuf_tensor("HQT", [128, CF * TTW], BF16))
            WQD = _c3.enter_context(nc.sbuf_tensor("WQD", [128, 2 * D], BF16))
            X1T = _c3.enter_context(nc.sbuf_tensor("X1T", [128, 2 * TTW], F32))
            d3w = [_c3.enter_context(nc.semaphore(f"d3w{i_}")) for i_ in range(2)]
            d3xs = [_c3.enter_context(nc.semaphore(f"d3xs{i_}")) for i_ in range(2)]
            d3h = _c3.enter_context(nc.semaphore("d3h"))
            d3k = _c3.enter_context(nc.semaphore("d3k"))
            d3s = _c3.enter_context(nc.semaphore("d3s"))
            d3v = _c3.enter_context(nc.semaphore("d3v"))
            d3x = _c3.enter_context(nc.semaphore("d3x"))
            dbgD = debug and stop == "x2"
            if dbgD:
                dbg3 = _c3.enter_context(nc.semaphore("dbg3"))
            with nc.Block() as b:
                @b.sync
                def _(s):
                    for tt in range(NTT):
                        for k in range(CF):
                            i = tt * CF + k
                            if i >= 2:
                                s.wait_ge(d3k, i - 1)
                            s.dma_start(WQD[:, (k % 2) * D:(k % 2) * D + D],
                                        wqo_scr[k * 128:(k + 1) * 128, :]).then_inc(d3w[k % 2], 16)
                        for m in range(CH):
                            j = tt * CH + m
                            if j >= 2:
                                s.wait_ge(d3v, j - 1)
                            s.dma_start(X1T[:, (j % 2) * TTW:(j % 2 + 1) * TTW],
                                        x1_in[m * 128:(m + 1) * 128,
                                              tt * TTW:(tt + 1) * TTW]).then_inc(d3xs[j % 2], 16)

                @b.gpsimd
                def _(g):
                    for tt in range(NTT):
                        if tt >= 1:
                            g.wait_ge(d3k, tt * CF)
                        if dbgD and tt == 1:
                            g.wait_ge(dbg3, 16 * (CH + CF))
                        for k in range(CF):
                            g.dma_start(HQT[:, k * TTW:(k + 1) * TTW],
                                        hq_scr[k * 128:(k + 1) * 128,
                                               tt * TTW:(tt + 1) * TTW]).then_inc(d3h, 16)

                @b.tensor
                def _(t):
                    for tt in range(NTT):
                        if tt >= 1:
                            t.wait_ge(d3s, 8 * tt)
                        t.wait_ge(d3h, 16 * CF * (tt + 1))
                        for k in range(CF):
                            i = tt * CF + k
                            t.wait_ge(d3w[k % 2], 16 * (i // 2 + 1))
                            for m in range(CH):
                                mi = t.matmul(PSALL[:, m * TTW:(m + 1) * TTW],
                                              WQD[:, (k % 2) * D + m * 128:(k % 2) * D + (m + 1) * 128],
                                              HQT[:, k * TTW:(k + 1) * TTW],
                                              start=(k == 0), stop=(k == CF - 1))
                            mi.then_inc(d3k, 1)

                @b.scalar
                def _(sc):
                    for tt in range(NTT):
                        if dbgD and tt == 0:
                            sc.wait_ge(d3k, CF)
                            for mm_ in range(CH):
                                sc.activation(SC1[:, (mm_ % 4) * TTW:(mm_ % 4 + 1) * TTW],
                                              PSALL[:, mm_ * TTW:(mm_ + 1) * TTW],
                                              AF.Identity, bias=EPSC[:, 0:1], scale=1.0)
                                sc.dma_start(pst_out[mm_ * 128:(mm_ + 1) * 128, :],
                                             SC1[:, (mm_ % 4) * TTW:(mm_ % 4 + 1) * TTW]).then_inc(dbg3, 16)
                                if mm_ % 4 == 3:
                                    sc.wait_ge(dbg3, 16 * (mm_ + 1))
                            for kk_ in range(CF):
                                sc.dma_start(hqt_out[kk_ * 128:(kk_ + 1) * 128, :],
                                             HQT[:, kk_ * TTW:(kk_ + 1) * TTW]).then_inc(dbg3, 16)
                            sc.wait_ge(dbg3, 16 * (CH + CF))
                        for m in range(CH):
                            sc.wait_ge(d3k, (tt + 1) * CF)
                            if tt * CH + m >= 4:
                                sc.wait_ge(d3v, tt * CH + m - 3)
                            sc.activation(SC1[:, (m % 4) * TTW:(m % 4 + 1) * TTW],
                                          PSALL[:, m * TTW:(m + 1) * TTW], AF.Identity,
                                          bias=BOC[:, m:m + 1], scale=1.0).then_inc(d3s, 1)

                @b.vector
                def _(v):
                    for tt in range(NTT):
                        for m in range(CH):
                            j = tt * CH + m
                            v.wait_ge(d3s, j + 1)
                            v.wait_ge(d3xs[j % 2], 16 * (j // 2 + 1))
                            v.tensor_add(STR[:, m * T + tt * TTW: m * T + (tt + 1) * TTW],
                                         SC1[:, (m % 4) * TTW:(m % 4 + 1) * TTW],
                                         X1T[:, (j % 2) * TTW:(j % 2 + 1) * TTW]).then_inc(d3v, 1)

            with nc.Block() as b:
                @b.sync
                def _(s):
                    s.wait_ge(d3v, NTT * CH)

        if debug and stop == "x2":
            with nc.Block() as b:
                @b.sync
                def _(s):
                    s.dma_start(fm(x3_out), sb3(STR, CH, T)).then_inc(sto, 16)
                    s.wait_ge(sto, 16)
        emit_postln(nc, env, STR)
        if debug:
            with nc.Block() as b:
                @b.sync
                def _(s):
                    base = 16 if (debug and stop == "x2") else 0
                    s.dma_start(fm(x2_out), sb3(STR, CH, T)).then_inc(sto, 16)
                    s.dma_start(hq_out[:, :], hq_scr[:, :]).then_inc(sto, 16)
                    s.dma_start(wq_out[:, :], wqo_scr[:, :]).then_inc(sto, 16)
                    s.wait_ge(sto, base + 48)
        if stop == "x2":
            with nc.Block() as b:
                @b.sync
                def _(s):
                    base = 64 if debug else 0
                    s.dma_start(fm(y_out), sb3(STR, CH, T)).then_inc(sto, 16)
                    s.wait_ge(sto, base + 16)
            return nc

        with ExitStack() as _c5o:
            VBUF = _c5o.enter_context(nc.sbuf_tensor("VBUF", [128, 10 * 1024], BF16))
            AB = _c5o.enter_context(nc.sbuf_tensor("AB", [128, 1280], BF16))
            d5s3 = _c5o.enter_context(nc.semaphore("d5s3"))
            with ExitStack() as _c5ab:
                QTM = _c5ab.enter_context(nc.sbuf_tensor("QTM", [128, 16 * 512], BF16))
                KBUF = _c5ab.enter_context(nc.sbuf_tensor("KBUF", [128, 10 * 512], BF16))
                with ExitStack() as _c5a:
                    _e5 = _c5a.enter_context
                    EYB = _e5(nc.sbuf_tensor("EYB", [128, 128], BF16))
                    WPB = _e5(nc.sbuf_tensor("WPB", [1, 40], BF16))
                    WPCC = _e5(nc.sbuf_tensor("WPCC", [128, 40], F32))
                    WI = _e5(nc.sbuf_tensor("WI", [128, 40 * 128], BF16))
                    WQR = _e5(nc.sbuf_tensor("WQR", [128, 8 * 512], BF16))
                    XC5 = _e5(nc.sbuf_tensor("XC5", [128, 2 * 512], BF16))
                    QF = _e5(nc.sbuf_tensor("QF", [128, 2 * 512], BF16))
                    XTM = _e5(nc.sbuf_tensor("XTM", [128, 8 * 1024], BF16))
                    PS5 = _e5(nc.psum_tensor("PS5", [128, 2048], F32))
                    PTB = _e5(nc.psum_tensor("PTB", [128, 2048], BF16))
                    PTF = _e5(nc.psum_tensor("PTF", [128, 1024], F32))
                    d5l = _e5(nc.semaphore("d5l"))
                    d5w = _e5(nc.semaphore("d5w"))
                    d5c = _e5(nc.semaphore("d5c"))
                    d5m = _e5(nc.semaphore("d5m"))
                    d5q = _e5(nc.semaphore("d5q"))
                    d5t = _e5(nc.semaphore("d5t"))
                    d5e = _e5(nc.semaphore("d5e"))
                    d5f = _e5(nc.semaphore("d5f"))
                    d5g = _e5(nc.semaphore("d5g"))
                    d5h = _e5(nc.semaphore("d5h"))
                    d5v = _e5(nc.semaphore("d5v"))
                    d5i = _e5(nc.semaphore("d5i"))
                    d5j = _e5(nc.semaphore("d5j"))
                    with nc.Block() as b:
                        @b.gpsimd
                        def _(g):
                            g.dma_start(WPB[:, :], wp_in[:, :]).then_inc(d5l, 16)
                            for k in range(CH):
                                g.dma_start(WQR[:, k * 512:(k + 1) * 512],
                                            wq1_in[k * 128:(k + 1) * 128, :]).then_inc(d5l, 16)

                        @b.vector
                        def _(v):
                            v.tensor_copy(EYB[:, :], EYF[:, :])
                            v.drain().then_inc(d5w, 1)

                        @b.tensor
                        def _(t):
                            t.wait_ge(d5l, 16 * 9)
                            t.wait_ge(d5w, 1)
                            t.matmul(PS5[:, 0:40], ONER[:, :], WPB[0:1, 0:40],
                                     start=True, stop=True).then_inc(d5w, 1)

                        @b.vector
                        def _(v):
                            v.wait_ge(d5w, 2)
                            v.tensor_copy(WPCC[:, :], PS5[:, 0:40])
                            v.drain()
                            for j in range(40):
                                v.tensor_scalar(WI[:, j * 128:(j + 1) * 128], EYB[:, :],
                                                WPCC[:, j:j + 1], None, AL.mult)
                            v.drain().then_inc(d5w, 1)

                    if d5stop == 1:
                        _store_y_dbg(nc, y_out, STR, 1)
                        return nc
                    with nc.Block() as b:
                        @b.vector
                        def _(v):
                            for tt in range(NTT):
                                for k in range(CH):
                                    i = tt * CH + k
                                    if i >= 2:
                                        v.wait_ge(d5m, i - 1)
                                    v.tensor_copy(XC5[:, (i % 2) * 512:(i % 2) * 512 + 512],
                                                  STR[:, k * T + tt * TTW: k * T + (tt + 1) * TTW])
                                    v.drain().then_inc(d5c, 1)
                                for cq in range(4):
                                    for j in range(4):
                                        n = tt * 16 + cq * 4 + j
                                        v.wait_ge(d5t, n + 1)
                                        v.tensor_copy(
                                            QTM[:, (tt * 4 + j) * 512 + cq * 128:
                                                   (tt * 4 + j) * 512 + (cq + 1) * 128],
                                            PTB[:, (n % 2) * 1024:(n % 2) * 1024 + 128])
                                        v.drain().then_inc(d5e, 1)

                        @b.tensor
                        def _(t):
                            for tt in range(NTT):
                                if tt >= 1:
                                    t.wait_ge(d5q, tt * 4)
                                for k in range(CH):
                                    i = tt * CH + k
                                    t.wait_ge(d5c, i + 1)
                                    for cq in range(4):
                                        mi = t.matmul(PS5[:, cq * 512:(cq + 1) * 512],
                                                      WQR[:, k * 512 + cq * 128:k * 512 + (cq + 1) * 128],
                                                      XC5[:, (i % 2) * 512:(i % 2) * 512 + 512],
                                                      start=(k == 0), stop=(k == CH - 1))
                                    mi.then_inc(d5m, 1)
                                for cq in range(4):
                                    t.wait_ge(d5q, tt * 4 + cq + 1)
                                    for j in range(4):
                                        n = tt * 16 + cq * 4 + j
                                        if n >= 2:
                                            t.wait_ge(d5e, n - 1)
                                        t.transpose(PTB[:, (n % 2) * 1024:(n % 2) * 1024 + 128],
                                                    QF[:, (cq % 2) * 512 + j * 128:
                                                          (cq % 2) * 512 + (j + 1) * 128],
                                                    EYB[:, :]).then_inc(d5t, 1)

                        @b.scalar
                        def _(sc):
                            for tt in range(NTT):
                                sc.wait_ge(d5m, tt * CH + CH)
                                for cq in range(4):
                                    if cq >= 2:
                                        sc.wait_ge(d5t, tt * 16 + (cq - 1) * 4)
                                    sc.activation(QF[:, (cq % 2) * 512:(cq % 2) * 512 + 512],
                                                  PS5[:, cq * 512:(cq + 1) * 512], AF.Identity,
                                                  bias=BQC[:, cq:cq + 1],
                                                  scale=1.0).then_inc(d5q, 1)

                    if d5stop == 2:
                        _store_y_dbg(nc, y_out, STR, 2)
                        return nc
                    for hf in range(2):
                        with nc.Block() as b:
                            @b.tensor
                            def _(t, hf=hf):
                                for n in range(64):
                                    bb, cd = n // 8, n % 8
                                    if n >= 2:
                                        t.wait_ge(d5f, hf * 64 + n - 1)
                                    t.transpose(PTF[:, (n % 2) * 512:(n % 2) * 512 + 128],
                                                STR[:, cd * T + bb * LLOC + hf * 128:
                                                       cd * T + bb * LLOC + hf * 128 + 128],
                                                EYF[:, :]).then_inc(d5g, 1)
                                t.wait_ge(d5f, hf * 64 + 64)
                                t.wait_ge(d5w, 3)
                                for p in range(5):
                                    for dh in range(2):
                                        g_ = p * 2 + dh
                                        if g_ >= 2:
                                            t.wait_ge(d5v, hf * 10 + g_ - 1)
                                        for bb in range(8):
                                            mi = t.matmul(PS5[:, (g_ % 2) * 512:(g_ % 2) * 512 + 512],
                                                          WI[:, (bb * 5 + p) * 128:(bb * 5 + p + 1) * 128],
                                                          XTM[:, bb * 1024 + dh * 512:bb * 1024 + (dh + 1) * 512],
                                                          start=(bb == 0), stop=(bb == 7))
                                        mi.then_inc(d5h, 1)
                                for p in range(5):
                                    if p >= 2:
                                        t.wait_ge(d5j, hf * 5 + p - 1)
                                    for bb in range(8):
                                        mi = t.matmul(PS5[:, 1024 + (p % 2) * 512:1024 + (p % 2) * 512 + 512],
                                                      WI[:, (bb * 5 + p) * 128:(bb * 5 + p + 1) * 128],
                                                      QTM[:, (bb * 2 + hf) * 512:(bb * 2 + hf + 1) * 512],
                                                      start=(bb == 0), stop=(bb == 7))
                                    mi.then_inc(d5i, 1)

                            @b.vector
                            def _(v, hf=hf):
                                for n in range(64):
                                    v.wait_ge(d5g, hf * 64 + n + 1)
                                    v.tensor_copy(XTM[:, (n // 8) * 1024 + (n % 8) * 128:
                                                     (n // 8) * 1024 + (n % 8 + 1) * 128],
                                                  PTF[:, (n % 2) * 512:(n % 2) * 512 + 128])
                                    v.drain().then_inc(d5f, 1)
                                for p in range(5):
                                    for dh in range(2):
                                        g_ = p * 2 + dh
                                        v.wait_ge(d5h, hf * 10 + g_ + 1)
                                        v.tensor_copy(VBUF[:, (hf * 5 + p) * 1024 + dh * 512:
                                                         (hf * 5 + p) * 1024 + (dh + 1) * 512],
                                                      PS5[:, (g_ % 2) * 512:(g_ % 2) * 512 + 512])
                                        v.drain().then_inc(d5v, 1)
                                for p in range(5):
                                    v.wait_ge(d5i, hf * 5 + p + 1)
                                    v.tensor_copy(KBUF[:, (hf * 5 + p) * 512:(hf * 5 + p + 1) * 512],
                                                  PS5[:, 1024 + (p % 2) * 512:1024 + (p % 2) * 512 + 512])
                                    v.drain().then_inc(d5j, 1)

                if d5stop == 3:
                    _store_y_dbg(nc, y_out, STR, 3)
                    return nc
                with ExitStack() as _c5b:
                    LSB = _c5b.enter_context(nc.sbuf_tensor("LSB", [128, 1280], F32))
                    LGP = _c5b.enter_context(nc.sbuf_tensor("LGP", [128, 1280], F32))
                    MXB = _c5b.enter_context(nc.sbuf_tensor("MXB", [128, 256], F32))
                    SMB = _c5b.enter_context(nc.sbuf_tensor("SMB", [128, 256], F32))
                    d5s1 = _c5b.enter_context(nc.semaphore("d5s1"))
                    d5s2 = _c5b.enter_context(nc.semaphore("d5s2"))
                    with nc.Block() as b:
                        @b.vector
                        def _(v):
                            for hf in range(2):
                                for h in range(16):
                                    qv = bass.AP(QTM, hf * 512 + h * 32,
                                                 [[16 * 512, 128], [1024, 8], [0, 5], [1, 32]])
                                    kv = bass.AP(KBUF, hf * 2560 + h,
                                                 [[10 * 512, 128], [0, 8], [512, 5], [16, 32]])
                                    v.tensor_mul(bass.AP(LGP, 0,
                                                         [[1280, 128], [160, 8], [32, 5], [1, 32]]),
                                                 qv, kv)
                                    v.tensor_reduce(bass.AP(LSB, hf * 640 + h * 40,
                                                            [[1280, 128], [5, 8], [1, 5]]),
                                                    bass.AP(LGP, 0,
                                                            [[1280, 128], [32, 40], [1, 32]]),
                                                    AX.X, AL.add)
                            v.drain()
                            v.tensor_reduce(MXB[:, :],
                                            bass.AP(LSB, 0, [[1280, 128], [5, 256], [1, 5]]),
                                            AX.X, AL.max)
                            v.drain()
                            v.tensor_sub(bass.AP(LSB, 0, [[1280, 128], [5, 256], [1, 5]]),
                                         bass.AP(LSB, 0, [[1280, 128], [5, 256], [1, 5]]),
                                         bass.AP(MXB, 0, [[256, 128], [1, 256], [0, 5]]))
                            v.drain().then_inc(d5s1, 1)

                        @b.scalar
                        def _(sc):
                            sc.wait_ge(d5s1, 1)
                            sc.activation(LGP[:, :], LSB[:, :], AF.Exp,
                                          bias=0.0, scale=0.125).then_inc(d5s2, 1)

                        @b.vector
                        def _(v):
                            v.wait_ge(d5s2, 1)
                            v.tensor_reduce(SMB[:, :],
                                            bass.AP(LGP, 0, [[1280, 128], [5, 256], [1, 5]]),
                                            AX.X, AL.add)
                            v.drain()
                            v.reciprocal(MXB[:, :], SMB[:, :])
                            v.drain()
                            v.tensor_mul(bass.AP(AB, 0, [[1280, 128], [5, 256], [1, 5]]),
                                         bass.AP(LGP, 0, [[1280, 128], [5, 256], [1, 5]]),
                                         bass.AP(MXB, 0, [[256, 128], [1, 256], [0, 5]]))
                            v.drain().then_inc(d5s3, 1)

            if d5stop == 4:
                _store_y_dbg(nc, y_out, STR, 4)
                return nc
            with ExitStack() as _c5c:
                _e5c = _c5c.enter_context
                OFM = _e5c(nc.sbuf_tensor("OFM", [128, CH * T], BF16))
                OTMB = _e5c(nc.sbuf_tensor("OTMB", [128, 2 * 1024], F32))
                PRD = _e5c(nc.sbuf_tensor("PRD", [128, 320], F32))
                W1R = _e5c(nc.sbuf_tensor("W1R", [128, 8 * 1024], BF16))
                d5o = _e5c(nc.semaphore("d5o"))
                d5r = _e5c(nc.semaphore("d5r"))
                d5p = _e5c(nc.semaphore("d5p"))
                d5u = _e5c(nc.semaphore("d5u"))
                with ExitStack() as _c5t:
                    PTF2 = _c5t.enter_context(nc.psum_tensor("PTF2", [128, 1024], F32))
                    with nc.Block() as b:
                        @b.gpsimd
                        def _(g):
                            for k in range(CH):
                                g.dma_start(W1R[:, k * 1024:(k + 1) * 1024],
                                            w1o_in[k * 128:(k + 1) * 128, :]).then_inc(d5u, 16)

                        @b.vector
                        def _(v):
                            v.wait_ge(d5s3, 1)
                            for m in range(16):
                                hf, bb = m // 8, m % 8
                                if m >= 2:
                                    v.wait_ge(d5r, (m - 1) * 8)
                                for h in range(16):
                                    av = bass.AP(AB, hf * 640 + h * 40 + bb * 5,
                                                 [[1280, 128], [0, 64], [1, 5]])
                                    vv = bass.AP(VBUF, hf * 5 * 1024 + h,
                                                 [[10 * 1024, 128], [16, 64], [1024, 5]])
                                    v.tensor_mul(bass.AP(PRD, 0, [[320, 128], [5, 64], [1, 5]]),
                                                 av, vv)
                                    v.tensor_reduce(bass.AP(OTMB, (m % 2) * 1024 + h,
                                                            [[2048, 128], [16, 64]]),
                                                    bass.AP(PRD, 0, [[320, 128], [5, 64], [1, 5]]),
                                                    AX.X, AL.add)
                                v.drain().then_inc(d5o, 1)
                                if m >= 1:
                                    mm_ = m - 1
                                    for cd in range(CH):
                                        v.wait_ge(d5r, mm_ * 8 + cd + 1)
                                        hfp, bbp = mm_ // 8, mm_ % 8
                                        v.tensor_copy(
                                            OFM[:, cd * T + bbp * LLOC + hfp * 128:
                                                   cd * T + bbp * LLOC + hfp * 128 + 128],
                                            PTF2[:, ((mm_ * 8 + cd) % 2) * 512:
                                                    ((mm_ * 8 + cd) % 2) * 512 + 128])
                                        v.drain().then_inc(d5p, 1)
                            mm_ = 15
                            for cd in range(CH):
                                v.wait_ge(d5r, mm_ * 8 + cd + 1)
                                v.tensor_copy(
                                    OFM[:, cd * T + (mm_ % 8) * LLOC + (mm_ // 8) * 128:
                                           cd * T + (mm_ % 8) * LLOC + (mm_ // 8) * 128 + 128],
                                    PTF2[:, ((mm_ * 8 + cd) % 2) * 512:
                                            ((mm_ * 8 + cd) % 2) * 512 + 128])
                                v.drain().then_inc(d5p, 1)

                        @b.tensor
                        def _(t):
                            for m in range(16):
                                t.wait_ge(d5o, m + 1)
                                for cd in range(CH):
                                    n = m * 8 + cd
                                    if n >= 2:
                                        t.wait_ge(d5p, n - 1)
                                    t.transpose(PTF2[:, (n % 2) * 512:(n % 2) * 512 + 128],
                                                OTMB[:, (m % 2) * 1024 + cd * 128:
                                                        (m % 2) * 1024 + (cd + 1) * 128],
                                                EYF[:, :]).then_inc(d5r, 1)
                if d5stop == 5:
                    _store_y_dbg(nc, y_out, STR, 5)
                    return nc
                with ExitStack() as _c5w:
                    PW = _c5w.enter_context(nc.psum_tensor("PW", [128, 4096], F32))
                    d5z = _c5w.enter_context(nc.semaphore("d5z"))
                    d5x2 = _c5w.enter_context(nc.semaphore("d5x2"))
                    d5y = _c5w.enter_context(nc.semaphore("d5y"))
                    with nc.Block() as b:
                        @b.tensor
                        def _(t):
                            t.wait_ge(d5u, 16 * CH)
                            for tt in range(NTT):
                                if tt >= 1:
                                    t.wait_ge(d5y, 8 * tt)
                                for k in range(CH):
                                    for m in range(CH):
                                        mi = t.matmul(PW[:, m * TTW:(m + 1) * TTW],
                                                      W1R[:, k * 1024 + m * 128:k * 1024 + (m + 1) * 128],
                                                      OFM[:, k * T + tt * TTW:k * T + (tt + 1) * TTW],
                                                      start=(k == 0), stop=(k == CH - 1))
                                    mi.then_inc(d5z, 1)

                        @b.scalar
                        def _(sc):
                            for tt in range(NTT):
                                sc.wait_ge(d5z, (tt + 1) * CH)
                                for m in range(CH):
                                    j = tt * CH + m
                                    if j >= 4:
                                        sc.wait_ge(d5y, j - 3)
                                    sc.activation(SC1[:, (m % 4) * TTW:(m % 4 + 1) * TTW],
                                                  PW[:, m * TTW:(m + 1) * TTW], AF.Identity,
                                                  bias=B1C[:, m:m + 1], scale=1.0).then_inc(d5x2, 1)

                        @b.vector
                        def _(v):
                            for tt in range(NTT):
                                for m in range(CH):
                                    j = tt * CH + m
                                    v.wait_ge(d5x2, j + 1)
                                    v.tensor_add(STR[:, m * T + tt * TTW:m * T + (tt + 1) * TTW],
                                                 SC1[:, (m % 4) * TTW:(m % 4 + 1) * TTW],
                                                 STR[:, m * T + tt * TTW:m * T + (tt + 1) * TTW]
                                                 ).then_inc(d5y, 1)
                            v.drain()
                if debug:
                    with nc.semaphore("dbg5") as dbg5:
                        with nc.Block() as b:
                            @b.sync
                            def _(s):
                                s.dma_start(fm(ofm_out), sb3(OFM, CH, T)).then_inc(dbg5, 16)
                                s.wait_ge(dbg5, 16)
        emit_postln(nc, env, STR)
        if debug:
            with nc.semaphore("dbg6") as dbg6:
                with nc.Block() as b:
                    @b.sync
                    def _(s):
                        s.dma_start(fm(x3_out), sb3(STR, CH, T)).then_inc(dbg6, 16)
                        s.wait_ge(dbg6, 16)
        if stop == "x3":
            with nc.Block() as b:
                @b.sync
                def _(s):
                    s.dma_start(fm(y_out), sb3(STR, CH, T)).then_inc(sto, 16)
                    s.wait_ge(sto, (48 if debug else 0) + 16)
            return nc

        for (w_in_, scr_, nch_, wcols_, tag_) in [(wfc_in, wfcq_scr, CH, DFF, "fc"),
                                                  (wpj_in, wpjq_scr, CF, D, "pj")]:
            with ExitStack() as _c6q:
                _e6 = _c6q.enter_context
                WFST = _e6(nc.sbuf_tensor(f"WFST{tag_}", [128, 2 * wcols_], F32))
                SCAQ = _e6(nc.sbuf_tensor(f"SCAQ{tag_}", [128, wcols_], F32))
                ICQ = _e6(nc.sbuf_tensor(f"ICQ{tag_}", [128, wcols_], I32))
                WQQ = _e6(nc.sbuf_tensor(f"WQQ{tag_}", [128, 2 * wcols_], BF16))
                AMA6 = _e6(nc.sbuf_tensor(f"AMA6{tag_}", [128, nch_], F32))
                AMT = _e6(nc.sbuf_tensor(f"AMT{tag_}", [128, 1], F32))
                SRF = _e6(nc.sbuf_tensor(f"SRF{tag_}", [128, 1], F32))
                SCF = _e6(nc.sbuf_tensor(f"SCF{tag_}", [128, 1], F32))
                q6a = [_e6(nc.semaphore(f"q6a{tag_}{i_}")) for i_ in range(2)]
                q6r = _e6(nc.semaphore(f"q6r{tag_}"))
                q6f = _e6(nc.semaphore(f"q6f{tag_}"))
                q6b = [_e6(nc.semaphore(f"q6b{tag_}{i_}")) for i_ in range(2)]
                q6p = _e6(nc.semaphore(f"q6p{tag_}"))
                q6c = _e6(nc.semaphore(f"q6c{tag_}"))
                q6d = _e6(nc.semaphore(f"q6d{tag_}"))
                q6e = [_e6(nc.semaphore(f"q6e{tag_}{i_}")) for i_ in range(2)]
                with nc.Block() as b:
                    def _mk1(w_in_=w_in_, nch_=nch_, wcols_=wcols_):
                        @b.sync
                        def _(s):
                            for k in range(nch_):
                                if k >= 2:
                                    s.wait_ge(q6r, k - 1)
                                s.dma_start(WFST[:, (k % 2) * wcols_:(k % 2 + 1) * wcols_],
                                            w_in_[k * 128:(k + 1) * 128, :]).then_inc(q6a[k % 2], 16)

                        @b.vector
                        def _(v):
                            for k in range(nch_):
                                v.wait_ge(q6a[k % 2], 16 * (k // 2 + 1))
                                v.tensor_reduce(AMA6[:, k:k + 1],
                                                WFST[:, (k % 2) * wcols_:(k % 2 + 1) * wcols_],
                                                AX.X, AL.max, apply_absolute_value=True)
                                v.drain().then_inc(q6r, 1)
                            v.tensor_reduce(AMT[:, :], AMA6[:, :], AX.X, AL.max)
                            v.drain().then_inc(q6p, 1)
                    _mk1()
                fcq = [0]
                emit_pfold_max(nc, AMT, FSC, 1, q6f, fcq, pre=(q6p, 1))
                emit_pdouble(nc, [AMT], 1, q6f, fcq)
                with nc.Block() as b:
                    @b.vector
                    def _(v):
                        v.tensor_scalar(WIDE[:, 0:512], AMT[:, 0:1].broadcast_to([128, 512]),
                                        1e-30, None, AL.max)
                        v.drain()
                        v.reciprocal(WIDE[:, 512:1024], WIDE[:, 0:512])
                        v.drain()
                        v.tensor_scalar(SRF[:, 0:1], WIDE[:, 512:513], 127.0, None, AL.mult)
                        v.drain()
                        v.tensor_scalar(SCF[:, 0:1], AMT[:, 0:1], 1.0 / 127.0, None, AL.mult)
                        v.drain().then_inc(q6f, 1)
                with nc.Block() as b:
                    def _mk2(w_in_=w_in_, scr_=scr_, nch_=nch_, wcols_=wcols_, tgt=fcq[0] + 1):
                        @b.sync
                        def _(s):
                            for k in range(nch_):
                                if k >= 2:
                                    s.wait_ge(q6c, k - 1)
                                s.dma_start(WFST[:, (k % 2) * wcols_:(k % 2 + 1) * wcols_],
                                            w_in_[k * 128:(k + 1) * 128, :]).then_inc(q6b[k % 2], 16)

                        @b.scalar
                        def _(sc):
                            sc.wait_ge(q6f, tgt)
                            for k in range(nch_):
                                sc.wait_ge(q6b[k % 2], 16 * (k // 2 + 1))
                                if k >= 1:
                                    sc.wait_ge(q6d, k)
                                sc.activation(SCAQ[:, :],
                                              WFST[:, (k % 2) * wcols_:(k % 2 + 1) * wcols_],
                                              AF.Identity, bias=RBC8[:, 0:1],
                                              scale=SRF[:, 0:1]).then_inc(q6c, 1)

                        @b.vector
                        def _(v):
                            for k in range(nch_):
                                v.wait_ge(q6c, k + 1)
                                if k >= 2:
                                    v.wait_ge(q6e[k % 2], 16 * (k // 2))
                                v.tensor_copy(ICQ[:, :], SCAQ[:, :])
                                v.tensor_scalar(SCAQ[:, :], ICQ[:, :], RS8, 127.0,
                                                AL.subtract, AL.min)
                                v.tensor_scalar(WQQ[:, (k % 2) * wcols_:(k % 2 + 1) * wcols_],
                                                SCAQ[:, :], -128.0, SCF[:, 0:1],
                                                AL.max, AL.mult).then_inc(q6d, 1)

                        @b.gpsimd
                        def _(g):
                            for k in range(nch_):
                                g.wait_ge(q6d, k + 1)
                                g.dma_start(scr_[k * 128:(k + 1) * 128, :],
                                            WQQ[:, (k % 2) * wcols_:(k % 2 + 1) * wcols_]
                                            ).then_inc(q6e[k % 2], 16)
                    _mk2()
                with nc.Block() as b:
                    def _mkw(nch_=nch_):
                        @b.vector
                        def _(v):
                            v.wait_ge(q6e[0], 16 * ((nch_ + 1) // 2))
                            v.wait_ge(q6e[1], 16 * (nch_ // 2))
                    _mkw()
        if d6stop == 1:
            _store_y_dbg(nc, y_out, STR, 61)
            return nc

        with ExitStack() as _c6h:
            _e6h = _c6h.enter_context
            X3B = _e6h(nc.sbuf_tensor("X3B", [128, CH * T], BF16))
            T1B = _e6h(nc.sbuf_tensor("T1B", [128, T], BF16))
            AFCB = _e6h(nc.sbuf_tensor("AFCB", [128, CH * 16], BF16))
            BFCB = _e6h(nc.sbuf_tensor("BFCB", [128, DFF], BF16))
            WGR = _e6h(nc.sbuf_tensor("WGR", [128, CH * 512], BF16))
            HG = _e6h(nc.sbuf_tensor("HG", [128, 2 * 2048], BF16))
            PS6 = _e6h(nc.psum_tensor("PS6", [128, 4096], F32))
            d6l = _e6h(nc.semaphore("d6l"))
            d6x3 = _e6h(nc.semaphore("d6x3"))
            d6t1 = _e6h(nc.semaphore("d6t1"))
            d6g = _e6h(nc.semaphore("d6g"))
            d6m = _e6h(nc.semaphore("d6m"))
            d6e2 = _e6h(nc.semaphore("d6e2"))
            d6s = _e6h(nc.semaphore("d6s"))
            with nc.Block() as b:
                @b.gpsimd
                def _(g):
                    for k in range(CH):
                        g.dma_start(AFCB[:, k * 16:(k + 1) * 16],
                                    afc_in[k * 128:(k + 1) * 128, :]).then_inc(d6l, 16)
                    g.dma_start(BFCB[0:16, :], bfcB_in[:, :]).then_inc(d6l, 16)

                @b.vector
                def _(v):
                    for k in range(CH):
                        v.tensor_copy(X3B[:, k * T:(k + 1) * T], STR[:, k * T:(k + 1) * T])
                    v.drain().then_inc(d6x3, 1)

                @b.tensor
                def _(t):
                    t.wait_ge(d6l, 16 * 9)
                    t.wait_ge(d6x3, 1)
                    for tt in range(NTT):
                        for k in range(CH):
                            mi = t.matmul(PS6[0:16, tt * 512:(tt + 1) * 512],
                                          AFCB[:, k * 16:(k + 1) * 16],
                                          X3B[:, k * T + tt * TTW:k * T + (tt + 1) * TTW],
                                          start=(k == 0), stop=(k == CH - 1))
                        mi.then_inc(d6t1, 1)

                @b.scalar
                def _(sc):
                    sc.wait_ge(d6t1, NTT)
                    sc.activation(T1B[0:16, :], PS6[0:16, 0:2048], AF.Identity,
                                  bias=0.0, scale=2.0).then_inc(d6t1, 1)
            with nc.Block() as b:
                @b.sync
                def _(s):
                    for mg in range(8):
                        if mg >= 1:
                            s.wait_ge(d6m, 4 * mg)
                        for k in range(CH):
                            s.dma_start(WGR[:, k * 512:(k + 1) * 512],
                                        wfcq_scr[k * 128:(k + 1) * 128,
                                                 mg * 512:(mg + 1) * 512]).then_inc(d6g, 16)

                @b.tensor
                def _(t):
                    t.wait_ge(d6t1, NTT + 1)
                    for mg in range(8):
                        t.wait_ge(d6g, 128 * (mg + 1))
                        for tt in range(NTT):
                            q = mg * 4 + tt
                            if q >= 2:
                                t.wait_ge(d6e2, 4 * (q - 1))
                            for k in range(CH):
                                for j in range(4):
                                    t.matmul(PS6[:, ((q % 2) * 4 + j) * 512:
                                                    ((q % 2) * 4 + j + 1) * 512],
                                             WGR[:, k * 512 + j * 128:k * 512 + (j + 1) * 128],
                                             X3B[:, k * T + tt * TTW:k * T + (tt + 1) * TTW],
                                             start=(k == 0), stop=False)
                            for j in range(4):
                                mi = t.matmul(PS6[:, ((q % 2) * 4 + j) * 512:
                                                    ((q % 2) * 4 + j + 1) * 512],
                                              BFCB[0:16, (mg * 4 + j) * 128:(mg * 4 + j + 1) * 128],
                                              T1B[0:16, tt * 512:(tt + 1) * 512],
                                              start=False, stop=True)
                            mi.then_inc(d6m, 1)

                @b.scalar
                def _(sc):
                    for q in range(32):
                        mg, tt = q // 4, q % 4
                        sc.wait_ge(d6m, q + 1)
                        if q >= 2:
                            sc.wait_ge(d6s, 64 * (q - 1))
                        for j in range(4):
                            sc.activation(HG[:, (q % 2) * 2048 + j * 512:
                                                (q % 2) * 2048 + (j + 1) * 512],
                                          PS6[:, ((q % 2) * 4 + j) * 512:
                                                  ((q % 2) * 4 + j + 1) * 512],
                                          AF.Gelu, bias=BFCC[:, mg * 4 + j:mg * 4 + j + 1],
                                          scale=1.0).then_inc(d6e2, 1)

                @b.gpsimd
                def _(g):
                    for q in range(32):
                        mg, tt = q // 4, q % 4
                        for j in range(4):
                            g.wait_ge(d6e2, 4 * q + j + 1)
                            g.dma_start(hid2_scr[(mg * 4 + j) * 128:(mg * 4 + j + 1) * 128,
                                                 tt * 512:(tt + 1) * 512],
                                        HG[:, (q % 2) * 2048 + j * 512:
                                              (q % 2) * 2048 + (j + 1) * 512]).then_inc(d6s, 16)
            with nc.Block() as b:
                @b.vector
                def _(v):
                    v.wait_ge(d6s, 64 * 32)
        if d6stop == 2:
            _store_y_dbg(nc, y_out, STR, 62)
            return nc

        with ExitStack() as _c6p:
            _e6p = _c6p.enter_context
            H1T = _e6p(nc.sbuf_tensor("H1T", [128, CF * TTW], BF16))
            APJB = _e6p(nc.sbuf_tensor("APJB", [128, CF * 16], BF16))
            T2B = _e6p(nc.sbuf_tensor("T2B", [128, T], BF16))
            BPJB = _e6p(nc.sbuf_tensor("BPJB", [128, D], BF16))
            WQD6 = _e6p(nc.sbuf_tensor("WQD6", [128, 2 * D], BF16))
            PS6B = _e6p(nc.psum_tensor("PS6B", [128, 4096], F32))
            d6l2 = _e6p(nc.semaphore("d6l2"))
            d6h2 = _e6p(nc.semaphore("d6h2"))
            d6w2 = [_e6p(nc.semaphore(f"d6w2{i_}")) for i_ in range(2)]
            d6q2 = _e6p(nc.semaphore("d6q2"))
            d6t2 = _e6p(nc.semaphore("d6t2"))
            d6k = _e6p(nc.semaphore("d6k"))
            d6lo = _e6p(nc.semaphore("d6lo"))
            d6sc = _e6p(nc.semaphore("d6sc"))
            d6v2 = _e6p(nc.semaphore("d6v2"))
            with nc.Block() as b:
                @b.gpsimd
                def _(g):
                    for k in range(CF):
                        g.dma_start(APJB[:, k * 16:(k + 1) * 16],
                                    apj_in[k * 128:(k + 1) * 128, :]).then_inc(d6l2, 16)
                    g.dma_start(BPJB[0:16, :], bpjB_in[:, :]).then_inc(d6l2, 16)
                    for tt in range(NTT):
                        if tt >= 1:
                            g.wait_ge(d6k, CF * tt)
                        for k in range(CF):
                            g.dma_start(H1T[:, k * TTW:(k + 1) * TTW],
                                        hid2_scr[k * 128:(k + 1) * 128,
                                                 tt * TTW:(tt + 1) * TTW]).then_inc(d6h2, 16)

                @b.sync
                def _(s):
                    for tt in range(NTT):
                        for k in range(CF):
                            i = tt * CF + k
                            if i >= 2:
                                s.wait_ge(d6k, i - 1)
                            s.dma_start(WQD6[:, (k % 2) * D:(k % 2) * D + D],
                                        wpjq_scr[k * 128:(k + 1) * 128, :]
                                        ).then_inc(d6w2[k % 2], 16)

                @b.tensor
                def _(t):
                    t.wait_ge(d6l2, 16 * (CF + 1))
                    for tt in range(NTT):
                        t.wait_ge(d6h2, 16 * CF * (tt + 1))
                        if tt >= 1:
                            t.wait_ge(d6sc, 8 * tt)
                        for k in range(CF):
                            mi = t.matmul(PS6B[0:16, 0:512], APJB[:, k * 16:(k + 1) * 16],
                                          H1T[:, k * TTW:(k + 1) * TTW],
                                          start=(k == 0), stop=(k == CF - 1))
                        mi.then_inc(d6q2, 1)
                        t.wait_ge(d6t2, tt + 1)
                        for k in range(CF):
                            i = tt * CF + k
                            t.wait_ge(d6w2[k % 2], 16 * (i // 2 + 1))
                            for m in range(CH):
                                t.matmul(PS6B[:, m * TTW:(m + 1) * TTW],
                                         WQD6[:, (k % 2) * D + m * 128:(k % 2) * D + (m + 1) * 128],
                                         H1T[:, k * TTW:(k + 1) * TTW],
                                         start=(k == 0), stop=False).then_inc(d6k, 1) \
                                    if m == CH - 1 else \
                                    t.matmul(PS6B[:, m * TTW:(m + 1) * TTW],
                                             WQD6[:, (k % 2) * D + m * 128:(k % 2) * D + (m + 1) * 128],
                                             H1T[:, k * TTW:(k + 1) * TTW],
                                             start=(k == 0), stop=False)
                        for m in range(CH):
                            mi = t.matmul(PS6B[:, m * TTW:(m + 1) * TTW],
                                          BPJB[0:16, m * 128:(m + 1) * 128],
                                          T2B[0:16, tt * 512:(tt + 1) * 512],
                                          start=False, stop=True)
                        mi.then_inc(d6lo, 1)

                @b.scalar
                def _(sc):
                    for tt in range(NTT):
                        sc.wait_ge(d6q2, tt + 1)
                        sc.activation(T2B[0:16, tt * 512:(tt + 1) * 512], PS6B[0:16, 0:512],
                                      AF.Identity, bias=0.0, scale=2.0).then_inc(d6t2, 1)
                        sc.wait_ge(d6lo, tt + 1)
                        for m in range(CH):
                            j = tt * CH + m
                            if j >= 4:
                                sc.wait_ge(d6v2, j - 3)
                            sc.activation(SC1[:, (m % 4) * TTW:(m % 4 + 1) * TTW],
                                          PS6B[:, m * TTW:(m + 1) * TTW], AF.Identity,
                                          bias=BPJC[:, m:m + 1], scale=1.0).then_inc(d6sc, 1)

                @b.vector
                def _(v):
                    for tt in range(NTT):
                        for m in range(CH):
                            j = tt * CH + m
                            v.wait_ge(d6sc, j + 1)
                            v.tensor_add(STR[:, m * T + tt * TTW:m * T + (tt + 1) * TTW],
                                         SC1[:, (m % 4) * TTW:(m % 4 + 1) * TTW],
                                         STR[:, m * T + tt * TTW:m * T + (tt + 1) * TTW]
                                         ).then_inc(d6v2, 1)
                    v.drain()
        emit_postln(nc, env, STR)
        with nc.Block() as b:
            @b.sync
            def _(s):
                base = 48 if debug else 0
                s.dma_start(fm(y_out), sb3(STR, CH, T)).then_inc(sto, 16)
                s.wait_ge(sto, base + 16)

    return nc


def prep_D_inputs(inputs, resB, resC):
    amax2_all = np.concatenate([np.asarray(resC[c]["amax2_out"]) for c in range(NCORES)], axis=1)
    w_kv = np.asarray(inputs["L1_attn_w_kv"], np.float32)
    wp_row = np.ascontiguousarray(w_kv[0:5, 0:8].T.reshape(1, 40))
    consts = dict(
        amax2_in=np.ascontiguousarray(amax2_all),
        wout_in=np.ascontiguousarray(np.asarray(inputs["L0_ffn_W_out"], np.float32)),
        bout_in=colz(inputs["L0_ffn_b_out"], CH),
        wq1_in=np.ascontiguousarray(np.asarray(inputs["L1_attn_w_q"], np.float32).T),
        bq_in=colz(inputs["L1_attn_b_q"], 4),
        wp_in=wp_row,
        w1o_in=np.ascontiguousarray(np.asarray(inputs["L1_attn_w_out"], np.float32).T),
        b1_in=colz(inputs["L1_attn_b_out"], CH),
        wfc_in=np.ascontiguousarray(np.asarray(inputs["L1_ffn_W_fc"], np.float32).T),
        bfc_in=colz(inputs["L1_ffn_b_fc"], CF),
        afc_in=np.ascontiguousarray(np.asarray(inputs["L1_ffn_A_fc"], np.float32).T),
        bfcB_in=np.ascontiguousarray(np.asarray(inputs["L1_ffn_B_fc"], np.float32).T),
        wpj_in=np.ascontiguousarray(np.asarray(inputs["L1_ffn_W_proj"], np.float32).T),
        bpj_in=colz(inputs["L1_ffn_b_proj"], CH),
        apj_in=np.ascontiguousarray(np.asarray(inputs["L1_ffn_A_proj"], np.float32).T),
        bpjB_in=np.ascontiguousarray(np.asarray(inputs["L1_ffn_B_proj"], np.float32).T),
        ones_in=np.full((128, 128), 1.0 / 1024.0, dtype=NBF),
        eye_in=np.eye(128, dtype=np.float32),
        onesr_in=np.ones((1, 128), dtype=NBF),
    )
    in_maps = []
    for c in range(NCORES):
        in_maps.append({"x1_in": np.asarray(resB[c]["x1_out"]),
                        "hid_in": np.asarray(resC[c]["hid_out"]), **consts})
    return in_maps



def prep_A_inputs(inputs):
    x = np.asarray(inputs["L0_attn_x"], np.float32)
    wx = np.ascontiguousarray(np.asarray(inputs["L0_attn_w_xproj"], np.float32).T)
    Amat = np.asarray(inputs["L0_attn_A"], np.float32).reshape(CH, 128, 16)
    acol = np.ascontiguousarray(Amat.transpose(1, 0, 2).reshape(128, CH * 16))
    consts = dict(
        wx_in=wx,
        bxp_in=colz(inputs["L0_attn_b_xproj"], 16),
        A_in=acol,
        gam_in=colz(inputs["L0_attn_gamma"], CH),
        bet_in=colz(inputs["L0_attn_beta"], CH),
        dp_in=colz(inputs["L0_attn_D_param"], CH),
        ones_in=np.full((128, 128), 1.0 / 1024.0, dtype=NBF),
    )
    in_maps = []
    for c in range(NCORES):
        sl = x[:, c * LLOC:(c + 1) * LLOC, :]
        xc = np.ascontiguousarray(sl.transpose(2, 0, 1).reshape(D, T))
        in_maps.append({"x_in": xc, **consts})
    return in_maps


_BUILT = {}


def _get(name, fn):
    if name not in _BUILT:
        _BUILT[name] = fn()
    return _BUILT[name]


def kernel(**inputs):
    cores = list(range(NCORES))
    ncA = _get("A", build_A)
    resA = run_bass_kernel_spmd(ncA, prep_A_inputs(inputs), core_ids=cores).results
    resA = [{k: np.asarray(v) for k, v in r.items()} for r in resA]
    ncB = _get("B", build_B)
    resB = run_bass_kernel_spmd(ncB, prep_B_inputs(inputs, resA), core_ids=cores).results
    resB = [{k: np.asarray(v) for k, v in r.items()} for r in resB]
    ncC = _get("C", build_C)
    resC = run_bass_kernel_spmd(ncC, prep_C_inputs(inputs, resB), core_ids=cores).results
    resC = [{k: np.asarray(v) for k, v in r.items()} for r in resC]
    ncD = _get("D", build_D)
    resD = run_bass_kernel_spmd(ncD, prep_D_inputs(inputs, resB, resC), core_ids=cores).results
    out = np.empty((B, NCORES * LLOC, D), np.float32)
    for c in cores:
        yc = np.asarray(resD[c]["y_out"], np.float32)
        out[:, c * LLOC:(c + 1) * LLOC, :] = yc.reshape(D, B, LLOC).transpose(1, 2, 0)
    return out
